# version 01
# baseline (speedup 1.0000x reference)
import math
from contextlib import ExitStack

import numpy as np
import ml_dtypes
import concourse.bass as bass
import concourse.mybir as mybir
from concourse.bass_core_rust import IndirectOffsetOnAxis
from concourse.bass_utils import run_bass_kernel_spmd

F32 = mybir.dt.float32
BF16 = mybir.dt.bfloat16
I8 = mybir.dt.int8
I32 = mybir.dt.int32
AF = mybir.ActivationFunctionType
OP = mybir.AluOpType
AX = mybir.AxisListType
bf16 = ml_dtypes.bfloat16

T = 2048
D = 2048
HD = 128
INTER = 8192
EPS_RMS = float(np.finfo(np.float32).eps)
EPS_LN = 1e-5
LAMBDA_INIT = 0.2
NEG = -1e30


class Prog:

    def __init__(self, nc, sems, serial=False):
        self.nc = nc
        self.sem = sems
        self.serial = serial
        self.cnt = {k: 0 for k in sems}
        self.lastw = {e: {k: 0 for k in sems} for e in
                      ("tensor", "vector", "scalar", "gpsimd", "sync")}
        self.own = {"tensor": "PE", "vector": "DV", "scalar": "AC", "gpsimd": "GP"}

    def inc(self, instr, s, n=1):
        instr.then_inc(self.sem[s], n)
        self.cnt[s] += n
        return (s, self.cnt[s])

    def w(self, eng, engname, *evs):
        if self.serial:
            evs = tuple(evs) + tuple((s, self.cnt[s]) for s in self.cnt
                                     if s not in ("CC",))
        for ev in evs:
            if ev is None:
                continue
            s, v = ev
            if v <= 0:
                continue
            if self.own.get(engname) == s:
                continue
            if self.lastw[engname][s] >= v:
                continue
            eng.wait_ge(self.sem[s], v)
            self.lastw[engname][s] = v


def chunks_aligned(c0, c1):
    out = []
    c = c0
    while c < c1:
        nxt = min(c1, (c // 512 + 1) * 512)
        out.append((c, nxt - c))
        c = nxt
    return out


DBG = False


def build_program(dbg=False, serial=False):
    nc = bass.Bass("TRN2", target_bir_lowering=False, detect_race_conditions=False)

    x_d = nc.dram_tensor("x", [T, D], F32, kind="ExternalInput")
    xown_d = nc.dram_tensor("xown", [1024, D], F32, kind="ExternalInput")
    wq_d = nc.dram_tensor("wq", [2, 128, 8192], F32, kind="ExternalInput")
    wk_d = nc.dram_tensor("wk", [2, 128, 8192], F32, kind="ExternalInput")
    wv_d = nc.dram_tensor("wv", [2, 128, 8192], F32, kind="ExternalInput")
    wg_d = nc.dram_tensor("wg", [16, 128, 8192], I8, kind="ExternalInput")
    wu_d = nc.dram_tensor("wu", [16, 128, 8192], I8, kind="ExternalInput")
    wd_d = nc.dram_tensor("wd", [16, 128, 8192], I8, kind="ExternalInput")
    sgt_d = nc.dram_tensor("sgt", [128, 64], F32, kind="ExternalInput")
    sut_d = nc.dram_tensor("sut", [128, 64], F32, kind="ExternalInput")
    sdt_d = nc.dram_tensor("sdt", [128, 16], F32, kind="ExternalInput")
    lam_d = nc.dram_tensor("lam", [1, 256], F32, kind="ExternalInput")
    scl_d = nc.dram_tensor("scl", [1, 8], F32, kind="ExternalInput")
    ctab_d = nc.dram_tensor("ctab", [128, 2048], BF16, kind="ExternalInput")
    stab_d = nc.dram_tensor("stab", [128, 2048], BF16, kind="ExternalInput")
    logpt_d = nc.dram_tensor("logpt", [128, 16], F32, kind="ExternalInput")
    maskd_d = nc.dram_tensor("maskd", [128, 128], BF16, kind="ExternalInput")
    identb_d = nc.dram_tensor("identb", [128, 128], BF16, kind="ExternalInput")
    identf_d = nc.dram_tensor("identf", [128, 128], F32, kind="ExternalInput")
    onesf_d = nc.dram_tensor("onesf", [1, 128], F32, kind="ExternalInput")
    idxg_d = nc.dram_tensor("idxg", [128, 16], I32, kind="ExternalInput")
    cst_d = nc.dram_tensor("cst", [128, 8], F32, kind="ExternalInput")
    out_d = nc.dram_tensor("out", [1024, D], F32, kind="ExternalOutput")
    ybuf_d = nc.dram_tensor("ybuf", [T, 1024], BF16)
    ygath_d = nc.dram_tensor("ygath", [2 * T, 1024], BF16)
    if dbg:
        dKT = nc.dram_tensor("dKT", [128, 16384], BF16, kind="ExternalOutput")
        dQT = nc.dram_tensor("dQT", [128, 16384], BF16, kind="ExternalOutput")
        dV = nc.dram_tensor("dV", [128, 16384], BF16, kind="ExternalOutput")
        dXNT = nc.dram_tensor("dXNT", [128, 16384], BF16, kind="ExternalOutput")
        dXNT0 = nc.dram_tensor("dXNT0", [128, 16384], BF16, kind="ExternalOutput")
        dXT0 = nc.dram_tensor("dXT0", [128, 2048], F32, kind="ExternalOutput")
        dLN = nc.dram_tensor("dLN", [128, 8], F32, kind="ExternalOutput")
        dSCL = nc.dram_tensor("dSCL", [128, 8], F32, kind="ExternalOutput")
        dSST = nc.dram_tensor("dSST", [128, 16], F32, kind="ExternalOutput")
        dE1 = nc.dram_tensor("dE1", [128, 2048], F32, kind="ExternalOutput")
        dE2 = nc.dram_tensor("dE2", [128, 2048], F32, kind="ExternalOutput")
        dPhT = nc.dram_tensor("dPhT", [128, 2048], BF16, kind="ExternalOutput")
        dZ = nc.dram_tensor("dZ", [128, 24], F32, kind="ExternalOutput")

    bx = {}

    es = ExitStack()
    E = es.enter_context
    with es:
        sems = {}
        for s in ("LT", "LX0", "LX1", "LXO", "LW0", "LW1", "LYC", "SY0", "SY1",
                  "SP2", "SG", "CC", "PE", "DV", "AC", "GP"):
            sems[s] = E(nc.semaphore(s))
        block = E(nc.Block())
        P = Prog(nc, sems, serial=serial)

        psS = E(nc.psum_tensor("psS", [128, 2048], F32))
        psS2 = E(nc.psum_tensor("psS2", [128, 1024], F32))
        ptA = E(nc.psum_tensor("ptA", [128, 2048], BF16))

        ctab = E(nc.sbuf_tensor("ctab_s", [128, 2048], BF16))
        stab = E(nc.sbuf_tensor("stab_s", [128, 2048], BF16))
        logpt = E(nc.sbuf_tensor("logpt_s", [128, 16], F32))
        maskd = E(nc.sbuf_tensor("maskd_s", [128, 128], BF16))
        identb = E(nc.sbuf_tensor("identb_s", [128, 128], BF16))
        identf = E(nc.sbuf_tensor("identf_s", [128, 128], F32))
        onesf = E(nc.sbuf_tensor("onesf_s", [1, 128], F32))
        scl = E(nc.sbuf_tensor("scl_s", [1, 8], F32))
        sclrep = E(nc.sbuf_tensor("sclrep_s", [128, 8], F32))
        lps = E(nc.sbuf_tensor("lps_s", [128, 128], F32))
        lam = E(nc.sbuf_tensor("lam_s", [1, 256], F32))
        lsc = E(nc.sbuf_tensor("lsc_s", [1, 80], F32))
        neglb = E(nc.sbuf_tensor("neglb_s", [128, 1], F32))
        sgt = E(nc.sbuf_tensor("sgt_s", [128, 64], F32))
        sut = E(nc.sbuf_tensor("sut_s", [128, 64], F32))
        sdt = E(nc.sbuf_tensor("sdt_s", [128, 16], F32))
        idxg = E(nc.sbuf_tensor("idxg_s", [128, 16], I32))
        cst = E(nc.sbuf_tensor("cst_s", [128, 8], F32))
        sqd = E(nc.sbuf_tensor("sqd_s", [128, 2048], BF16))
        lnst = E(nc.sbuf_tensor("lnst_s", [128, 8], F32))
        zst = E(nc.sbuf_tensor("zst_s", [128, 24], F32))
        dmy = E(nc.sbuf_tensor("dmy_s", [128, 512], BF16))

        @block.sync
        def _(sy):
            for dst, src in ((ctab, ctab_d), (stab, stab_d), (logpt, logpt_d),
                             (maskd, maskd_d), (identb, identb_d), (identf, identf_d),
                             (onesf, onesf_d), (scl, scl_d), (lam, lam_d),
                             (sgt, sgt_d), (sut, sut_d), (sdt, sdt_d), (idxg, idxg_d),
                             (cst, cst_d)):
                P.inc(sy.dma_start(dst[:], src[:, :]), "LT", 16)
        ev_tabs = ("LT", P.cnt["LT"])

        @block.tensor
        def _(t):
            P.w(t, "tensor", ev_tabs)
            ins = t.matmul(psS2[:, 0:8], onesf[:, :], scl[:, :], start=True, stop=True)
            bx["e"] = P.inc(ins, "PE")
        ev_sclmm = bx["e"]

        @block.vector
        def _(v):
            P.w(v, "vector", ev_sclmm, ev_tabs)
            v.tensor_scalar_mul(sclrep[:], psS2[:, 0:8], cst[:, 6:7])
            bx["e"] = P.inc(v.tensor_copy(dmy[:], stab[:, 0:512]), "DV")
        ev_sclrep = bx["e"]

        @block.vector
        def _(v):
            lp3 = logpt[:, :].unsqueeze(2).broadcast_to([128, 16, 8])
            sc3 = sclrep[:, :].unsqueeze(1).broadcast_to([128, 16, 8])
            o3 = lps[:, :].rearrange("p (t h) -> p t h", t=16)
            bx["e"] = P.inc(v.tensor_tensor(o3, lp3, sc3, op=OP.mult), "DV")
        ev_lps = bx["e"]

        esAB = ExitStack()
        KT = esAB.enter_context(nc.sbuf_tensor("KT_s", [128, 16384], BF16))
        QT = esAB.enter_context(nc.sbuf_tensor("QT_s", [128, 16384], BF16))
        Vsb = esAB.enter_context(nc.sbuf_tensor("Vsb_s", [128, 16384], BF16))

        ev_ptA_copy = [None, None]

        with ExitStack() as esA:
            EA = esA.enter_context
            xnT = EA(nc.sbuf_tensor("xnT_s", [128, 16384], BF16))
            xt = [EA(nc.sbuf_tensor(f"xt{i}_s", [128, 2048], F32)) for i in range(2)]
            xnb = EA(nc.sbuf_tensor("xnb_s", [128, 2048], BF16))
            Wbuf = [EA(nc.sbuf_tensor(f"Wb{i}_s", [128, 8192], BF16)) for i in range(2)]
            stg = EA(nc.sbuf_tensor("stg_s", [128, 3 * 512], BF16))
            sst = EA(nc.sbuf_tensor("sst_s", [128, 16], F32))

            ev_xt_free = {}
            ev_xnb_free = {}
            ev_unit_mm = {}
            ev_slice_free = {}
            ev_rot_prev = [None]
            ev_rt_prev = [None]
            ev_evac_prev = [None]
            guse = [0]
            SLICES = [psS[:, 0:512], psS[:, 512:1024], psS[:, 1024:1536],
                      psS[:, 1536:2048], psS2[:, 0:512], psS2[:, 512:1024]]
            wd_map = {"k": wk_d, "q": wq_d, "v": wv_d}

            for half in range(2):
                tiles = list(range(8 * half, 8 * half + 8))
                for s, tt in enumerate(tiles):
                    gi = half * 8 + s
                    par = gi % 2

                    @block.sync
                    def _(sy, tt=tt, par=par, gi=gi):
                        P.w(sy, "sync", ev_xt_free.get(gi - 2))
                        if dbg and gi == 8:
                            P.w(sy, "sync", ("SP2", P.cnt["SP2"]))
                        bx["e"] = P.inc(
                            sy.dma_start(xt[par][:], x_d[tt * 128:(tt + 1) * 128, :]),
                            f"LX{gi % 2}", 16)
                    ev_x = bx["e"]

                    @block.scalar
                    def _(sc, par=par):
                        P.w(sc, "scalar", ev_x)
                        bx["e"] = P.inc(sc.activation(sqd[:], xt[par][:], AF.Square,
                                                      accum_out=lnst[:, 0:1]), "AC")
                    ev_sq = bx["e"]

                    @block.vector
                    def _(v, par=par):
                        P.w(v, "vector", ev_x, ev_sq)
                        v.reduce_sum(lnst[:, 1:2], xt[par][:], axis=AX.X)
                        v.tensor_copy(dmy[:], stab[:, 0:512])
                        v.tensor_scalar_mul(lnst[:, 2:3], lnst[:, 1:2], cst[:, 0:1])
                        bx["e"] = P.inc(v.tensor_copy(dmy[:], stab[:, 0:512]), "DV")
                    ev_mean = bx["e"]

                    @block.scalar
                    def _(sc):
                        P.w(sc, "scalar", ev_mean)
                        sc.activation(lnst[:, 3:4], lnst[:, 2:3], AF.Square)
                        bx["e"] = P.inc(sc.activation(dmy[:], stab[:, 0:512], AF.Copy), "AC")
                    ev_msq = bx["e"]

                    @block.vector
                    def _(v):
                        P.w(v, "vector", ev_msq)
                        v.tensor_scalar(lnst[:, 4:5], lnst[:, 3:4], cst[:, 1:2], cst[:, 2:3],
                                        op0=OP.mult, op1=OP.add)
                        bx["e"] = P.inc(v.tensor_copy(dmy[:], stab[:, 0:512]), "DV")
                    ev_stats = bx["e"]

                    @block.scalar
                    def _(sc):
                        P.w(sc, "scalar", ev_stats)
                        sc.activation(lnst[:, 5:6], lnst[:, 0:1], AF.Ln,
                                      scale=cst[:, 0:1], bias=lnst[:, 4:5])
                        sc.activation(lnst[:, 6:7], lnst[:, 5:6], AF.Exp,
                                      scale=cst[:, 7:8])
                        bx["e"] = P.inc(sc.activation(dmy[:], stab[:, 0:512], AF.Copy), "AC")
                    ev_std = bx["e"]

                    @block.vector
                    def _(v, par=par, gi=gi):
                        P.w(v, "vector", ev_std, ev_xnb_free.get(gi - 1))
                        ev = P.inc(v.tensor_scalar(xnb[:], xt[par][:], lnst[:, 2:3],
                                                   lnst[:, 6:7], op0=OP.subtract,
                                                   op1=OP.mult), "DV")
                        bx["e"] = ev
                        ev_xt_free[gi] = ev
                    ev_xn = bx["e"]

                    for g in range(4):
                        hsel = g % 2

                        @block.tensor
                        def _(t, g=g, hsel=hsel, gi=gi):
                            P.w(t, "tensor", ev_xn, ev_ptA_copy[hsel])
                            for d4 in range(4):
                                dc = g * 4 + d4
                                ins = t.transpose(
                                    ptA[:, hsel * 1024 + d4 * 128:hsel * 1024 + (d4 + 1) * 128],
                                    xnb[:, dc * 128:(dc + 1) * 128], identb[:])
                            ev = P.inc(ins, "PE")
                            bx["e"] = ev
                            if g == 3:
                                ev_xnb_free[gi] = ev
                        ev_tr = bx["e"]

                        @block.vector
                        def _(v, g=g, hsel=hsel, s=s):
                            P.w(v, "vector", ev_tr)
                            ev = P.inc(v.tensor_copy(
                                xnT[:, s * 2048 + g * 512:s * 2048 + (g + 1) * 512],
                                ptA[:, hsel * 1024:hsel * 1024 + 512]), "DV")
                            bx["e"] = ev
                            ev_ptA_copy[hsel] = ev
                ev_xnT_done = ("DV", P.cnt["DV"])

                if dbg and half == 0:
                    @block.sync
                    def _(sy):
                        P.w(sy, "sync", ev_xnT_done)
                        P.inc(sy.dma_start(dXNT0[:, :], xnT[:]), "SP2", 16)
                        P.inc(sy.dma_start(dLN[:, :], lnst[:]), "SP2", 16)
                        P.inc(sy.dma_start(dXT0[:, :], xt[1][:]), "SP2", 16)

                units = [("k", 0), ("k", 1), ("q", 0), ("q", 1), ("v", 0), ("v", 1)]
                for ui, (wname, cc) in enumerate(units):
                    guid = half * 6 + ui
                    wpar = guid % 2

                    @block.gpsimd
                    def _(g, wname=wname, cc=cc, wpar=wpar, guid=guid):
                        P.w(g, "gpsimd", ev_unit_mm.get(guid - 2))
                        bx["e"] = P.inc(
                            g.dma_start(Wbuf[wpar][:], wd_map[wname][cc, :, :]),
                            f"LW{wpar}", 16)
                    ev_W = bx["e"]

                    for s, tt in enumerate(tiles):
                        uid = guse[0]
                        guse[0] += 1
                        sl_ap = SLICES[uid % 6]

                        @block.tensor
                        def _(t, s=s, wpar=wpar, sl_ap=sl_ap, uid=uid, guid=guid,
                              last=(s == 7)):
                            P.w(t, "tensor", ev_W, ev_xnT_done)
                            if uid == 4:
                                P.w(t, "tensor", ev_sclrep)
                            for fev in ev_slice_free.pop(uid - 6, []):
                                P.w(t, "tensor", fev)
                            for dc in range(16):
                                ins = t.matmul(sl_ap,
                                         xnT[:, s * 2048 + dc * 128:s * 2048 + (dc + 1) * 128],
                                         Wbuf[wpar][:, dc * 512:(dc + 1) * 512],
                                         start=(dc == 0), stop=(dc == 15))
                            ev = P.inc(ins, "PE")
                            bx["e"] = ev
                            if last:
                                ev_unit_mm[guid] = ev
                        ev_mm = bx["e"]

                        frees = []
                        if wname == "v":

                            @block.vector
                            def _(v, tt=tt, cc=cc, sl_ap=sl_ap):
                                P.w(v, "vector", ev_mm)
                                bx["e"] = P.inc(v.tensor_copy(
                                    Vsb[:, tt * 1024 + cc * 512:tt * 1024 + (cc + 1) * 512],
                                    sl_ap), "DV")
                            frees.append(bx["e"])
                        else:

                            @block.scalar
                            def _(sc, sl_ap=sl_ap):
                                P.w(sc, "scalar", ev_mm, ev_evac_prev[0])
                                for lh in range(4):
                                    sc.activation(sqd[:, 0:128],
                                                  sl_ap[:, lh * 128:(lh + 1) * 128],
                                                  AF.Square, accum_out=sst[:, lh:lh + 1])
                                sc.activation(sst[:, 4:8], sst[:, 0:4], AF.Ln,
                                              scale=cst[:, 3:4], bias=cst[:, 4:5])
                                sc.activation(sst[:, 8:12], sst[:, 4:8], AF.Exp,
                                              scale=cst[:, 7:8])
                                bx["e"] = P.inc(
                                    sc.activation(dmy[:], stab[:, 0:512], AF.Copy), "AC")
                            ev_ss = bx["e"]
                            frees.append(ev_ss)

                            @block.vector
                            def _(v, tt=tt, cc=cc, wname=wname, sl_ap=sl_ap):
                                P.w(v, "vector", ev_ss)
                                if wname == "q":
                                    P.w(v, "vector", ev_lps)
                                    v.tensor_tensor(sst[:, 12:16], sst[:, 8:12],
                                                    lps[:, tt * 8 + cc * 4:tt * 8 + cc * 4 + 4],
                                                    op=OP.mult)
                                    v.tensor_copy(dmy[:], stab[:, 0:512])
                                    scol = 12
                                else:
                                    scol = 8
                                P.w(v, "vector", ev_rot_prev[0])
                                for lh in range(4):
                                    v.tensor_scalar_mul(stg[:, lh * 128:(lh + 1) * 128],
                                                        sl_ap[:, lh * 128:(lh + 1) * 128],
                                                        sst[:, scol + lh:scol + lh + 1])
                                bx["e"] = P.inc(v.tensor_copy(dmy[:], stab[:, 0:512]), "DV")
                            ev_evac = bx["e"]
                            ev_evac_prev[0] = ev_evac
                            frees.append(ev_evac)

                            @block.gpsimd
                            def _(g, tt=tt):
                                P.w(g, "gpsimd", ev_evac, ev_rt_prev[0])
                                c3 = ctab[:, tt * 128:(tt + 1) * 128].unsqueeze(1) \
                                    .broadcast_to([128, 4, 128])
                                s3 = stab[:, tt * 128:(tt + 1) * 128].unsqueeze(1) \
                                    .broadcast_to([128, 4, 128])
                                q3 = stg[:, 0:512].rearrange("p (h d) -> p h d", h=4)
                                sw = bass.AP(stg, 64, [[3 * 512, 128], [128, 4], [-64, 2], [1, 64]])
                                rotA = stg[:, 512:1024].rearrange("p (h d) -> p h d", h=4)
                                rotT = stg[:, 1024:1536].rearrange("p (h d) -> p h d", h=4)
                                g.tensor_tensor(rotA, q3, c3, op=OP.mult)
                                g.tensor_tensor(rotT, sw, s3, op=OP.mult)
                                ev = P.inc(g.tensor_tensor(
                                    stg[:, 512:1024], stg[:, 512:1024],
                                    stg[:, 1024:1536], op=OP.add), "GP")
                                bx["e"] = ev
                                ev_rot_prev[0] = ev
                            ev_rot = bx["e"]

                            hsel = uid % 2

                            @block.tensor
                            def _(t, hsel=hsel):
                                P.w(t, "tensor", ev_rot, ev_ptA_copy[hsel])
                                for lh in range(4):
                                    ins = t.transpose(
                                        ptA[:, hsel * 1024 + lh * 128:hsel * 1024 + (lh + 1) * 128],
                                        stg[:, 512 + lh * 128:512 + (lh + 1) * 128],
                                        identb[:])
                                ev = P.inc(ins, "PE")
                                bx["e"] = ev
                                ev_rt_prev[0] = ev
                            ev_rt = bx["e"]

                            @block.vector
                            def _(v, tt=tt, cc=cc, wname=wname, hsel=hsel):
                                P.w(v, "vector", ev_rt)
                                dstt = KT if wname == "k" else QT
                                dst = bass.AP(dstt, cc * 4 * 2048 + tt * 128,
                                              [[16384, 128], [2048, 4], [1, 128]])
                                src = ptA[:, hsel * 1024:hsel * 1024 + 512].rearrange(
                                    "p (h d) -> p h d", h=4)
                                ev = P.inc(v.tensor_copy(dst, src), "DV")
                                bx["e"] = ev
                                ev_ptA_copy[hsel] = ev
                            frees.append(bx["e"])
                        ev_slice_free[uid] = frees

            ev_A_done_dv = ("DV", P.cnt["DV"])
            ev_A_done_ac = ("AC", P.cnt["AC"])

            if dbg:
                @block.sync
                def _(sy):
                    P.w(sy, "sync", ev_A_done_dv, ev_A_done_ac)
                    P.inc(sy.dma_start(dKT[:, :], KT[:]), "SP2", 16)
                    P.inc(sy.dma_start(dSCL[:, :], sclrep[:]), "SP2", 16)
                    P.inc(sy.dma_start(dSST[:, :], sst[:]), "SP2", 16)
                    P.inc(sy.dma_start(dQT[:, :], QT[:]), "SP2", 16)
                    P.inc(sy.dma_start(dV[:, :], Vsb[:]), "SP2", 16)
                    P.inc(sy.dma_start(dXNT[:, :], xnT[:]), "SP2", 16)

        with ExitStack() as esB:
            EB = esB.enter_context
            E1 = EB(nc.sbuf_tensor("E1_s", [128, 2048], F32))
            E2 = EB(nc.sbuf_tensor("E2_s", [128, 2048], F32))
            Ph = EB(nc.sbuf_tensor("Ph_s", [128, 2048], BF16))
            PhT = EB(nc.sbuf_tensor("PhT_s", [128, 2048], BF16))
            ysb = [EB(nc.sbuf_tensor(f"ysb{i}_s", [128, 1024], BF16)) for i in range(2)]

            @block.vector
            def _(v):
                P.w(v, "vector", ev_tabs)
                v.tensor_tensor(lsc[:, 0:64], lam[:, 0:64], lam[:, 64:128], op=OP.mult)
                v.tensor_copy(dmy[:], stab[:, 0:512])
                v.reduce_sum(lsc[:, 64:65], lsc[:, 0:64], axis=AX.X)
                v.tensor_tensor(lsc[:, 0:64], lam[:, 128:192], lam[:, 192:256], op=OP.mult)
                v.tensor_copy(dmy[:], stab[:, 0:512])
                v.reduce_sum(lsc[:, 65:66], lsc[:, 0:64], axis=AX.X)
                bx["e"] = P.inc(v.tensor_copy(dmy[:], stab[:, 0:512]), "DV")
            ev_lam = bx["e"]

            @block.scalar
            def _(sc):
                P.w(sc, "scalar", ev_lam)
                sc.activation(lsc[:, 66:68], lsc[:, 64:66], AF.Exp)
                bx["e"] = P.inc(sc.activation(dmy[:], stab[:, 0:512], AF.Copy), "AC")
            ev_lexp = bx["e"]

            @block.vector
            def _(v):
                P.w(v, "vector", ev_lexp)
                v.tensor_tensor(lsc[:, 68:69], lsc[:, 67:68], lsc[:, 66:67], op=OP.subtract)
                v.tensor_copy(dmy[:], stab[:, 0:512])
                v.tensor_tensor(lsc[:, 69:70], lsc[:, 68:69], cst[0:1, 5:6], op=OP.add)
                bx["e"] = P.inc(v.tensor_copy(dmy[:], stab[:, 0:512]), "DV")
            ev_negl = bx["e"]

            @block.tensor
            def _(t):
                P.w(t, "tensor", ev_negl, ev_A_done_dv)
                ins = t.matmul(psS2[:, 0:1], onesf[:, :], lsc[:, 69:70], start=True, stop=True)
                bx["e"] = P.inc(ins, "PE")
            ev_neglmm = bx["e"]

            @block.vector
            def _(v):
                P.w(v, "vector", ev_neglmm)
                v.tensor_copy(neglb[:], psS2[:, 0:1])
                bx["e"] = P.inc(v.tensor_copy(dmy[:], stab[:, 0:512]), "DV")
            ev_neglb = bx["e"]

            ev_exp1 = {}
            ev_exp2a = {}
            ev_exp2b = {}
            ev_yevac = {}
            ev_ypark = {}
            ev_phtcopy = {}
            ev_ph = {}
            ev_pv = {}
            ev_trlast = {}
            phsel = [0]

            for r in range(16):
                L = (r + 1) * 128
                L2a = min(L, 1024)
                L2b = L - L2a
                b0 = 3072 - L
                rpar = r % 2
                for lv in range(4):
                    p = r * 4 + lv
                    lh1, lh2 = lv, 4 + lv

                    @block.tensor
                    def _(t, r=r, L=L, lh1=lh1, p=p):
                        P.w(t, "tensor", ev_exp1.get(p - 1), ev_exp2b.get(p - 1),
                            ev_A_done_dv, ev_A_done_ac)
                        if dbg:
                            P.w(t, "tensor", ("SP2", P.cnt["SP2"]))
                        for (c0, w_) in chunks_aligned(0, L):
                            ins = t.matmul(psS[:, c0:c0 + w_],
                                     QT[:, lh1 * 2048 + r * 128:lh1 * 2048 + (r + 1) * 128],
                                     KT[:, lh1 * 2048 + c0:lh1 * 2048 + c0 + w_],
                                     start=True, stop=True)
                        bx["e"] = P.inc(ins, "PE")
                    ev_s1 = bx["e"]

                    @block.vector
                    def _(v, L=L, zb=12 * (p % 2)):
                        P.w(v, "vector", ev_s1)
                        v.tensor_tensor(psS[:, L - 128:L], psS[:, L - 128:L], maskd[:],
                                        op=OP.add)
                        v.tensor_reduce(zst[:, zb:zb + 1], psS[:, 0:L],
                                        axis=AX.X, op=OP.max, negate=True)
                        bx["e"] = P.inc(v.tensor_copy(dmy[:], stab[:, 0:512]), "DV")
                    ev_m1 = bx["e"]

                    @block.scalar
                    def _(sc, L=L, p=p, zb=12 * (p % 2)):
                        P.w(sc, "scalar", ev_m1, ev_ph.get(p - 1))
                        ev = P.inc(sc.activation(E1[:, 0:L], psS[:, 0:L], AF.Exp,
                                                 bias=zst[:, zb:zb + 1],
                                                 accum_out=zst[:, zb + 2:zb + 3]), "AC")
                        ev_exp1[p] = ev

                    @block.tensor
                    def _(t, r=r, L=L, L2a=L2a, L2b=L2b, b0=b0, lh2=lh2, p=p):
                        P.w(t, "tensor", ev_exp2a.get(p - 1), ev_yevac.get(p - 1),
                            ev_neglb, ev_A_done_ac)
                        for (c0, w_) in chunks_aligned(0, L2a):
                            ins = t.matmul(psS2[:, c0:c0 + w_],
                                     QT[:, lh2 * 2048 + r * 128:lh2 * 2048 + (r + 1) * 128],
                                     KT[:, lh2 * 2048 + c0:lh2 * 2048 + c0 + w_],
                                     start=True, stop=True)
                        ev = P.inc(ins, "PE")
                        bx["ea"] = ev
                        if L2b:
                            if L > 1536:
                                P.w(t, "tensor", ev_exp1[p])
                            P.w(t, "tensor", ev_exp2b.get(p - 1))
                            for (c0, w_) in chunks_aligned(b0, 2048):
                                kk = c0 + L - 2048
                                ins = t.matmul(psS[:, c0:c0 + w_],
                                         QT[:, lh2 * 2048 + r * 128:lh2 * 2048 + (r + 1) * 128],
                                         KT[:, lh2 * 2048 + kk:lh2 * 2048 + kk + w_],
                                         start=True, stop=True)
                            ev = P.inc(ins, "PE")
                        bx["eb"] = ev
                    ev_s2a, ev_s2b = bx["ea"], bx["eb"]

                    @block.vector
                    def _(v, L=L, L2a=L2a, L2b=L2b, b0=b0, zb=12 * (p % 2)):
                        P.w(v, "vector", ev_s2a, ev_s2b)
                        if L2b:
                            v.tensor_tensor(psS[:, 1920:2048], psS[:, 1920:2048],
                                            maskd[:], op=OP.add)
                            v.tensor_reduce(zst[:, zb + 8:zb + 9], psS2[:, 0:L2a], axis=AX.X,
                                            op=OP.max)
                            v.tensor_reduce(zst[:, zb + 9:zb + 10], psS[:, b0:2048], axis=AX.X,
                                            op=OP.max)
                            v.tensor_copy(dmy[:], stab[:, 0:512])
                            v.tensor_reduce(zst[:, zb + 1:zb + 2],
                                            zst[:, zb + 8:zb + 10],
                                            axis=AX.X, op=OP.max, negate=True)
                            bx["e"] = P.inc(v.tensor_copy(dmy[:], stab[:, 0:512]), "DV")
                        else:
                            v.tensor_tensor(psS2[:, L - 128:L], psS2[:, L - 128:L],
                                            maskd[:], op=OP.add)
                            v.tensor_reduce(zst[:, zb + 1:zb + 2], psS2[:, 0:L2a],
                                            axis=AX.X, op=OP.max, negate=True)
                            bx["e"] = P.inc(v.tensor_copy(dmy[:], stab[:, 0:512]), "DV")
                    ev_m2 = bx["e"]

                    @block.scalar
                    def _(sc, L=L, L2a=L2a, L2b=L2b, b0=b0, p=p, zb=12 * (p % 2)):
                        P.w(sc, "scalar", ev_m2, ev_ph.get(p - 1))
                        ev = P.inc(sc.activation(E2[:, 0:L2a], psS2[:, 0:L2a], AF.Exp,
                                                 bias=zst[:, zb + 1:zb + 2],
                                                 accum_out=zst[:, zb + 3:zb + 4]), "AC")
                        ev_exp2a[p] = ev
                        if L2b:
                            ev = P.inc(sc.activation(E2[:, 1024:L], psS[:, b0:2048], AF.Exp,
                                                     bias=zst[:, zb + 1:zb + 2],
                                                     accum_out=zst[:, zb + 4:zb + 5]), "AC")
                        ev_exp2b[p] = ev
                        bx["e"] = ev
                    ev_e2done = bx["e"]

                    @block.vector
                    def _(v, L2b=L2b, p=p, zb=12 * (p % 2)):
                        P.w(v, "vector", ev_exp1[p], ev_e2done, ev_neglb)
                        if L2b:
                            v.tensor_tensor(zst[:, zb + 3:zb + 4], zst[:, zb + 3:zb + 4],
                                            zst[:, zb + 4:zb + 5], op=OP.add)
                        bx["e"] = P.inc(v.tensor_copy(dmy[:], stab[:, 0:512]), "DV")
                    ev_zc = bx["e"]

                    @block.scalar
                    def _(sc, p=p, zb=12 * (p % 2)):
                        P.w(sc, "scalar", ev_zc)
                        sc.activation(zst[:, zb + 6:zb + 8], zst[:, zb + 2:zb + 4], AF.Ln)
                        bx["e"] = P.inc(sc.activation(dmy[:], stab[:, 0:512], AF.Copy), "AC")
                    ev_lnz = bx["e"]

                    @block.vector
                    def _(v, p=p, zb=12 * (p % 2)):
                        P.w(v, "vector", ev_lnz)
                        v.tensor_tensor(zst[:, zb + 4:zb + 5], zst[:, zb + 6:zb + 7],
                                        zst[:, zb + 7:zb + 8], op=OP.subtract)
                        bx["e"] = P.inc(v.tensor_copy(dmy[:], stab[:, 0:512]), "DV")
                    ev_lzr = bx["e"]

                    @block.scalar
                    def _(sc, p=p, zb=12 * (p % 2)):
                        P.w(sc, "scalar", ev_lzr)
                        sc.activation(zst[:, zb + 5:zb + 6], zst[:, zb + 4:zb + 5], AF.Exp)
                        sc.activation(zst[:, zb + 7:zb + 8], zst[:, zb + 6:zb + 7], AF.Exp,
                                      scale=cst[:, 1:2])
                        bx["e"] = P.inc(sc.activation(dmy[:], stab[:, 0:512], AF.Copy), "AC")
                    ev_rz = bx["e"]

                    @block.vector
                    def _(v, p=p, zb=12 * (p % 2)):
                        P.w(v, "vector", ev_rz)
                        v.tensor_tensor(zst[:, zb + 5:zb + 6], zst[:, zb + 5:zb + 6],
                                        neglb[:], op=OP.mult)
                        bx["e"] = P.inc(v.tensor_copy(dmy[:], stab[:, 0:512]), "DV")
                    ev_c2 = bx["e"]

                    @block.gpsimd
                    def _(g, L=L, p=p, zb=12 * (p % 2)):
                        P.w(g, "gpsimd", ev_c2, ev_trlast.get(p - 1))
                        g.tensor_tensor(Ph[:, 0:L], E2[:, 0:L],
                                        zst[:, zb + 5:zb + 6].broadcast_to([128, L]), op=OP.mult)
                        ev = P.inc(g.tensor_tensor(Ph[:, 0:L], Ph[:, 0:L], E1[:, 0:L],
                                                   op=OP.add), "GP")
                        ev_ph[p] = ev
                        bx["e"] = ev
                    ev_phv = bx["e"]

                    ngr = (L + 511) // 512
                    for gi2 in range(ngr):
                        w_ = min(512, L - gi2 * 512)
                        ntr = (w_ + 127) // 128
                        hsel = phsel[0] % 2
                        phsel[0] += 1

                        @block.tensor
                        def _(t, gi2=gi2, ntr=ntr, hsel=hsel, p=p, last=(gi2 == ngr - 1)):
                            P.w(t, "tensor", ev_phv, ev_ptA_copy[hsel])
                            for ti in range(ntr):
                                ins = t.transpose(
                                    ptA[:, hsel * 1024 + ti * 128:hsel * 1024 + (ti + 1) * 128],
                                    Ph[:, gi2 * 512 + ti * 128:gi2 * 512 + (ti + 1) * 128],
                                    identb[:])
                            ev = P.inc(ins, "PE")
                            bx["e"] = ev
                            if last:
                                ev_trlast[p] = ev
                        ev_tr2 = bx["e"]

                        @block.vector
                        def _(v, gi2=gi2, w_=w_, hsel=hsel, p=p):
                            P.w(v, "vector", ev_tr2, ev_pv.get(p - 1))
                            ev = P.inc(v.tensor_copy(PhT[:, gi2 * 512:gi2 * 512 + w_],
                                                     ptA[:, hsel * 1024:hsel * 1024 + w_]),
                                       "DV")
                            bx["e"] = ev
                            ev_ptA_copy[hsel] = ev
                            ev_phtcopy[(p, gi2)] = ev

                    @block.tensor
                    def _(t, L=L, lv=lv, p=p):
                        P.w(t, "tensor", ev_yevac.get(p - 1), ev_neglb)
                        nkc = L // 128
                        for kc in range(nkc):
                            P.w(t, "tensor", ev_phtcopy[(p, kc // 4)])
                            ins = t.matmul(psS2[:, 512:768],
                                     PhT[:, kc * 128:(kc + 1) * 128],
                                     Vsb[:, kc * 1024 + lv * 256:kc * 1024 + (lv + 1) * 256],
                                     start=(kc == 0), stop=(kc == nkc - 1))
                        ev = P.inc(ins, "PE")
                        ev_pv[p] = ev
                        bx["e"] = ev
                    ev_pvv = bx["e"]

                    @block.scalar
                    def _(sc, lv=lv, rpar=rpar, p=p, r=r, zb=12 * (p % 2)):
                        P.w(sc, "scalar", ev_pvv, ev_c2)
                        if lv == 0:
                            P.w(sc, "scalar", ev_ypark.get(r - 2))
                        ev = P.inc(sc.activation(ysb[rpar][:, lv * 256:(lv + 1) * 256],
                                                 psS2[:, 512:768], AF.Copy,
                                                 scale=zst[:, zb + 7:zb + 8]), "AC")
                        ev_yevac[p] = ev

                if dbg and r == 15:
                    @block.sync
                    def _(sy):
                        P.w(sy, "sync", ev_yevac[15 * 4 + 3])
                        P.inc(sy.dma_start(dE1[:, :], E1[:]), "SP2", 16)
                        P.inc(sy.dma_start(dE2[:, :], E2[:]), "SP2", 16)
                        P.inc(sy.dma_start(dPhT[:, :], PhT[:]), "SP2", 16)
                        P.inc(sy.dma_start(dZ[:, :], zst[:]), "SP2", 16)

                @block.sync
                def _(sy, r=r, rpar=rpar):
                    P.w(sy, "sync", ev_yevac[r * 4 + 3])
                    ev = P.inc(sy.dma_start(ybuf_d[r * 128:(r + 1) * 128, :],
                                            ysb[rpar][:]), f"SY{r % 2}", 16)
                    ev_ypark[r] = ev

            @block.gpsimd
            def _(g):
                P.w(g, "gpsimd", ("SY0", P.cnt["SY0"]), ("SY1", P.cnt["SY1"]))
                if dbg:
                    P.w(g, "gpsimd", ("SP2", P.cnt["SP2"]))
                bx["e"] = P.inc(g.collective_compute(
                    "AllGather", OP.bypass,
                    replica_groups=[[0, 1], [2, 3], [4, 5], [6, 7]],
                    ins=[ybuf_d.ap().opt()], outs=[ygath_d.ap().opt()]), "CC")
            ev_cc = bx["e"]
            ev_B_done_dv = ("DV", P.cnt["DV"])
            ev_B_done_ac = ("AC", P.cnt["AC"])
            ev_B_done_pe = ("PE", P.cnt["PE"])
        esAB.close()

        with ExitStack() as esC:
            EC = esC.enter_context
            xn2T = EC(nc.sbuf_tensor("xn2T_s", [128, 16384], BF16))
            fuse = EC(nc.sbuf_tensor("fuse_s", [128, 32768], BF16))
            Wf = [EC(nc.sbuf_tensor(f"Wf{i}_s", [128, 8192], BF16)) for i in range(2)]
            gatesb = EC(nc.sbuf_tensor("gatesb_s", [128, 2048], BF16))
            upsb = EC(nc.sbuf_tensor("upsb_s", [128, 512], BF16))
            outtm = EC(nc.sbuf_tensor("outtm_s", [128, 8192], F32))
            x2t = EC(nc.sbuf_tensor("x2t_s", [128, 2048], F32))
            yc = EC(nc.sbuf_tensor("yc_s", [128, 2048], BF16))
            xn2 = EC(nc.sbuf_tensor("xn2_s", [128, 2048], BF16))
            dwn = EC(nc.sbuf_tensor("dwn_s", [128, 512], F32))

            ev_park = {}
            ev_x2done = {}
            for tp in range(8):

                @block.gpsimd
                def _(g, tp=tp):
                    P.w(g, "gpsimd", ev_cc, ev_tabs, ev_x2done.get(tp - 1))
                    P.inc(g.indirect_dma_start(
                        yc[:, 0:1024], None, ygath_d[:, :],
                        IndirectOffsetOnAxis(idxg[:, tp:tp + 1], 0)), "LYC", 16)
                    bx["e"] = P.inc(g.indirect_dma_start(
                        yc[:, 1024:2048], None, ygath_d[:, :],
                        IndirectOffsetOnAxis(idxg[:, 8 + tp:9 + tp], 0)), "LYC", 16)
                ev_yc = bx["e"]

                @block.sync
                def _(sy, tp=tp):
                    P.w(sy, "sync", ev_x2done.get(tp - 1))
                    bx["e"] = P.inc(
                        sy.dma_start(x2t[:], xown_d[tp * 128:(tp + 1) * 128, :]),
                        "LXO", 16)
                ev_xo = bx["e"]

                @block.vector
                def _(v):
                    P.w(v, "vector", ev_yc, ev_xo)
                    bx["e"] = P.inc(v.tensor_tensor(x2t[:], x2t[:], yc[:], op=OP.add),
                                    "DV")
                ev_x2 = bx["e"]

                @block.sync
                def _(sy, tp=tp):
                    P.w(sy, "sync", ev_x2)
                    ev = P.inc(sy.dma_start(out_d[tp * 128:(tp + 1) * 128, :], x2t[:]),
                               "SP2", 16)
                    ev_park[tp] = ev

                @block.scalar
                def _(sc):
                    P.w(sc, "scalar", ev_x2)
                    bx["e"] = P.inc(sc.activation(sqd[:], x2t[:], AF.Square,
                                                  accum_out=lnst[:, 0:1]), "AC")
                ev_sq2 = bx["e"]

                @block.vector
                def _(v):
                    P.w(v, "vector", ev_sq2)
                    v.reduce_sum(lnst[:, 1:2], x2t[:], axis=AX.X)
                    v.tensor_copy(dmy[:], stab[:, 0:512])
                    v.tensor_scalar_mul(lnst[:, 2:3], lnst[:, 1:2], cst[:, 0:1])
                    bx["e"] = P.inc(v.tensor_copy(dmy[:], stab[:, 0:512]), "DV")
                ev_mean2 = bx["e"]

                @block.scalar
                def _(sc):
                    P.w(sc, "scalar", ev_mean2)
                    sc.activation(lnst[:, 3:4], lnst[:, 2:3], AF.Square)
                    bx["e"] = P.inc(sc.activation(dmy[:], stab[:, 0:512], AF.Copy), "AC")
                ev_msq2 = bx["e"]

                @block.vector
                def _(v):
                    P.w(v, "vector", ev_msq2)
                    v.tensor_scalar(lnst[:, 4:5], lnst[:, 3:4], cst[:, 1:2], cst[:, 2:3],
                                    op0=OP.mult, op1=OP.add)
                    bx["e"] = P.inc(v.tensor_copy(dmy[:], stab[:, 0:512]), "DV")
                ev_st2 = bx["e"]

                @block.scalar
                def _(sc):
                    P.w(sc, "scalar", ev_st2)
                    sc.activation(lnst[:, 5:6], lnst[:, 0:1], AF.Ln,
                                  scale=cst[:, 0:1], bias=lnst[:, 4:5])
                    sc.activation(lnst[:, 6:7], lnst[:, 5:6], AF.Exp,
                                  scale=cst[:, 7:8])
                    bx["e"] = P.inc(sc.activation(dmy[:], stab[:, 0:512], AF.Copy), "AC")
                ev_sd2 = bx["e"]

                @block.vector
                def _(v, tp=tp):
                    P.w(v, "vector", ev_sd2)
                    ev = P.inc(v.tensor_scalar(xn2[:], x2t[:], lnst[:, 2:3], lnst[:, 6:7],
                                               op0=OP.subtract, op1=OP.mult), "DV")
                    bx["e"] = ev
                    ev_x2done[tp] = ev
                ev_xn2 = bx["e"]

                for g2 in range(4):
                    hsel = g2 % 2

                    @block.tensor
                    def _(t, g2=g2, hsel=hsel):
                        P.w(t, "tensor", ev_xn2, ev_ptA_copy[hsel], ev_B_done_dv)
                        for d4 in range(4):
                            dc = g2 * 4 + d4
                            ins = t.transpose(
                                ptA[:, hsel * 1024 + d4 * 128:hsel * 1024 + (d4 + 1) * 128],
                                xn2[:, dc * 128:(dc + 1) * 128], identb[:])
                        bx["e"] = P.inc(ins, "PE")
                    ev_t2 = bx["e"]

                    @block.vector
                    def _(v, g2=g2, hsel=hsel, tp=tp):
                        P.w(v, "vector", ev_t2)
                        dst = bass.AP(xn2T, (g2 * 4) * 1024 + tp * 128,
                                      [[16384, 128], [1024, 4], [1, 128]])
                        src = ptA[:, hsel * 1024:hsel * 1024 + 512].rearrange(
                            "p (h d) -> p h d", h=4)
                        ev = P.inc(v.tensor_copy(dst, src), "DV")
                        bx["e"] = ev
                        ev_ptA_copy[hsel] = ev
            ev_xn2T_done = ("DV", P.cnt["DV"])

            GP_SL = [psS2[:, 0:512], psS[:, 0:512]]
            UP_SL = [psS[:, 512:1024], psS2[:, 512:1024]]
            ev_silu = {}
            ev_upev = {}
            ev_fuse_ev = {}
            ev_dts = {}
            ev_dcopy = {}
            ev_acc = {}
            ev_gmm = {}
            ev_umm = {}
            ev_dmm = {}
            ev_dtr_prev = [None]

            for g in range(2):
                for f_i in range(16):

                    @block.gpsimd
                    def _(gp, g=g, f_i=f_i):
                        P.w(gp, "gpsimd", ev_gmm.get((g, f_i - 1)),
                            ev_dmm.get((g - 1, 15)))
                        bx["e0"] = P.inc(gp.dma_start(Wf[0][:], wg_d[f_i, :, :]), "LW0", 16)
                        P.w(gp, "gpsimd", ev_umm.get((g, f_i - 1)))
                        bx["e"] = P.inc(gp.dma_start(Wf[1][:], wu_d[f_i, :, :]), "LW1", 16)
                    ev_Wg0 = bx["e0"]
                    ev_Wgu = bx["e"]

                    for f128 in range(4):
                        ic = f_i * 4 + f128
                        gsl = GP_SL[ic % 2]
                        usl = UP_SL[ic % 2]

                        @block.tensor
                        def _(t, g=g, f_i=f_i, f128=f128, ic=ic, gsl=gsl, usl=usl):
                            P.w(t, "tensor", ev_Wg0, ev_Wgu, ev_xn2T_done,
                                ev_B_done_ac)
                            P.w(t, "tensor", ev_silu.get((g, ic - 2)))
                            if g > 0 and ic < 2:
                                P.w(t, "tensor", ev_dts.get((g - 1, 12 + ic)))
                            for kc in range(16):
                                ins = t.matmul(gsl,
                                         Wf[0][:, kc * 512 + f128 * 128:kc * 512 + (f128 + 1) * 128],
                                         xn2T[:, kc * 1024 + g * 512:kc * 1024 + (g + 1) * 512],
                                         start=(kc == 0), stop=(kc == 15))
                            ev = P.inc(ins, "PE")
                            ev_gmm[(g, f_i)] = ev
                            bx["eg"] = ev
                            P.w(t, "tensor", ev_upev.get((g, ic - 2)))
                            if g > 0 and ic < 2:
                                P.w(t, "tensor", ev_dts.get((g - 1, 13 + ic)))
                            for kc in range(16):
                                ins = t.matmul(usl,
                                         Wf[1][:, kc * 512 + f128 * 128:kc * 512 + (f128 + 1) * 128],
                                         xn2T[:, kc * 1024 + g * 512:kc * 1024 + (g + 1) * 512],
                                         start=(kc == 0), stop=(kc == 15))
                            ev = P.inc(ins, "PE")
                            ev_umm[(g, f_i)] = ev
                            bx["eu"] = ev
                        evg, evu = bx["eg"], bx["eu"]

                        @block.scalar
                        def _(sc, g=g, ic=ic, f128=f128, gsl=gsl):
                            P.w(sc, "scalar", evg, ev_fuse_ev.get((g, ic - 4)))
                            ev = P.inc(sc.activation(gatesb[:, f128 * 512:(f128 + 1) * 512],
                                                     gsl, AF.Silu,
                                                     scale=sgt[:, ic:ic + 1]), "AC")
                            ev_silu[(g, ic)] = ev
                            bx["e"] = ev
                        evsi = bx["e"]

                        @block.vector
                        def _(v, g=g, ic=ic, usl=usl):
                            P.w(v, "vector", evu, ev_fuse_ev.get((g, ic - 1)))
                            ev = P.inc(v.tensor_scalar_mul(upsb[:], usl,
                                                           sut[:, ic:ic + 1]), "DV")
                            ev_upev[(g, ic)] = ev
                            bx["e"] = ev
                        evup = bx["e"]

                        @block.gpsimd
                        def _(gp, g=g, ic=ic, f128=f128):
                            P.w(gp, "gpsimd", evsi, evup)
                            if ic == 0:
                                P.w(gp, "gpsimd", ev_dmm.get((g - 1, 15)))
                            ev = P.inc(gp.tensor_tensor(
                                fuse[:, ic * 512:(ic + 1) * 512],
                                gatesb[:, f128 * 512:(f128 + 1) * 512],
                                upsb[:], op=OP.mult), "GP")
                            ev_fuse_ev[(g, ic)] = ev

                ev_fuse_all = ("GP", P.cnt["GP"])
                for fc in range(16):

                    @block.gpsimd
                    def _(gp, g=g, fc=fc):
                        P.w(gp, "gpsimd", ev_dmm.get((g, fc - 2)))
                        if fc < 2:
                            P.w(gp, "gpsimd", ev_umm[(g, 15)])
                        bx["e"] = P.inc(gp.dma_start(Wf[fc % 2][:], wd_d[fc, :, :]),
                                        f"LW{fc % 2}", 16)
                    ev_Wd = bx["e"]

                    @block.tensor
                    def _(t, g=g, fc=fc):
                        P.w(t, "tensor", ev_Wd, ev_fuse_all)
                        P.w(t, "tensor", ev_dts.get((g, fc - 4)))
                        dsl = psS[:, (fc % 4) * 512:((fc % 4) + 1) * 512]
                        for ic in range(64):
                            ins = t.matmul(dsl,
                                     Wf[fc % 2][:, ic * 128:(ic + 1) * 128],
                                     fuse[:, ic * 512:(ic + 1) * 512],
                                     start=(ic == 0), stop=(ic == 63))
                        ev = P.inc(ins, "PE")
                        ev_dmm[(g, fc)] = ev
                        bx["e"] = ev
                    ev_dm = bx["e"]

                    @block.vector
                    def _(v, g=g, fc=fc):
                        P.w(v, "vector", ev_dm, ev_dtr_prev[0])
                        dsl = psS[:, (fc % 4) * 512:((fc % 4) + 1) * 512]
                        ev = P.inc(v.tensor_scalar_mul(dwn[:], dsl, sdt[:, fc:fc + 1]),
                                   "DV")
                        ev_dts[(g, fc)] = ev
                        bx["e"] = ev
                    ev_dt = bx["e"]

                    @block.tensor
                    def _(t, g=g, fc=fc):
                        P.w(t, "tensor", ev_dt, ev_dcopy.get((g, fc - 1)))
                        for tl in range(4):
                            ins = t.transpose(psS2[:, tl * 128:(tl + 1) * 128],
                                        dwn[:, tl * 128:(tl + 1) * 128], identf[:])
                        ev = P.inc(ins, "PE")
                        ev_dtr_prev[0] = ev
                        bx["e"] = ev
                    ev_dtr = bx["e"]

                    @block.vector
                    def _(v, g=g, fc=fc):
                        P.w(v, "vector", ev_dtr)
                        if fc == 0:
                            P.w(v, "vector", ev_acc.get(g - 1))
                        dst = bass.AP(outtm, fc * 128, [[8192, 128], [2048, 4], [1, 128]])
                        src = psS2[:, 0:512].rearrange("p (h d) -> p h d", h=4)
                        ev = P.inc(v.tensor_copy(dst, src), "DV")
                        ev_dcopy[(g, fc)] = ev

                for tl in range(4):
                    tp = g * 4 + tl

                    @block.gpsimd
                    def _(gp, g=g, tl=tl, tp=tp):
                        P.w(gp, "gpsimd", ev_dcopy[(g, 15)])
                        P.w(gp, "gpsimd", ("SP2", P.cnt["SP2"]))
                        ev = P.inc(gp.dma_start(out_d[tp * 128:(tp + 1) * 128, :],
                                                outtm[:, tl * 2048:(tl + 1) * 2048],
                                                accum_op=OP.add), "SG", 16)
                        bx["e"] = ev
                ev_acc[g] = ("SG", P.cnt["SG"])

            @block.gpsimd
            def _(gp):
                gp.wait_ge(sems["SG"], P.cnt["SG"])
                gp.wait_ge(sems["SP2"], P.cnt["SP2"])
                gp.wait_ge(sems["SY0"], P.cnt["SY0"])
                gp.wait_ge(sems["SY1"], P.cnt["SY1"])

            @block.sync
            def _(sy):
                sy.wait_ge(sems["SP2"], P.cnt["SP2"])

    return nc


_NC = None


def _get_nc():
    global _NC
    if _NC is None:
        _NC = build_program()
    return _NC


def _host_tables():
    i = np.arange(64, dtype=np.float64)
    inv_freq = 1.0 / (10000.0 ** (2 * i / HD))
    t = np.arange(T, dtype=np.float64)
    fr = t[:, None] * inv_freq[None, :]
    cos = np.cos(fr).astype(np.float32)
    sin = np.sin(fr).astype(np.float32)
    C = np.concatenate([cos, cos], axis=1)
    S = np.concatenate([sin, -sin], axis=1)
    ctab = np.ascontiguousarray(
        C.reshape(16, 128, 128).transpose(1, 0, 2).reshape(128, 2048)).astype(bf16)
    stab = np.ascontiguousarray(
        S.reshape(16, 128, 128).transpose(1, 0, 2).reshape(128, 2048)).astype(bf16)
    logpos = np.log(np.arange(1, T + 1, dtype=np.float32))
    logpt = np.ascontiguousarray(logpos.reshape(16, 128).T).astype(np.float32)
    p = np.arange(128)
    ii = np.arange(128)
    maskd = np.where(ii[None, :] <= p[:, None], 0.0, NEG).astype(np.float32).astype(bf16)
    identb = np.eye(128, dtype=np.float32).astype(bf16)
    identf = np.eye(128, dtype=np.float32)
    onesf = np.ones((1, 128), np.float32)
    return ctab, stab, logpt, maskd, identb, identf, onesf


def make_inmaps(L0_attn_x, L0_attn_Wq, L0_attn_Wk, L0_attn_Wv,
                L0_attn_lambda_q1, L0_attn_lambda_k1, L0_attn_lambda_q2, L0_attn_lambda_k2,
                L0_attn_softmax_scaler, L0_ffn_w_gate, L0_ffn_w_up, L0_ffn_w_down,
                L0_ffn_s_gate, L0_ffn_s_up, L0_ffn_s_down):
    x = np.asarray(L0_attn_x, dtype=np.float32)
    Wq = np.asarray(L0_attn_Wq, dtype=np.float32)
    Wk = np.asarray(L0_attn_Wk, dtype=np.float32)
    Wv = np.asarray(L0_attn_Wv, dtype=np.float32)
    Wg = np.asarray(L0_ffn_w_gate)
    Wu = np.asarray(L0_ffn_w_up)
    Wd = np.asarray(L0_ffn_w_down)
    sg = np.asarray(L0_ffn_s_gate, dtype=np.float32)
    su = np.asarray(L0_ffn_s_up, dtype=np.float32)
    sd = np.asarray(L0_ffn_s_down, dtype=np.float32)
    scaler = np.asarray(L0_attn_softmax_scaler, dtype=np.float32).reshape(16)
    lam = np.concatenate([np.asarray(v, np.float32).reshape(64) for v in
                          (L0_attn_lambda_q1, L0_attn_lambda_k1,
                           L0_attn_lambda_q2, L0_attn_lambda_k2)]).reshape(1, 256)

    ctab, stab, logpt, maskd, identb, identf, onesf = _host_tables()
    cstv = np.broadcast_to(np.array([1.0 / D, -1.0, EPS_LN, 1.0 / HD, EPS_RMS,
                                     -LAMBDA_INIT, 1.0 / math.sqrt(HD), -0.5],
                                    np.float32), (128, 8)).copy()
    wg_r = np.ascontiguousarray(
        Wg.reshape(16, 128, 16, 512).transpose(2, 1, 0, 3).reshape(16, 128, 8192))
    wu_r = np.ascontiguousarray(
        Wu.reshape(16, 128, 16, 512).transpose(2, 1, 0, 3).reshape(16, 128, 8192))
    wd_r = np.ascontiguousarray(
        Wd.reshape(64, 128, 16, 128).transpose(2, 1, 0, 3).reshape(16, 128, 8192))
    sgt = np.ascontiguousarray(sg.reshape(64, 128).T)
    sut = np.ascontiguousarray(su.reshape(64, 128).T)
    sdt = np.ascontiguousarray(sd.reshape(16, 128).T)

    in_maps = []
    for c in range(8):
        b, j = c // 2, c % 2
        qheads = [4 * j + i for i in range(4)] + [8 + 4 * j + i for i in range(4)]
        qcols = np.concatenate([np.arange(h * 128, (h + 1) * 128) for h in qheads])
        vcols = np.arange(4 * j * 256, 4 * j * 256 + 1024)

        def tile_w(W, cols):
            Ws = np.ascontiguousarray(W[:, cols])
            return np.ascontiguousarray(
                Ws.reshape(16, 128, 2, 512).transpose(2, 1, 0, 3).reshape(2, 128, 8192))

        pp = np.arange(128, dtype=np.int64)
        idxg = np.zeros((128, 16), np.int32)
        for tp in range(8):
            idxg[:, tp] = j * 1024 + tp * 128 + pp
            idxg[:, 8 + tp] = 2048 + j * 1024 + tp * 128 + pp

        in_maps.append({
            "x": np.ascontiguousarray(x[b]),
            "xown": np.ascontiguousarray(x[b, j * 1024:(j + 1) * 1024]),
            "wq": tile_w(Wq, qcols),
            "wk": tile_w(Wk, qcols),
            "wv": tile_w(Wv, vcols),
            "wg": wg_r, "wu": wu_r, "wd": wd_r,
            "sgt": sgt, "sut": sut, "sdt": sdt,
            "lam": lam,
            "scl": np.ascontiguousarray(scaler[qheads]).reshape(1, 8),
            "ctab": ctab, "stab": stab, "logpt": logpt, "maskd": maskd,
            "identb": identb, "identf": identf, "onesf": onesf,
            "idxg": idxg, "cst": cstv,
        })

    return in_maps


def kernel(**inputs):
    in_maps = make_inmaps(**inputs)
    nc = _get_nc()
    res = run_bass_kernel_spmd(nc, in_maps, core_ids=list(range(8)))
    out = np.empty((4, T, D), np.float32)
    for c in range(8):
        b, j = c // 2, c % 2
        out[b, j * 1024:(j + 1) * 1024, :] = np.asarray(res.results[c]["out"])
    return out


# version 02
# speedup vs baseline: 1.0424x; 1.0424x over previous
import math
from contextlib import ExitStack

import numpy as np
import ml_dtypes
import concourse.bass as bass
import concourse.mybir as mybir
from concourse.bass_core_rust import IndirectOffsetOnAxis
from concourse.bass_utils import run_bass_kernel_spmd

F32 = mybir.dt.float32
BF16 = mybir.dt.bfloat16
I8 = mybir.dt.int8
I32 = mybir.dt.int32
AF = mybir.ActivationFunctionType
OP = mybir.AluOpType
AX = mybir.AxisListType
bf16 = ml_dtypes.bfloat16

T = 2048
D = 2048
HD = 128
INTER = 8192
EPS_RMS = float(np.finfo(np.float32).eps)
EPS_LN = 1e-5
LAMBDA_INIT = 0.2
NEG = -1e30


class Prog:

    def __init__(self, nc, sems, serial=False):
        self.nc = nc
        self.sem = sems
        self.serial = serial
        self.cnt = {k: 0 for k in sems}
        self.lastw = {e: {k: 0 for k in sems} for e in
                      ("tensor", "vector", "scalar", "gpsimd", "sync")}
        self.own = {"tensor": "PE", "vector": "DV", "scalar": "AC", "gpsimd": "GP"}

    def inc(self, instr, s, n=1):
        instr.then_inc(self.sem[s], n)
        self.cnt[s] += n
        return (s, self.cnt[s])

    def w(self, eng, engname, *evs):
        if self.serial:
            evs = tuple(evs) + tuple((s, self.cnt[s]) for s in self.cnt
                                     if s not in ("CC",))
        for ev in evs:
            if ev is None:
                continue
            s, v = ev
            if v <= 0:
                continue
            if self.own.get(engname) == s:
                continue
            if self.lastw[engname][s] >= v:
                continue
            eng.wait_ge(self.sem[s], v)
            self.lastw[engname][s] = v


def chunks_aligned(c0, c1):
    out = []
    c = c0
    while c < c1:
        nxt = min(c1, (c // 512 + 1) * 512)
        out.append((c, nxt - c))
        c = nxt
    return out


DBG = False


def build_program(dbg=False, serial=False):
    nc = bass.Bass("TRN2", target_bir_lowering=False, detect_race_conditions=False)

    x_d = nc.dram_tensor("x", [T, D], F32, kind="ExternalInput")
    xown_d = nc.dram_tensor("xown", [1024, D], F32, kind="ExternalInput")
    wq_d = nc.dram_tensor("wq", [2, 128, 8192], F32, kind="ExternalInput")
    wk_d = nc.dram_tensor("wk", [2, 128, 8192], F32, kind="ExternalInput")
    wv_d = nc.dram_tensor("wv", [2, 128, 8192], F32, kind="ExternalInput")
    wg_d = nc.dram_tensor("wg", [16, 128, 8192], I8, kind="ExternalInput")
    wu_d = nc.dram_tensor("wu", [16, 128, 8192], I8, kind="ExternalInput")
    wd_d = nc.dram_tensor("wd", [16, 128, 8192], I8, kind="ExternalInput")
    sgt_d = nc.dram_tensor("sgt", [128, 64], F32, kind="ExternalInput")
    sut_d = nc.dram_tensor("sut", [128, 64], F32, kind="ExternalInput")
    sdt_d = nc.dram_tensor("sdt", [128, 16], F32, kind="ExternalInput")
    lam_d = nc.dram_tensor("lam", [1, 256], F32, kind="ExternalInput")
    scl_d = nc.dram_tensor("scl", [1, 8], F32, kind="ExternalInput")
    ctab_d = nc.dram_tensor("ctab", [128, 2048], BF16, kind="ExternalInput")
    stab_d = nc.dram_tensor("stab", [128, 2048], BF16, kind="ExternalInput")
    logpt_d = nc.dram_tensor("logpt", [128, 16], F32, kind="ExternalInput")
    maskd_d = nc.dram_tensor("maskd", [128, 128], BF16, kind="ExternalInput")
    identb_d = nc.dram_tensor("identb", [128, 128], BF16, kind="ExternalInput")
    identf_d = nc.dram_tensor("identf", [128, 128], F32, kind="ExternalInput")
    onesf_d = nc.dram_tensor("onesf", [1, 128], F32, kind="ExternalInput")
    idxg_d = nc.dram_tensor("idxg", [128, 16], I32, kind="ExternalInput")
    cst_d = nc.dram_tensor("cst", [128, 8], F32, kind="ExternalInput")
    out_d = nc.dram_tensor("out", [1024, D], F32, kind="ExternalOutput")
    ybuf_d = nc.dram_tensor("ybuf", [T, 1024], BF16)
    ygath_d = nc.dram_tensor("ygath", [2 * T, 1024], BF16)
    if dbg:
        dKT = nc.dram_tensor("dKT", [128, 16384], BF16, kind="ExternalOutput")
        dQT = nc.dram_tensor("dQT", [128, 16384], BF16, kind="ExternalOutput")
        dV = nc.dram_tensor("dV", [128, 16384], BF16, kind="ExternalOutput")
        dXNT = nc.dram_tensor("dXNT", [128, 16384], BF16, kind="ExternalOutput")
        dXNT0 = nc.dram_tensor("dXNT0", [128, 16384], BF16, kind="ExternalOutput")
        dXT0 = nc.dram_tensor("dXT0", [128, 2048], F32, kind="ExternalOutput")
        dLN = nc.dram_tensor("dLN", [128, 8], F32, kind="ExternalOutput")
        dSCL = nc.dram_tensor("dSCL", [128, 8], F32, kind="ExternalOutput")
        dSST = nc.dram_tensor("dSST", [128, 16], F32, kind="ExternalOutput")
        dE1 = nc.dram_tensor("dE1", [128, 2048], F32, kind="ExternalOutput")
        dE2 = nc.dram_tensor("dE2", [128, 2048], F32, kind="ExternalOutput")
        dPhT = nc.dram_tensor("dPhT", [128, 2048], BF16, kind="ExternalOutput")
        dZ = nc.dram_tensor("dZ", [128, 24], F32, kind="ExternalOutput")
        dNL = nc.dram_tensor("dNL", [128, 1], F32, kind="ExternalOutput")
        dLS = nc.dram_tensor("dLS", [1, 80], F32, kind="ExternalOutput")

    bx = {}

    es = ExitStack()
    E = es.enter_context
    with es:
        sems = {}
        for s in ("LT", "LX0", "LX1", "LXO", "LW0", "LW1", "LYC", "SY0", "SY1",
                  "SP2", "SG", "CC", "PE", "DV", "AC", "GP"):
            sems[s] = E(nc.semaphore(s))
        block = E(nc.Block())
        P = Prog(nc, sems, serial=serial)

        psS = E(nc.psum_tensor("psS", [128, 2048], F32))
        psS2 = E(nc.psum_tensor("psS2", [128, 512], F32))
        pvout = E(nc.psum_tensor("pvout", [128, 512], F32))
        ptA = E(nc.psum_tensor("ptA", [128, 2048], BF16))

        ctab = E(nc.sbuf_tensor("ctab_s", [128, 2048], BF16))
        stab = E(nc.sbuf_tensor("stab_s", [128, 2048], BF16))
        logpt = E(nc.sbuf_tensor("logpt_s", [128, 16], F32))
        maskd = E(nc.sbuf_tensor("maskd_s", [128, 128], BF16))
        identb = E(nc.sbuf_tensor("identb_s", [128, 128], BF16))
        identf = E(nc.sbuf_tensor("identf_s", [128, 128], F32))
        onesf = E(nc.sbuf_tensor("onesf_s", [1, 128], F32))
        scl = E(nc.sbuf_tensor("scl_s", [1, 8], F32))
        sclrep = E(nc.sbuf_tensor("sclrep_s", [128, 8], F32))
        lps = E(nc.sbuf_tensor("lps_s", [128, 128], F32))
        lam = E(nc.sbuf_tensor("lam_s", [1, 256], F32))
        lsc = E(nc.sbuf_tensor("lsc_s", [1, 80], F32))
        neglb = E(nc.sbuf_tensor("neglb_s", [128, 1], F32))
        sgt = E(nc.sbuf_tensor("sgt_s", [128, 64], F32))
        sut = E(nc.sbuf_tensor("sut_s", [128, 64], F32))
        sdt = E(nc.sbuf_tensor("sdt_s", [128, 16], F32))
        idxg = E(nc.sbuf_tensor("idxg_s", [128, 16], I32))
        cst = E(nc.sbuf_tensor("cst_s", [128, 8], F32))
        sqd = E(nc.sbuf_tensor("sqd_s", [128, 2048], BF16))
        lnst = E(nc.sbuf_tensor("lnst_s", [128, 8], F32))
        zst = E(nc.sbuf_tensor("zst_s", [128, 24], F32))
        dmy = E(nc.sbuf_tensor("dmy_s", [128, 512], BF16))

        @block.sync
        def _(sy):
            for dst, src in ((ctab, ctab_d), (stab, stab_d), (logpt, logpt_d),
                             (maskd, maskd_d), (identb, identb_d), (identf, identf_d),
                             (onesf, onesf_d), (scl, scl_d), (lam, lam_d),
                             (sgt, sgt_d), (sut, sut_d), (sdt, sdt_d), (idxg, idxg_d),
                             (cst, cst_d)):
                P.inc(sy.dma_start(dst[:], src[:, :]), "LT", 16)
        ev_tabs = ("LT", P.cnt["LT"])

        @block.tensor
        def _(t):
            P.w(t, "tensor", ev_tabs)
            ins = t.matmul(psS2[:, 0:8], onesf[:, :], scl[:, :], start=True, stop=True)
            bx["e"] = P.inc(ins, "PE")
        ev_sclmm = bx["e"]

        @block.vector
        def _(v):
            P.w(v, "vector", ev_sclmm, ev_tabs)
            v.tensor_scalar_mul(sclrep[:], psS2[:, 0:8], cst[:, 6:7])
            bx["e"] = P.inc(v.tensor_copy(dmy[:], stab[:, 0:512]), "DV")
        ev_sclrep = bx["e"]

        @block.vector
        def _(v):
            lp3 = logpt[:, :].unsqueeze(2).broadcast_to([128, 16, 8])
            sc3 = sclrep[:, :].unsqueeze(1).broadcast_to([128, 16, 8])
            o3 = lps[:, :].rearrange("p (t h) -> p t h", t=16)
            bx["e"] = P.inc(v.tensor_tensor(o3, lp3, sc3, op=OP.mult), "DV")
        ev_lps = bx["e"]

        esAB = ExitStack()
        KT = esAB.enter_context(nc.sbuf_tensor("KT_s", [128, 16384], BF16))
        QT = esAB.enter_context(nc.sbuf_tensor("QT_s", [128, 16384], BF16))
        Vsb = esAB.enter_context(nc.sbuf_tensor("Vsb_s", [128, 16384], BF16))

        ev_ptA_copy = [None, None]

        with ExitStack() as esA:
            EA = esA.enter_context
            xnT = EA(nc.sbuf_tensor("xnT_s", [128, 16384], BF16))
            xt = [EA(nc.sbuf_tensor(f"xt{i}_s", [128, 2048], F32)) for i in range(2)]
            xnb = EA(nc.sbuf_tensor("xnb_s", [128, 2048], BF16))
            Wbuf = [EA(nc.sbuf_tensor(f"Wb{i}_s", [128, 8192], BF16)) for i in range(2)]
            stg = EA(nc.sbuf_tensor("stg_s", [128, 3 * 512], BF16))
            sst = EA(nc.sbuf_tensor("sst_s", [128, 16], F32))

            ev_xt_free = {}
            ev_xnb_free = {}
            ev_unit_mm = {}
            ev_slice_free = {}
            ev_rot_prev = [None]
            ev_rt_prev = [None]
            ev_evac_prev = [None]
            guse = [0]
            SLICES = [psS[:, 0:512], psS[:, 512:1024], psS[:, 1024:1536],
                      psS[:, 1536:2048], psS2[:, 0:512], pvout[:, 0:512]]
            wd_map = {"k": wk_d, "q": wq_d, "v": wv_d}

            for half in range(2):
                tiles = list(range(8 * half, 8 * half + 8))
                for s, tt in enumerate(tiles):
                    gi = half * 8 + s
                    par = gi % 2

                    @block.sync
                    def _(sy, tt=tt, par=par, gi=gi):
                        P.w(sy, "sync", ev_xt_free.get(gi - 2))
                        if dbg and gi == 8:
                            P.w(sy, "sync", ("SP2", P.cnt["SP2"]))
                        bx["e"] = P.inc(
                            sy.dma_start(xt[par][:], x_d[tt * 128:(tt + 1) * 128, :]),
                            f"LX{gi % 2}", 16)
                    ev_x = bx["e"]

                    @block.scalar
                    def _(sc, par=par):
                        P.w(sc, "scalar", ev_x)
                        bx["e"] = P.inc(sc.activation(sqd[:], xt[par][:], AF.Square,
                                                      accum_out=lnst[:, 0:1]), "AC")
                    ev_sq = bx["e"]

                    @block.vector
                    def _(v, par=par):
                        P.w(v, "vector", ev_x, ev_sq)
                        v.reduce_sum(lnst[:, 1:2], xt[par][:], axis=AX.X)
                        v.tensor_copy(dmy[:], stab[:, 0:512])
                        v.tensor_scalar_mul(lnst[:, 2:3], lnst[:, 1:2], cst[:, 0:1])
                        bx["e"] = P.inc(v.tensor_copy(dmy[:], stab[:, 0:512]), "DV")
                    ev_mean = bx["e"]

                    @block.scalar
                    def _(sc):
                        P.w(sc, "scalar", ev_mean)
                        sc.activation(lnst[:, 3:4], lnst[:, 2:3], AF.Square)
                        bx["e"] = P.inc(sc.activation(dmy[:], stab[:, 0:512], AF.Copy), "AC")
                    ev_msq = bx["e"]

                    @block.vector
                    def _(v):
                        P.w(v, "vector", ev_msq)
                        v.tensor_scalar(lnst[:, 4:5], lnst[:, 3:4], cst[:, 1:2], cst[:, 2:3],
                                        op0=OP.mult, op1=OP.add)
                        bx["e"] = P.inc(v.tensor_copy(dmy[:], stab[:, 0:512]), "DV")
                    ev_stats = bx["e"]

                    @block.scalar
                    def _(sc):
                        P.w(sc, "scalar", ev_stats)
                        sc.activation(lnst[:, 5:6], lnst[:, 0:1], AF.Sqrt,
                                      scale=cst[:, 0:1], bias=lnst[:, 4:5])
                        bx["e"] = P.inc(sc.activation(dmy[:], stab[:, 0:512], AF.Copy), "AC")
                    ev_std = bx["e"]

                    @block.vector
                    def _(v, par=par, gi=gi):
                        P.w(v, "vector", ev_std, ev_xnb_free.get(gi - 1))
                        v.reciprocal(lnst[:, 6:7], lnst[:, 5:6])
                        v.tensor_copy(dmy[:], stab[:, 0:512])
                        ev = P.inc(v.tensor_scalar(xnb[:], xt[par][:], lnst[:, 2:3],
                                                   lnst[:, 6:7], op0=OP.subtract,
                                                   op1=OP.mult), "DV")
                        bx["e"] = ev
                        ev_xt_free[gi] = ev
                    ev_xn = bx["e"]

                    for g in range(4):
                        hsel = g % 2

                        @block.tensor
                        def _(t, g=g, hsel=hsel, gi=gi):
                            P.w(t, "tensor", ev_xn, ev_ptA_copy[hsel])
                            for d4 in range(4):
                                dc = g * 4 + d4
                                ins = t.transpose(
                                    ptA[:, hsel * 1024 + d4 * 128:hsel * 1024 + (d4 + 1) * 128],
                                    xnb[:, dc * 128:(dc + 1) * 128], identb[:])
                            ev = P.inc(ins, "PE")
                            bx["e"] = ev
                            if g == 3:
                                ev_xnb_free[gi] = ev
                        ev_tr = bx["e"]

                        @block.vector
                        def _(v, g=g, hsel=hsel, s=s):
                            P.w(v, "vector", ev_tr)
                            ev = P.inc(v.tensor_copy(
                                xnT[:, s * 2048 + g * 512:s * 2048 + (g + 1) * 512],
                                ptA[:, hsel * 1024:hsel * 1024 + 512]), "DV")
                            bx["e"] = ev
                            ev_ptA_copy[hsel] = ev
                ev_xnT_done = ("DV", P.cnt["DV"])

                if dbg and half == 0:
                    @block.sync
                    def _(sy):
                        P.w(sy, "sync", ev_xnT_done)
                        P.inc(sy.dma_start(dXNT0[:, :], xnT[:]), "SP2", 16)
                        P.inc(sy.dma_start(dLN[:, :], lnst[:]), "SP2", 16)
                        P.inc(sy.dma_start(dXT0[:, :], xt[1][:]), "SP2", 16)

                units = [("k", 0), ("k", 1), ("q", 0), ("q", 1), ("v", 0), ("v", 1)]
                for ui, (wname, cc) in enumerate(units):
                    guid = half * 6 + ui
                    wpar = guid % 2

                    @block.gpsimd
                    def _(g, wname=wname, cc=cc, wpar=wpar, guid=guid):
                        P.w(g, "gpsimd", ev_unit_mm.get(guid - 2))
                        bx["e"] = P.inc(
                            g.dma_start(Wbuf[wpar][:], wd_map[wname][cc, :, :]),
                            f"LW{wpar}", 16)
                    ev_W = bx["e"]

                    for s, tt in enumerate(tiles):
                        uid = guse[0]
                        guse[0] += 1
                        sl_ap = SLICES[uid % 6]

                        @block.tensor
                        def _(t, s=s, wpar=wpar, sl_ap=sl_ap, uid=uid, guid=guid,
                              last=(s == 7)):
                            P.w(t, "tensor", ev_W, ev_xnT_done)
                            if uid == 4:
                                P.w(t, "tensor", ev_sclrep)
                            for fev in ev_slice_free.pop(uid - 6, []):
                                P.w(t, "tensor", fev)
                            for dc in range(16):
                                ins = t.matmul(sl_ap,
                                         xnT[:, s * 2048 + dc * 128:s * 2048 + (dc + 1) * 128],
                                         Wbuf[wpar][:, dc * 512:(dc + 1) * 512],
                                         start=(dc == 0), stop=(dc == 15))
                            ev = P.inc(ins, "PE")
                            bx["e"] = ev
                            if last:
                                ev_unit_mm[guid] = ev
                        ev_mm = bx["e"]

                        frees = []
                        if wname == "v":

                            @block.vector
                            def _(v, tt=tt, cc=cc, sl_ap=sl_ap):
                                P.w(v, "vector", ev_mm)
                                bx["e"] = P.inc(v.tensor_copy(
                                    Vsb[:, tt * 1024 + cc * 512:tt * 1024 + (cc + 1) * 512],
                                    sl_ap), "DV")
                            frees.append(bx["e"])
                        else:

                            @block.scalar
                            def _(sc, sl_ap=sl_ap):
                                P.w(sc, "scalar", ev_mm, ev_evac_prev[0])
                                for lh in range(4):
                                    sc.activation(sqd[:, 0:128],
                                                  sl_ap[:, lh * 128:(lh + 1) * 128],
                                                  AF.Square, accum_out=sst[:, lh:lh + 1])
                                sc.activation(sst[:, 4:8], sst[:, 0:4], AF.Sqrt,
                                              scale=cst[:, 3:4], bias=cst[:, 4:5])
                                bx["e"] = P.inc(
                                    sc.activation(dmy[:], stab[:, 0:512], AF.Copy), "AC")
                            ev_ss = bx["e"]
                            frees.append(ev_ss)

                            @block.vector
                            def _(v, tt=tt, cc=cc, wname=wname, sl_ap=sl_ap):
                                P.w(v, "vector", ev_ss)
                                v.reciprocal(sst[:, 8:12], sst[:, 4:8])
                                v.tensor_copy(dmy[:], stab[:, 0:512])
                                if wname == "q":
                                    P.w(v, "vector", ev_lps)
                                    v.tensor_tensor(sst[:, 12:16], sst[:, 8:12],
                                                    lps[:, tt * 8 + cc * 4:tt * 8 + cc * 4 + 4],
                                                    op=OP.mult)
                                    v.tensor_copy(dmy[:], stab[:, 0:512])
                                    scol = 12
                                else:
                                    scol = 8
                                P.w(v, "vector", ev_rot_prev[0])
                                for lh in range(4):
                                    v.tensor_scalar_mul(stg[:, lh * 128:(lh + 1) * 128],
                                                        sl_ap[:, lh * 128:(lh + 1) * 128],
                                                        sst[:, scol + lh:scol + lh + 1])
                                bx["e"] = P.inc(v.tensor_copy(dmy[:], stab[:, 0:512]), "DV")
                            ev_evac = bx["e"]
                            ev_evac_prev[0] = ev_evac
                            frees.append(ev_evac)

                            @block.gpsimd
                            def _(g, tt=tt):
                                P.w(g, "gpsimd", ev_evac, ev_rt_prev[0])
                                c3 = ctab[:, tt * 128:(tt + 1) * 128].unsqueeze(1) \
                                    .broadcast_to([128, 4, 128])
                                s3 = stab[:, tt * 128:(tt + 1) * 128].unsqueeze(1) \
                                    .broadcast_to([128, 4, 128])
                                q3 = stg[:, 0:512].rearrange("p (h d) -> p h d", h=4)
                                sw = bass.AP(stg, 64, [[3 * 512, 128], [128, 4], [-64, 2], [1, 64]])
                                rotA = stg[:, 512:1024].rearrange("p (h d) -> p h d", h=4)
                                rotT = stg[:, 1024:1536].rearrange("p (h d) -> p h d", h=4)
                                g.tensor_tensor(rotA, q3, c3, op=OP.mult)
                                g.tensor_tensor(rotT, sw, s3, op=OP.mult)
                                ev = P.inc(g.tensor_tensor(
                                    stg[:, 512:1024], stg[:, 512:1024],
                                    stg[:, 1024:1536], op=OP.add), "GP")
                                bx["e"] = ev
                                ev_rot_prev[0] = ev
                            ev_rot = bx["e"]

                            hsel = uid % 2

                            @block.tensor
                            def _(t, hsel=hsel):
                                P.w(t, "tensor", ev_rot, ev_ptA_copy[hsel])
                                for lh in range(4):
                                    ins = t.transpose(
                                        ptA[:, hsel * 1024 + lh * 128:hsel * 1024 + (lh + 1) * 128],
                                        stg[:, 512 + lh * 128:512 + (lh + 1) * 128],
                                        identb[:])
                                ev = P.inc(ins, "PE")
                                bx["e"] = ev
                                ev_rt_prev[0] = ev
                            ev_rt = bx["e"]

                            @block.vector
                            def _(v, tt=tt, cc=cc, wname=wname, hsel=hsel):
                                P.w(v, "vector", ev_rt)
                                dstt = KT if wname == "k" else QT
                                dst = bass.AP(dstt, cc * 4 * 2048 + tt * 128,
                                              [[16384, 128], [2048, 4], [1, 128]])
                                src = ptA[:, hsel * 1024:hsel * 1024 + 512].rearrange(
                                    "p (h d) -> p h d", h=4)
                                ev = P.inc(v.tensor_copy(dst, src), "DV")
                                bx["e"] = ev
                                ev_ptA_copy[hsel] = ev
                            frees.append(bx["e"])
                        ev_slice_free[uid] = frees

            ev_A_done_dv = ("DV", P.cnt["DV"])
            ev_A_done_ac = ("AC", P.cnt["AC"])

            if dbg:
                @block.sync
                def _(sy):
                    P.w(sy, "sync", ev_A_done_dv, ev_A_done_ac)
                    P.inc(sy.dma_start(dKT[:, :], KT[:]), "SP2", 16)
                    P.inc(sy.dma_start(dSCL[:, :], sclrep[:]), "SP2", 16)
                    P.inc(sy.dma_start(dSST[:, :], sst[:]), "SP2", 16)
                    P.inc(sy.dma_start(dQT[:, :], QT[:]), "SP2", 16)
                    P.inc(sy.dma_start(dV[:, :], Vsb[:]), "SP2", 16)
                    P.inc(sy.dma_start(dXNT[:, :], xnT[:]), "SP2", 16)

        with ExitStack() as esB:
            EB = esB.enter_context
            E1 = EB(nc.sbuf_tensor("E1_s", [128, 2048], F32))
            E2 = EB(nc.sbuf_tensor("E2_s", [128, 2048], F32))
            Ph = EB(nc.sbuf_tensor("Ph_s", [128, 2048], BF16))
            PhT = EB(nc.sbuf_tensor("PhT_s", [128, 2048], BF16))
            ysb = [EB(nc.sbuf_tensor(f"ysb{i}_s", [128, 1024], BF16)) for i in range(2)]

            @block.vector
            def _(v):
                P.w(v, "vector", ev_tabs)
                v.tensor_tensor(lsc[:, 0:64], lam[:, 0:64], lam[:, 64:128], op=OP.mult)
                v.tensor_copy(dmy[:], stab[:, 0:512])
                v.reduce_sum(lsc[:, 64:65], lsc[:, 0:64], axis=AX.X)
                v.tensor_tensor(lsc[:, 0:64], lam[:, 128:192], lam[:, 192:256], op=OP.mult)
                v.tensor_copy(dmy[:], stab[:, 0:512])
                v.reduce_sum(lsc[:, 65:66], lsc[:, 0:64], axis=AX.X)
                bx["e"] = P.inc(v.tensor_copy(dmy[:], stab[:, 0:512]), "DV")
            ev_lam = bx["e"]

            @block.scalar
            def _(sc):
                P.w(sc, "scalar", ev_lam)
                sc.activation(lsc[:, 66:68], lsc[:, 64:66], AF.Exp)
                bx["e"] = P.inc(sc.activation(dmy[:], stab[:, 0:512], AF.Copy), "AC")
            ev_lexp = bx["e"]

            @block.vector
            def _(v):
                P.w(v, "vector", ev_lexp)
                v.tensor_tensor(lsc[:, 68:69], lsc[:, 67:68], lsc[:, 66:67], op=OP.subtract)
                v.tensor_copy(dmy[:], stab[:, 0:512])
                v.tensor_tensor(lsc[:, 69:70], lsc[:, 68:69], cst[0:1, 5:6], op=OP.add)
                bx["e"] = P.inc(v.tensor_copy(dmy[:], stab[:, 0:512]), "DV")
            ev_negl = bx["e"]

            @block.tensor
            def _(t):
                P.w(t, "tensor", ev_negl, ev_A_done_dv)
                ins = t.matmul(psS2[:, 0:1], onesf[:, :], lsc[:, 69:70], start=True, stop=True)
                bx["e"] = P.inc(ins, "PE")
            ev_neglmm = bx["e"]

            @block.vector
            def _(v):
                P.w(v, "vector", ev_neglmm)
                v.tensor_copy(neglb[:], psS2[:, 0:1])
                bx["e"] = P.inc(v.tensor_copy(dmy[:], stab[:, 0:512]), "DV")
            ev_neglb = bx["e"]

            ev_exp1 = {}
            ev_exp2a = {}
            ev_exp2b = {}
            ev_yevac = {}
            ev_ypark = {}
            ev_phtcopy = {}
            ev_ph = {}
            ev_pv = {}
            ev_trlast = {}
            phsel = [0]

            def emit_head(p):
                r, lv = p // 4, p % 4
                L = (r + 1) * 128
                L2a = min(L, 512)
                L2b = L - L2a
                b0 = 2560 - L
                lh1, lh2 = lv, 4 + lv
                zb = 12 * (p % 2)

                @block.tensor
                def _(t, r=r, L=L, lh1=lh1, p=p):
                    P.w(t, "tensor", ev_exp1.get(p - 1), ev_exp2b.get(p - 1),
                        ev_A_done_dv, ev_A_done_ac)
                    if dbg:
                        P.w(t, "tensor", ("SP2", P.cnt["SP2"]))
                    for (c0, w_) in chunks_aligned(0, L):
                        ins = t.matmul(psS[:, c0:c0 + w_],
                                 QT[:, lh1 * 2048 + r * 128:lh1 * 2048 + (r + 1) * 128],
                                 KT[:, lh1 * 2048 + c0:lh1 * 2048 + c0 + w_],
                                 start=True, stop=True)
                    bx["e"] = P.inc(ins, "PE")
                ev_s1 = bx["e"]

                @block.vector
                def _(v, L=L, zb=zb):
                    P.w(v, "vector", ev_s1)
                    v.tensor_tensor(psS[:, L - 128:L], psS[:, L - 128:L], maskd[:],
                                    op=OP.add)
                    v.tensor_reduce(zst[:, zb:zb + 1], psS[:, 0:L],
                                    axis=AX.X, op=OP.max, negate=True)
                    bx["e"] = P.inc(v.tensor_copy(dmy[:], stab[:, 0:512]), "DV")
                ev_m1 = bx["e"]

                @block.scalar
                def _(sc, L=L, p=p, zb=zb):
                    P.w(sc, "scalar", ev_m1, ev_ph.get(p - 1))
                    ev = P.inc(sc.activation(E1[:, 0:L], psS[:, 0:L], AF.Exp,
                                             bias=zst[:, zb:zb + 1],
                                             accum_out=zst[:, zb + 2:zb + 3]), "AC")
                    ev_exp1[p] = ev

                @block.tensor
                def _(t, r=r, L=L, L2a=L2a, L2b=L2b, b0=b0, lh2=lh2, p=p):
                    P.w(t, "tensor", ev_exp2a.get(p - 1), ev_neglb, ev_A_done_ac)
                    for (c0, w_) in chunks_aligned(0, L2a):
                        ins = t.matmul(psS2[:, c0:c0 + w_],
                                 QT[:, lh2 * 2048 + r * 128:lh2 * 2048 + (r + 1) * 128],
                                 KT[:, lh2 * 2048 + c0:lh2 * 2048 + c0 + w_],
                                 start=True, stop=True)
                    ev = P.inc(ins, "PE")
                    bx["ea"] = ev
                    if L2b:
                        if L >= 1152:
                            P.w(t, "tensor", ev_exp1[p])
                        P.w(t, "tensor", ev_exp2b.get(p - 1))
                        for (c0, w_) in chunks_aligned(b0, 2048):
                            kk = c0 - b0 + 512
                            ins = t.matmul(psS[:, c0:c0 + w_],
                                     QT[:, lh2 * 2048 + r * 128:lh2 * 2048 + (r + 1) * 128],
                                     KT[:, lh2 * 2048 + kk:lh2 * 2048 + kk + w_],
                                     start=True, stop=True)
                        ev = P.inc(ins, "PE")
                    bx["eb"] = ev
                ev_s2a, ev_s2b = bx["ea"], bx["eb"]

                @block.vector
                def _(v, L=L, L2a=L2a, L2b=L2b, b0=b0, zb=zb):
                    P.w(v, "vector", ev_s2a, ev_s2b)
                    if L2b:
                        v.tensor_tensor(psS[:, 1920:2048], psS[:, 1920:2048],
                                        maskd[:], op=OP.add)
                        v.tensor_reduce(zst[:, zb + 8:zb + 9], psS2[:, 0:L2a], axis=AX.X,
                                        op=OP.max)
                        v.tensor_reduce(zst[:, zb + 9:zb + 10], psS[:, b0:2048], axis=AX.X,
                                        op=OP.max)
                        v.tensor_copy(dmy[:], stab[:, 0:512])
                        v.tensor_reduce(zst[:, zb + 1:zb + 2],
                                        zst[:, zb + 8:zb + 10],
                                        axis=AX.X, op=OP.max, negate=True)
                        bx["e"] = P.inc(v.tensor_copy(dmy[:], stab[:, 0:512]), "DV")
                    else:
                        v.tensor_tensor(psS2[:, L - 128:L], psS2[:, L - 128:L],
                                        maskd[:], op=OP.add)
                        v.tensor_reduce(zst[:, zb + 1:zb + 2], psS2[:, 0:L2a],
                                        axis=AX.X, op=OP.max, negate=True)
                        bx["e"] = P.inc(v.tensor_copy(dmy[:], stab[:, 0:512]), "DV")
                ev_m2 = bx["e"]

                @block.scalar
                def _(sc, L=L, L2a=L2a, L2b=L2b, b0=b0, p=p, zb=zb):
                    P.w(sc, "scalar", ev_m2, ev_ph.get(p - 1))
                    ev = P.inc(sc.activation(E2[:, 0:L2a], psS2[:, 0:L2a], AF.Exp,
                                             bias=zst[:, zb + 1:zb + 2],
                                             accum_out=zst[:, zb + 3:zb + 4]), "AC")
                    ev_exp2a[p] = ev
                    if L2b:
                        ev = P.inc(sc.activation(E2[:, 512:L], psS[:, b0:2048], AF.Exp,
                                                 bias=zst[:, zb + 1:zb + 2],
                                                 accum_out=zst[:, zb + 4:zb + 5]), "AC")
                    ev_exp2b[p] = ev
                    bx["e"] = ev
                ev_e2done = bx["e"]

                @block.vector
                def _(v, L2b=L2b, p=p, zb=zb):
                    P.w(v, "vector", ev_exp1[p], ev_e2done, ev_neglb)
                    if L2b:
                        v.tensor_tensor(zst[:, zb + 3:zb + 4], zst[:, zb + 3:zb + 4],
                                        zst[:, zb + 4:zb + 5], op=OP.add)
                    v.reciprocal(zst[:, zb + 7:zb + 8], zst[:, zb + 2:zb + 3])
                    v.tensor_tensor(zst[:, zb + 5:zb + 6], zst[:, zb + 2:zb + 3],
                                    neglb[:], op=OP.mult)
                    v.reciprocal(zst[:, zb + 6:zb + 7], zst[:, zb + 3:zb + 4])
                    v.tensor_copy(dmy[:], stab[:, 0:512])
                    v.tensor_tensor(zst[:, zb + 5:zb + 6], zst[:, zb + 5:zb + 6],
                                    zst[:, zb + 6:zb + 7], op=OP.mult)
                    bx["e"] = P.inc(v.tensor_copy(dmy[:], stab[:, 0:512]), "DV")
                ev_c2 = bx["e"]
                ev_c2d[p] = ev_c2

                @block.gpsimd
                def _(g, L=L, p=p, zb=zb):
                    P.w(g, "gpsimd", ev_c2, ev_trlast.get(p - 1))
                    g.tensor_tensor(Ph[:, 0:L], E2[:, 0:L],
                                    zst[:, zb + 5:zb + 6].broadcast_to([128, L]), op=OP.mult)
                    ev = P.inc(g.tensor_tensor(Ph[:, 0:L], Ph[:, 0:L], E1[:, 0:L],
                                               op=OP.add), "GP")
                    ev_ph[p] = ev

            def emit_tail(p):
                r, lv = p // 4, p % 4
                L = (r + 1) * 128
                rpar = r % 2
                zb = 12 * (p % 2)
                ngr = (L + 511) // 512
                for gi2 in range(ngr):
                    w_ = min(512, L - gi2 * 512)
                    ntr = (w_ + 127) // 128
                    hsel = phsel[0] % 2
                    phsel[0] += 1

                    @block.tensor
                    def _(t, gi2=gi2, ntr=ntr, hsel=hsel, p=p, last=(gi2 == ngr - 1)):
                        P.w(t, "tensor", ev_ph[p], ev_ptA_copy[hsel])
                        for ti in range(ntr):
                            ins = t.transpose(
                                ptA[:, hsel * 1024 + ti * 128:hsel * 1024 + (ti + 1) * 128],
                                Ph[:, gi2 * 512 + ti * 128:gi2 * 512 + (ti + 1) * 128],
                                identb[:])
                        ev = P.inc(ins, "PE")
                        bx["e"] = ev
                        if last:
                            ev_trlast[p] = ev
                    ev_tr2 = bx["e"]

                    @block.vector
                    def _(v, gi2=gi2, w_=w_, hsel=hsel, p=p):
                        P.w(v, "vector", ev_tr2, ev_pv.get(p - 1))
                        ev = P.inc(v.tensor_copy(PhT[:, gi2 * 512:gi2 * 512 + w_],
                                                 ptA[:, hsel * 1024:hsel * 1024 + w_]),
                                   "DV")
                        bx["e"] = ev
                        ev_ptA_copy[hsel] = ev
                        ev_phtcopy[(p, gi2)] = ev

                @block.tensor
                def _(t, L=L, lv=lv, p=p):
                    P.w(t, "tensor", ev_yevac.get(p - 1), ev_neglb)
                    nkc = L // 128
                    for kc in range(nkc):
                        P.w(t, "tensor", ev_phtcopy[(p, kc // 4)])
                        ins = t.matmul(pvout[:, 0:256],
                                 PhT[:, kc * 128:(kc + 1) * 128],
                                 Vsb[:, kc * 1024 + lv * 256:kc * 1024 + (lv + 1) * 256],
                                 start=(kc == 0), stop=(kc == nkc - 1))
                    ev = P.inc(ins, "PE")
                    ev_pv[p] = ev
                    bx["e"] = ev
                ev_pvv = bx["e"]

                @block.scalar
                def _(sc, lv=lv, rpar=rpar, p=p, r=r, zb=zb):
                    P.w(sc, "scalar", ev_pvv, ev_c2d[p])
                    if lv == 0:
                        P.w(sc, "scalar", ev_ypark.get(r - 2))
                    ev = P.inc(sc.activation(ysb[rpar][:, lv * 256:(lv + 1) * 256],
                                             pvout[:, 0:256], AF.Copy,
                                             scale=zst[:, zb + 7:zb + 8]), "AC")
                    ev_yevac[p] = ev

                if lv == 3:
                    @block.sync
                    def _(sy, r=r, rpar=rpar):
                        P.w(sy, "sync", ev_yevac[r * 4 + 3])
                        ev = P.inc(sy.dma_start(ybuf_d[r * 128:(r + 1) * 128, :],
                                                ysb[rpar][:]), f"SY{r % 2}", 16)
                        ev_ypark[r] = ev

            ev_c2d = {}
            for p in range(64):
                if p >= 1:
                    emit_tail(p - 1)
                emit_head(p)
            emit_tail(63)

            if dbg:
                @block.sync
                def _(sy):
                    P.w(sy, "sync", ev_yevac[63])
                    P.inc(sy.dma_start(dE1[:, :], E1[:]), "SP2", 16)
                    P.inc(sy.dma_start(dE2[:, :], E2[:]), "SP2", 16)
                    P.inc(sy.dma_start(dPhT[:, :], PhT[:]), "SP2", 16)
                    P.inc(sy.dma_start(dZ[:, :], zst[:]), "SP2", 16)
                    P.inc(sy.dma_start(dNL[:, :], neglb[:]), "SP2", 16)
                    P.inc(sy.dma_start(dLS[:, :], lsc[:]), "SP2", 16)

            @block.gpsimd
            def _(g):
                P.w(g, "gpsimd", ("SY0", P.cnt["SY0"]), ("SY1", P.cnt["SY1"]))
                if dbg:
                    P.w(g, "gpsimd", ("SP2", P.cnt["SP2"]))
                bx["e"] = P.inc(g.collective_compute(
                    "AllGather", OP.bypass,
                    replica_groups=[[0, 1], [2, 3], [4, 5], [6, 7]],
                    ins=[ybuf_d.ap().opt()], outs=[ygath_d.ap().opt()]), "CC")
            ev_cc = bx["e"]
            ev_B_done_dv = ("DV", P.cnt["DV"])
            ev_B_done_ac = ("AC", P.cnt["AC"])
            ev_B_done_pe = ("PE", P.cnt["PE"])
        esAB.close()

        with ExitStack() as esC:
            EC = esC.enter_context
            xn2T = EC(nc.sbuf_tensor("xn2T_s", [128, 16384], BF16))
            fuse = EC(nc.sbuf_tensor("fuse_s", [128, 32768], BF16))
            Wf = [EC(nc.sbuf_tensor(f"Wf{i}_s", [128, 8192], BF16)) for i in range(2)]
            gatesb = EC(nc.sbuf_tensor("gatesb_s", [128, 2048], BF16))
            upsb = EC(nc.sbuf_tensor("upsb_s", [128, 512], BF16))
            outtm = EC(nc.sbuf_tensor("outtm_s", [128, 8192], F32))
            x2t = EC(nc.sbuf_tensor("x2t_s", [128, 2048], F32))
            yc = EC(nc.sbuf_tensor("yc_s", [128, 2048], BF16))
            xn2 = EC(nc.sbuf_tensor("xn2_s", [128, 2048], BF16))
            dwn = EC(nc.sbuf_tensor("dwn_s", [128, 512], F32))

            ev_park = {}
            ev_x2done = {}
            for tp in range(8):

                @block.gpsimd
                def _(g, tp=tp):
                    P.w(g, "gpsimd", ev_cc, ev_tabs, ev_x2done.get(tp - 1))
                    P.inc(g.indirect_dma_start(
                        yc[:, 0:1024], None, ygath_d[:, :],
                        IndirectOffsetOnAxis(idxg[:, tp:tp + 1], 0)), "LYC", 16)
                    bx["e"] = P.inc(g.indirect_dma_start(
                        yc[:, 1024:2048], None, ygath_d[:, :],
                        IndirectOffsetOnAxis(idxg[:, 8 + tp:9 + tp], 0)), "LYC", 16)
                ev_yc = bx["e"]

                @block.sync
                def _(sy, tp=tp):
                    P.w(sy, "sync", ev_x2done.get(tp - 1))
                    bx["e"] = P.inc(
                        sy.dma_start(x2t[:], xown_d[tp * 128:(tp + 1) * 128, :]),
                        "LXO", 16)
                ev_xo = bx["e"]

                @block.vector
                def _(v):
                    P.w(v, "vector", ev_yc, ev_xo)
                    bx["e"] = P.inc(v.tensor_tensor(x2t[:], x2t[:], yc[:], op=OP.add),
                                    "DV")
                ev_x2 = bx["e"]

                @block.sync
                def _(sy, tp=tp):
                    P.w(sy, "sync", ev_x2)
                    ev = P.inc(sy.dma_start(out_d[tp * 128:(tp + 1) * 128, :], x2t[:]),
                               "SP2", 16)
                    ev_park[tp] = ev

                @block.scalar
                def _(sc):
                    P.w(sc, "scalar", ev_x2)
                    bx["e"] = P.inc(sc.activation(sqd[:], x2t[:], AF.Square,
                                                  accum_out=lnst[:, 0:1]), "AC")
                ev_sq2 = bx["e"]

                @block.vector
                def _(v):
                    P.w(v, "vector", ev_sq2)
                    v.reduce_sum(lnst[:, 1:2], x2t[:], axis=AX.X)
                    v.tensor_copy(dmy[:], stab[:, 0:512])
                    v.tensor_scalar_mul(lnst[:, 2:3], lnst[:, 1:2], cst[:, 0:1])
                    bx["e"] = P.inc(v.tensor_copy(dmy[:], stab[:, 0:512]), "DV")
                ev_mean2 = bx["e"]

                @block.scalar
                def _(sc):
                    P.w(sc, "scalar", ev_mean2)
                    sc.activation(lnst[:, 3:4], lnst[:, 2:3], AF.Square)
                    bx["e"] = P.inc(sc.activation(dmy[:], stab[:, 0:512], AF.Copy), "AC")
                ev_msq2 = bx["e"]

                @block.vector
                def _(v):
                    P.w(v, "vector", ev_msq2)
                    v.tensor_scalar(lnst[:, 4:5], lnst[:, 3:4], cst[:, 1:2], cst[:, 2:3],
                                    op0=OP.mult, op1=OP.add)
                    bx["e"] = P.inc(v.tensor_copy(dmy[:], stab[:, 0:512]), "DV")
                ev_st2 = bx["e"]

                @block.scalar
                def _(sc):
                    P.w(sc, "scalar", ev_st2)
                    sc.activation(lnst[:, 5:6], lnst[:, 0:1], AF.Sqrt,
                                  scale=cst[:, 0:1], bias=lnst[:, 4:5])
                    bx["e"] = P.inc(sc.activation(dmy[:], stab[:, 0:512], AF.Copy), "AC")
                ev_sd2 = bx["e"]

                @block.vector
                def _(v, tp=tp):
                    P.w(v, "vector", ev_sd2)
                    v.reciprocal(lnst[:, 6:7], lnst[:, 5:6])
                    v.tensor_copy(dmy[:], stab[:, 0:512])
                    ev = P.inc(v.tensor_scalar(xn2[:], x2t[:], lnst[:, 2:3], lnst[:, 6:7],
                                               op0=OP.subtract, op1=OP.mult), "DV")
                    bx["e"] = ev
                    ev_x2done[tp] = ev
                ev_xn2 = bx["e"]

                for g2 in range(4):
                    hsel = g2 % 2

                    @block.tensor
                    def _(t, g2=g2, hsel=hsel):
                        P.w(t, "tensor", ev_xn2, ev_ptA_copy[hsel], ev_B_done_dv)
                        for d4 in range(4):
                            dc = g2 * 4 + d4
                            ins = t.transpose(
                                ptA[:, hsel * 1024 + d4 * 128:hsel * 1024 + (d4 + 1) * 128],
                                xn2[:, dc * 128:(dc + 1) * 128], identb[:])
                        bx["e"] = P.inc(ins, "PE")
                    ev_t2 = bx["e"]

                    @block.vector
                    def _(v, g2=g2, hsel=hsel, tp=tp):
                        P.w(v, "vector", ev_t2)
                        dst = bass.AP(xn2T, (g2 * 4) * 1024 + tp * 128,
                                      [[16384, 128], [1024, 4], [1, 128]])
                        src = ptA[:, hsel * 1024:hsel * 1024 + 512].rearrange(
                            "p (h d) -> p h d", h=4)
                        ev = P.inc(v.tensor_copy(dst, src), "DV")
                        bx["e"] = ev
                        ev_ptA_copy[hsel] = ev
            ev_xn2T_done = ("DV", P.cnt["DV"])

            GP_SL = [psS2[:, 0:512], psS[:, 0:512]]
            UP_SL = [psS[:, 512:1024], pvout[:, 0:512]]
            ev_silu = {}
            ev_upev = {}
            ev_fuse_ev = {}
            ev_dts = {}
            ev_dcopy = {}
            ev_acc = {}
            ev_gmm = {}
            ev_umm = {}
            ev_dmm = {}
            ev_dtr_prev = [None]

            for g in range(2):
                for f_i in range(16):

                    @block.gpsimd
                    def _(gp, g=g, f_i=f_i):
                        P.w(gp, "gpsimd", ev_gmm.get((g, f_i - 1)),
                            ev_dmm.get((g - 1, 15)))
                        bx["e0"] = P.inc(gp.dma_start(Wf[0][:], wg_d[f_i, :, :]), "LW0", 16)
                        P.w(gp, "gpsimd", ev_umm.get((g, f_i - 1)))
                        bx["e"] = P.inc(gp.dma_start(Wf[1][:], wu_d[f_i, :, :]), "LW1", 16)
                    ev_Wg0 = bx["e0"]
                    ev_Wgu = bx["e"]

                    for f128 in range(4):
                        ic = f_i * 4 + f128
                        gsl = GP_SL[ic % 2]
                        usl = UP_SL[ic % 2]

                        @block.tensor
                        def _(t, g=g, f_i=f_i, f128=f128, ic=ic, gsl=gsl, usl=usl):
                            P.w(t, "tensor", ev_Wg0, ev_Wgu, ev_xn2T_done,
                                ev_B_done_ac)
                            P.w(t, "tensor", ev_silu.get((g, ic - 2)))
                            if g > 0 and ic < 2:
                                P.w(t, "tensor", ev_dts.get((g - 1, 12 + ic)))
                            for kc in range(16):
                                ins = t.matmul(gsl,
                                         Wf[0][:, kc * 512 + f128 * 128:kc * 512 + (f128 + 1) * 128],
                                         xn2T[:, kc * 1024 + g * 512:kc * 1024 + (g + 1) * 512],
                                         start=(kc == 0), stop=(kc == 15))
                            ev = P.inc(ins, "PE")
                            ev_gmm[(g, f_i)] = ev
                            bx["eg"] = ev
                            P.w(t, "tensor", ev_upev.get((g, ic - 2)))
                            if g > 0 and ic < 2:
                                P.w(t, "tensor", ev_dts.get((g - 1, 13 + ic)))
                            for kc in range(16):
                                ins = t.matmul(usl,
                                         Wf[1][:, kc * 512 + f128 * 128:kc * 512 + (f128 + 1) * 128],
                                         xn2T[:, kc * 1024 + g * 512:kc * 1024 + (g + 1) * 512],
                                         start=(kc == 0), stop=(kc == 15))
                            ev = P.inc(ins, "PE")
                            ev_umm[(g, f_i)] = ev
                            bx["eu"] = ev
                        evg, evu = bx["eg"], bx["eu"]

                        @block.scalar
                        def _(sc, g=g, ic=ic, f128=f128, gsl=gsl):
                            P.w(sc, "scalar", evg, ev_fuse_ev.get((g, ic - 4)))
                            ev = P.inc(sc.activation(gatesb[:, f128 * 512:(f128 + 1) * 512],
                                                     gsl, AF.Silu,
                                                     scale=sgt[:, ic:ic + 1]), "AC")
                            ev_silu[(g, ic)] = ev
                            bx["e"] = ev
                        evsi = bx["e"]

                        @block.vector
                        def _(v, g=g, ic=ic, usl=usl):
                            P.w(v, "vector", evu, ev_fuse_ev.get((g, ic - 1)))
                            ev = P.inc(v.tensor_scalar_mul(upsb[:], usl,
                                                           sut[:, ic:ic + 1]), "DV")
                            ev_upev[(g, ic)] = ev
                            bx["e"] = ev
                        evup = bx["e"]

                        @block.gpsimd
                        def _(gp, g=g, ic=ic, f128=f128):
                            P.w(gp, "gpsimd", evsi, evup)
                            if ic == 0:
                                P.w(gp, "gpsimd", ev_dmm.get((g - 1, 15)))
                            ev = P.inc(gp.tensor_tensor(
                                fuse[:, ic * 512:(ic + 1) * 512],
                                gatesb[:, f128 * 512:(f128 + 1) * 512],
                                upsb[:], op=OP.mult), "GP")
                            ev_fuse_ev[(g, ic)] = ev

                ev_fuse_all = ("GP", P.cnt["GP"])
                for fc in range(16):

                    @block.gpsimd
                    def _(gp, g=g, fc=fc):
                        P.w(gp, "gpsimd", ev_dmm.get((g, fc - 2)))
                        if fc < 2:
                            P.w(gp, "gpsimd", ev_umm[(g, 15)])
                        bx["e"] = P.inc(gp.dma_start(Wf[fc % 2][:], wd_d[fc, :, :]),
                                        f"LW{fc % 2}", 16)
                    ev_Wd = bx["e"]

                    @block.tensor
                    def _(t, g=g, fc=fc):
                        P.w(t, "tensor", ev_Wd, ev_fuse_all)
                        P.w(t, "tensor", ev_dts.get((g, fc - 4)))
                        dsl = psS[:, (fc % 4) * 512:((fc % 4) + 1) * 512]
                        for ic in range(64):
                            ins = t.matmul(dsl,
                                     Wf[fc % 2][:, ic * 128:(ic + 1) * 128],
                                     fuse[:, ic * 512:(ic + 1) * 512],
                                     start=(ic == 0), stop=(ic == 63))
                        ev = P.inc(ins, "PE")
                        ev_dmm[(g, fc)] = ev
                        bx["e"] = ev
                    ev_dm = bx["e"]

                    @block.vector
                    def _(v, g=g, fc=fc):
                        P.w(v, "vector", ev_dm, ev_dtr_prev[0])
                        dsl = psS[:, (fc % 4) * 512:((fc % 4) + 1) * 512]
                        ev = P.inc(v.tensor_scalar_mul(dwn[:], dsl, sdt[:, fc:fc + 1]),
                                   "DV")
                        ev_dts[(g, fc)] = ev
                        bx["e"] = ev
                    ev_dt = bx["e"]

                    @block.tensor
                    def _(t, g=g, fc=fc):
                        P.w(t, "tensor", ev_dt, ev_dcopy.get((g, fc - 1)))
                        for tl in range(4):
                            ins = t.transpose(pvout[:, tl * 128:(tl + 1) * 128],
                                        dwn[:, tl * 128:(tl + 1) * 128], identf[:])
                        ev = P.inc(ins, "PE")
                        ev_dtr_prev[0] = ev
                        bx["e"] = ev
                    ev_dtr = bx["e"]

                    @block.vector
                    def _(v, g=g, fc=fc):
                        P.w(v, "vector", ev_dtr)
                        if fc == 0:
                            P.w(v, "vector", ev_acc.get(g - 1))
                        dst = bass.AP(outtm, fc * 128, [[8192, 128], [2048, 4], [1, 128]])
                        src = pvout[:, 0:512].rearrange("p (h d) -> p h d", h=4)
                        ev = P.inc(v.tensor_copy(dst, src), "DV")
                        ev_dcopy[(g, fc)] = ev

                for tl in range(4):
                    tp = g * 4 + tl

                    @block.gpsimd
                    def _(gp, g=g, tl=tl, tp=tp):
                        P.w(gp, "gpsimd", ev_dcopy[(g, 15)])
                        P.w(gp, "gpsimd", ("SP2", P.cnt["SP2"]))
                        ev = P.inc(gp.dma_start(out_d[tp * 128:(tp + 1) * 128, :],
                                                outtm[:, tl * 2048:(tl + 1) * 2048],
                                                accum_op=OP.add), "SG", 16)
                        bx["e"] = ev
                ev_acc[g] = ("SG", P.cnt["SG"])

            @block.gpsimd
            def _(gp):
                gp.wait_ge(sems["SG"], P.cnt["SG"])
                gp.wait_ge(sems["SP2"], P.cnt["SP2"])
                gp.wait_ge(sems["SY0"], P.cnt["SY0"])
                gp.wait_ge(sems["SY1"], P.cnt["SY1"])

            @block.sync
            def _(sy):
                sy.wait_ge(sems["SP2"], P.cnt["SP2"])

    return nc


_NC = None


def _get_nc():
    global _NC
    if _NC is None:
        _NC = build_program()
    return _NC


def _host_tables():
    i = np.arange(64, dtype=np.float64)
    inv_freq = 1.0 / (10000.0 ** (2 * i / HD))
    t = np.arange(T, dtype=np.float64)
    fr = t[:, None] * inv_freq[None, :]
    cos = np.cos(fr).astype(np.float32)
    sin = np.sin(fr).astype(np.float32)
    C = np.concatenate([cos, cos], axis=1)
    S = np.concatenate([sin, -sin], axis=1)
    ctab = np.ascontiguousarray(
        C.reshape(16, 128, 128).transpose(1, 0, 2).reshape(128, 2048)).astype(bf16)
    stab = np.ascontiguousarray(
        S.reshape(16, 128, 128).transpose(1, 0, 2).reshape(128, 2048)).astype(bf16)
    logpos = np.log(np.arange(1, T + 1, dtype=np.float32))
    logpt = np.ascontiguousarray(logpos.reshape(16, 128).T).astype(np.float32)
    p = np.arange(128)
    ii = np.arange(128)
    maskd = np.where(ii[None, :] <= p[:, None], 0.0, NEG).astype(np.float32).astype(bf16)
    identb = np.eye(128, dtype=np.float32).astype(bf16)
    identf = np.eye(128, dtype=np.float32)
    onesf = np.ones((1, 128), np.float32)
    return ctab, stab, logpt, maskd, identb, identf, onesf


def make_inmaps(L0_attn_x, L0_attn_Wq, L0_attn_Wk, L0_attn_Wv,
                L0_attn_lambda_q1, L0_attn_lambda_k1, L0_attn_lambda_q2, L0_attn_lambda_k2,
                L0_attn_softmax_scaler, L0_ffn_w_gate, L0_ffn_w_up, L0_ffn_w_down,
                L0_ffn_s_gate, L0_ffn_s_up, L0_ffn_s_down):
    x = np.asarray(L0_attn_x, dtype=np.float32)
    Wq = np.asarray(L0_attn_Wq, dtype=np.float32)
    Wk = np.asarray(L0_attn_Wk, dtype=np.float32)
    Wv = np.asarray(L0_attn_Wv, dtype=np.float32)
    Wg = np.asarray(L0_ffn_w_gate)
    Wu = np.asarray(L0_ffn_w_up)
    Wd = np.asarray(L0_ffn_w_down)
    sg = np.asarray(L0_ffn_s_gate, dtype=np.float32)
    su = np.asarray(L0_ffn_s_up, dtype=np.float32)
    sd = np.asarray(L0_ffn_s_down, dtype=np.float32)
    scaler = np.asarray(L0_attn_softmax_scaler, dtype=np.float32).reshape(16)
    lam = np.concatenate([np.asarray(v, np.float32).reshape(64) for v in
                          (L0_attn_lambda_q1, L0_attn_lambda_k1,
                           L0_attn_lambda_q2, L0_attn_lambda_k2)]).reshape(1, 256)

    ctab, stab, logpt, maskd, identb, identf, onesf = _host_tables()
    cstv = np.broadcast_to(np.array([1.0 / D, -1.0, EPS_LN, 1.0 / HD, EPS_RMS,
                                     -LAMBDA_INIT, 1.0 / math.sqrt(HD), -0.5],
                                    np.float32), (128, 8)).copy()
    wg_r = np.ascontiguousarray(
        Wg.reshape(16, 128, 16, 512).transpose(2, 1, 0, 3).reshape(16, 128, 8192))
    wu_r = np.ascontiguousarray(
        Wu.reshape(16, 128, 16, 512).transpose(2, 1, 0, 3).reshape(16, 128, 8192))
    wd_r = np.ascontiguousarray(
        Wd.reshape(64, 128, 16, 128).transpose(2, 1, 0, 3).reshape(16, 128, 8192))
    sgt = np.ascontiguousarray(sg.reshape(64, 128).T)
    sut = np.ascontiguousarray(su.reshape(64, 128).T)
    sdt = np.ascontiguousarray(sd.reshape(16, 128).T)

    in_maps = []
    for c in range(8):
        b, j = c // 2, c % 2
        qheads = [4 * j + i for i in range(4)] + [8 + 4 * j + i for i in range(4)]
        qcols = np.concatenate([np.arange(h * 128, (h + 1) * 128) for h in qheads])
        vcols = np.arange(4 * j * 256, 4 * j * 256 + 1024)

        def tile_w(W, cols):
            Ws = np.ascontiguousarray(W[:, cols])
            return np.ascontiguousarray(
                Ws.reshape(16, 128, 2, 512).transpose(2, 1, 0, 3).reshape(2, 128, 8192))

        pp = np.arange(128, dtype=np.int64)
        idxg = np.zeros((128, 16), np.int32)
        for tp in range(8):
            idxg[:, tp] = j * 1024 + tp * 128 + pp
            idxg[:, 8 + tp] = 2048 + j * 1024 + tp * 128 + pp

        in_maps.append({
            "x": np.ascontiguousarray(x[b]),
            "xown": np.ascontiguousarray(x[b, j * 1024:(j + 1) * 1024]),
            "wq": tile_w(Wq, qcols),
            "wk": tile_w(Wk, qcols),
            "wv": tile_w(Wv, vcols),
            "wg": wg_r, "wu": wu_r, "wd": wd_r,
            "sgt": sgt, "sut": sut, "sdt": sdt,
            "lam": lam,
            "scl": np.ascontiguousarray(scaler[qheads]).reshape(1, 8),
            "ctab": ctab, "stab": stab, "logpt": logpt, "maskd": maskd,
            "identb": identb, "identf": identf, "onesf": onesf,
            "idxg": idxg, "cst": cstv,
        })

    return in_maps


def kernel(**inputs):
    in_maps = make_inmaps(**inputs)
    nc = _get_nc()
    res = run_bass_kernel_spmd(nc, in_maps, core_ids=list(range(8)))
    out = np.empty((4, T, D), np.float32)
    for c in range(8):
        b, j = c // 2, c % 2
        out[b, j * 1024:(j + 1) * 1024, :] = np.asarray(res.results[c]["out"])
    return out


# version 03
# speedup vs baseline: 1.2549x; 1.2038x over previous
import math
from contextlib import ExitStack

import numpy as np
import ml_dtypes
import concourse.bass as bass
import concourse.mybir as mybir
from concourse.bass_core_rust import IndirectOffsetOnAxis
from concourse.bass_utils import run_bass_kernel_spmd

F32 = mybir.dt.float32
BF16 = mybir.dt.bfloat16
I8 = mybir.dt.int8
I32 = mybir.dt.int32
AF = mybir.ActivationFunctionType
OP = mybir.AluOpType
AX = mybir.AxisListType
bf16 = ml_dtypes.bfloat16

T = 2048
D = 2048
HD = 128
INTER = 8192
EPS_RMS = float(np.finfo(np.float32).eps)
EPS_LN = 1e-5
LAMBDA_INIT = 0.2
NEG = -1e30


class Prog:

    def __init__(self, nc, sems, serial=False):
        self.nc = nc
        self.sem = sems
        self.serial = serial
        self.cnt = {k: 0 for k in sems}
        self.lastw = {e: {k: 0 for k in sems} for e in
                      ("tensor", "vector", "scalar", "gpsimd", "sync")}
        self.own = {"tensor": "PE", "vector": "DV", "scalar": "AC", "gpsimd": "GP"}

    def inc(self, instr, s, n=1):
        instr.then_inc(self.sem[s], n)
        self.cnt[s] += n
        return (s, self.cnt[s])

    def w(self, eng, engname, *evs):
        if self.serial:
            evs = tuple(evs) + tuple((s, self.cnt[s]) for s in self.cnt
                                     if s not in ("CC",))
        for ev in evs:
            if ev is None:
                continue
            s, v = ev
            if v <= 0:
                continue
            if self.own.get(engname) == s:
                continue
            if self.lastw[engname][s] >= v:
                continue
            eng.wait_ge(self.sem[s], v)
            self.lastw[engname][s] = v


def chunks_aligned(c0, c1):
    out = []
    c = c0
    while c < c1:
        nxt = min(c1, (c // 512 + 1) * 512)
        out.append((c, nxt - c))
        c = nxt
    return out


DBG = False


def build_program(dbg=False, serial=False):
    nc = bass.Bass("TRN2", target_bir_lowering=False, detect_race_conditions=False)

    x_d = nc.dram_tensor("x", [T, D], F32, kind="ExternalInput")
    xown_d = nc.dram_tensor("xown", [1024, D], F32, kind="ExternalInput")
    wq_d = nc.dram_tensor("wq", [2, 128, 8192], F32, kind="ExternalInput")
    wk_d = nc.dram_tensor("wk", [2, 128, 8192], F32, kind="ExternalInput")
    wv_d = nc.dram_tensor("wv", [2, 128, 8192], F32, kind="ExternalInput")
    wg_d = nc.dram_tensor("wg", [16, 128, 8192], I8, kind="ExternalInput")
    wu_d = nc.dram_tensor("wu", [16, 128, 8192], I8, kind="ExternalInput")
    wd_d = nc.dram_tensor("wd", [16, 128, 8192], I8, kind="ExternalInput")
    sgt_d = nc.dram_tensor("sgt", [128, 64], F32, kind="ExternalInput")
    sut_d = nc.dram_tensor("sut", [128, 64], F32, kind="ExternalInput")
    sdt_d = nc.dram_tensor("sdt", [128, 16], F32, kind="ExternalInput")
    lam_d = nc.dram_tensor("lam", [1, 256], F32, kind="ExternalInput")
    scl_d = nc.dram_tensor("scl", [1, 8], F32, kind="ExternalInput")
    ctab_d = nc.dram_tensor("ctab", [128, 2048], BF16, kind="ExternalInput")
    stab_d = nc.dram_tensor("stab", [128, 2048], BF16, kind="ExternalInput")
    logpt_d = nc.dram_tensor("logpt", [128, 16], F32, kind="ExternalInput")
    maskd_d = nc.dram_tensor("maskd", [128, 128], BF16, kind="ExternalInput")
    identb_d = nc.dram_tensor("identb", [128, 128], BF16, kind="ExternalInput")
    identf_d = nc.dram_tensor("identf", [128, 128], F32, kind="ExternalInput")
    onesf_d = nc.dram_tensor("onesf", [1, 128], F32, kind="ExternalInput")
    idxg_d = nc.dram_tensor("idxg", [128, 16], I32, kind="ExternalInput")
    cst_d = nc.dram_tensor("cst", [128, 8], F32, kind="ExternalInput")
    out_d = nc.dram_tensor("out", [1024, D], F32, kind="ExternalOutput")
    ybuf_d = nc.dram_tensor("ybuf", [T, 1024], BF16)
    ygath_d = nc.dram_tensor("ygath", [2 * T, 1024], BF16)
    if dbg:
        dKT = nc.dram_tensor("dKT", [128, 16384], BF16, kind="ExternalOutput")
        dQT = nc.dram_tensor("dQT", [128, 16384], BF16, kind="ExternalOutput")
        dV = nc.dram_tensor("dV", [128, 16384], BF16, kind="ExternalOutput")
        dXNT = nc.dram_tensor("dXNT", [128, 16384], BF16, kind="ExternalOutput")
        dXNT0 = nc.dram_tensor("dXNT0", [128, 16384], BF16, kind="ExternalOutput")
        dXT0 = nc.dram_tensor("dXT0", [128, 2048], F32, kind="ExternalOutput")
        dLN = nc.dram_tensor("dLN", [128, 8], F32, kind="ExternalOutput")
        dSCL = nc.dram_tensor("dSCL", [128, 8], F32, kind="ExternalOutput")
        dSST = nc.dram_tensor("dSST", [128, 16], F32, kind="ExternalOutput")
        dE1 = nc.dram_tensor("dE1", [128, 2048], F32, kind="ExternalOutput")
        dE2 = nc.dram_tensor("dE2", [128, 2048], F32, kind="ExternalOutput")
        dPhT = nc.dram_tensor("dPhT", [128, 2048], BF16, kind="ExternalOutput")
        dZ = nc.dram_tensor("dZ", [128, 36], F32, kind="ExternalOutput")
        dNL = nc.dram_tensor("dNL", [128, 1], F32, kind="ExternalOutput")
        dLS = nc.dram_tensor("dLS", [1, 80], F32, kind="ExternalOutput")

    bx = {}

    es = ExitStack()
    E = es.enter_context
    with es:
        sems = {}
        for s in ("LT", "LX0", "LX1", "LXO", "LW0", "LW1", "LYC", "SY0", "SY1",
                  "SP2", "SG", "CC", "PE", "DV", "AC", "GP"):
            sems[s] = E(nc.semaphore(s))
        block = E(nc.Block())
        P = Prog(nc, sems, serial=serial)

        psS = E(nc.psum_tensor("psS", [128, 2048], F32))
        psS2 = E(nc.psum_tensor("psS2", [128, 512], F32))
        pvout = E(nc.psum_tensor("pvout", [128, 512], F32))
        ptA = E(nc.psum_tensor("ptA", [128, 2048], BF16))

        ctab = E(nc.sbuf_tensor("ctab_s", [128, 2048], BF16))
        stab = E(nc.sbuf_tensor("stab_s", [128, 2048], BF16))
        logpt = E(nc.sbuf_tensor("logpt_s", [128, 16], F32))
        maskd = E(nc.sbuf_tensor("maskd_s", [128, 128], BF16))
        identb = E(nc.sbuf_tensor("identb_s", [128, 128], BF16))
        identf = E(nc.sbuf_tensor("identf_s", [128, 128], F32))
        onesf = E(nc.sbuf_tensor("onesf_s", [1, 128], F32))
        scl = E(nc.sbuf_tensor("scl_s", [1, 8], F32))
        sclrep = E(nc.sbuf_tensor("sclrep_s", [128, 8], F32))
        lps = E(nc.sbuf_tensor("lps_s", [128, 128], F32))
        lam = E(nc.sbuf_tensor("lam_s", [1, 256], F32))
        lsc = E(nc.sbuf_tensor("lsc_s", [1, 80], F32))
        neglb = E(nc.sbuf_tensor("neglb_s", [128, 1], F32))
        sgt = E(nc.sbuf_tensor("sgt_s", [128, 64], F32))
        sut = E(nc.sbuf_tensor("sut_s", [128, 64], F32))
        sdt = E(nc.sbuf_tensor("sdt_s", [128, 16], F32))
        idxg = E(nc.sbuf_tensor("idxg_s", [128, 16], I32))
        cst = E(nc.sbuf_tensor("cst_s", [128, 8], F32))
        sqd = E(nc.sbuf_tensor("sqd_s", [128, 2048], BF16))
        lnst = E(nc.sbuf_tensor("lnst_s", [128, 8], F32))
        zst = E(nc.sbuf_tensor("zst_s", [128, 36], F32))
        dmy = E(nc.sbuf_tensor("dmy_s", [128, 512], BF16))

        @block.sync
        def _(sy):
            for dst, src in ((ctab, ctab_d), (stab, stab_d), (logpt, logpt_d),
                             (maskd, maskd_d), (identb, identb_d), (identf, identf_d),
                             (onesf, onesf_d), (scl, scl_d), (lam, lam_d),
                             (sgt, sgt_d), (sut, sut_d), (sdt, sdt_d), (idxg, idxg_d),
                             (cst, cst_d)):
                P.inc(sy.dma_start(dst[:], src[:, :]), "LT", 16)
        ev_tabs = ("LT", P.cnt["LT"])

        @block.tensor
        def _(t):
            P.w(t, "tensor", ev_tabs)
            ins = t.matmul(psS2[:, 0:8], onesf[:, :], scl[:, :], start=True, stop=True)
            bx["e"] = P.inc(ins, "PE")
        ev_sclmm = bx["e"]

        @block.vector
        def _(v):
            P.w(v, "vector", ev_sclmm, ev_tabs)
            v.tensor_scalar_mul(sclrep[:], psS2[:, 0:8], cst[:, 6:7])
            bx["e"] = P.inc(v.tensor_copy(dmy[:], stab[:, 0:512]), "DV")
        ev_sclrep = bx["e"]

        @block.vector
        def _(v):
            lp3 = logpt[:, :].unsqueeze(2).broadcast_to([128, 16, 8])
            sc3 = sclrep[:, :].unsqueeze(1).broadcast_to([128, 16, 8])
            o3 = lps[:, :].rearrange("p (t h) -> p t h", t=16)
            bx["e"] = P.inc(v.tensor_tensor(o3, lp3, sc3, op=OP.mult), "DV")
        ev_lps = bx["e"]

        esAB = ExitStack()
        KT = esAB.enter_context(nc.sbuf_tensor("KT_s", [128, 16384], BF16))
        QT = esAB.enter_context(nc.sbuf_tensor("QT_s", [128, 16384], BF16))
        Vsb = esAB.enter_context(nc.sbuf_tensor("Vsb_s", [128, 16384], BF16))

        ev_ptA_copy = [None, None]

        with ExitStack() as esA:
            EA = esA.enter_context
            xnT = EA(nc.sbuf_tensor("xnT_s", [128, 16384], BF16))
            xt = [EA(nc.sbuf_tensor(f"xt{i}_s", [128, 2048], F32)) for i in range(2)]
            xnb = EA(nc.sbuf_tensor("xnb_s", [128, 2048], BF16))
            Wbuf = [EA(nc.sbuf_tensor(f"Wb{i}_s", [128, 8192], BF16)) for i in range(2)]
            stg = EA(nc.sbuf_tensor("stg_s", [128, 2 * 3 * 512], BF16))
            sst = EA(nc.sbuf_tensor("sst_s", [128, 16], F32))

            ev_xt_free = {}
            ev_xnb_free = {}
            ev_unit_mm = {}
            ev_slice_free = {}
            ev_rot_prev = [None]
            pend_rot = []
            rot_hsel = [0]
            kqc = [0]
            ev_rt_prev = [None]
            ev_evac_prev = [None]
            guse = [0]
            SLICES = [psS[:, 0:512], psS[:, 512:1024], psS[:, 1024:1536],
                      psS[:, 1536:2048], psS2[:, 0:512], pvout[:, 0:512]]
            wd_map = {"k": wk_d, "q": wq_d, "v": wv_d}

            for half in range(2):
                tiles = list(range(8 * half, 8 * half + 8))
                for s, tt in enumerate(tiles):
                    gi = half * 8 + s
                    par = gi % 2

                    @block.sync
                    def _(sy, tt=tt, par=par, gi=gi):
                        P.w(sy, "sync", ev_xt_free.get(gi - 2))
                        if dbg and gi == 8:
                            P.w(sy, "sync", ("SP2", P.cnt["SP2"]))
                        bx["e"] = P.inc(
                            sy.dma_start(xt[par][:], x_d[tt * 128:(tt + 1) * 128, :]),
                            f"LX{gi % 2}", 16)
                    ev_x = bx["e"]

                    @block.scalar
                    def _(sc, par=par):
                        P.w(sc, "scalar", ev_x)
                        bx["e"] = P.inc(sc.activation(sqd[:], xt[par][:], AF.Square,
                                                      accum_out=lnst[:, 0:1]), "AC")
                    ev_sq = bx["e"]

                    @block.vector
                    def _(v, par=par):
                        P.w(v, "vector", ev_x, ev_sq)
                        v.reduce_sum(lnst[:, 1:2], xt[par][:], axis=AX.X)
                        v.tensor_copy(dmy[:], stab[:, 0:512])
                        v.tensor_scalar_mul(lnst[:, 2:3], lnst[:, 1:2], cst[:, 0:1])
                        bx["e"] = P.inc(v.tensor_copy(dmy[:], stab[:, 0:512]), "DV")
                    ev_mean = bx["e"]

                    @block.scalar
                    def _(sc):
                        P.w(sc, "scalar", ev_mean)
                        sc.activation(lnst[:, 3:4], lnst[:, 2:3], AF.Square)
                        bx["e"] = P.inc(sc.activation(dmy[:, 0:256], stab[:, 0:256], AF.Copy), "AC")
                    ev_msq = bx["e"]

                    @block.vector
                    def _(v):
                        P.w(v, "vector", ev_msq)
                        v.tensor_scalar(lnst[:, 4:5], lnst[:, 3:4], cst[:, 1:2], cst[:, 2:3],
                                        op0=OP.mult, op1=OP.add)
                        bx["e"] = P.inc(v.tensor_copy(dmy[:], stab[:, 0:512]), "DV")
                    ev_stats = bx["e"]

                    @block.scalar
                    def _(sc):
                        P.w(sc, "scalar", ev_stats)
                        sc.activation(lnst[:, 5:6], lnst[:, 0:1], AF.Sqrt,
                                      scale=cst[:, 0:1], bias=lnst[:, 4:5])
                        bx["e"] = P.inc(sc.activation(dmy[:, 0:256], stab[:, 0:256], AF.Copy), "AC")
                    ev_std = bx["e"]

                    @block.vector
                    def _(v, par=par, gi=gi):
                        P.w(v, "vector", ev_std, ev_xnb_free.get(gi - 1))
                        v.reciprocal(lnst[:, 6:7], lnst[:, 5:6])
                        v.tensor_copy(dmy[:], stab[:, 0:512])
                        ev = P.inc(v.tensor_scalar(xnb[:], xt[par][:], lnst[:, 2:3],
                                                   lnst[:, 6:7], op0=OP.subtract,
                                                   op1=OP.mult), "DV")
                        bx["e"] = ev
                        ev_xt_free[gi] = ev
                    ev_xn = bx["e"]

                    for g in range(4):
                        hsel = g % 2

                        @block.tensor
                        def _(t, g=g, hsel=hsel, gi=gi):
                            P.w(t, "tensor", ev_xn, ev_ptA_copy[hsel])
                            for d4 in range(4):
                                dc = g * 4 + d4
                                ins = t.transpose(
                                    ptA[:, hsel * 1024 + d4 * 128:hsel * 1024 + (d4 + 1) * 128],
                                    xnb[:, dc * 128:(dc + 1) * 128], identb[:])
                            ev = P.inc(ins, "PE")
                            bx["e"] = ev
                            if g == 3:
                                ev_xnb_free[gi] = ev
                        ev_tr = bx["e"]

                        @block.vector
                        def _(v, g=g, hsel=hsel, s=s):
                            P.w(v, "vector", ev_tr)
                            ev = P.inc(v.tensor_copy(
                                xnT[:, s * 2048 + g * 512:s * 2048 + (g + 1) * 512],
                                ptA[:, hsel * 1024:hsel * 1024 + 512]), "DV")
                            bx["e"] = ev
                            ev_ptA_copy[hsel] = ev
                ev_xnT_done = ("DV", P.cnt["DV"])

                if dbg and half == 0:
                    @block.sync
                    def _(sy):
                        P.w(sy, "sync", ev_xnT_done)
                        P.inc(sy.dma_start(dXNT0[:, :], xnT[:]), "SP2", 16)
                        P.inc(sy.dma_start(dLN[:, :], lnst[:]), "SP2", 16)
                        P.inc(sy.dma_start(dXT0[:, :], xt[1][:]), "SP2", 16)

                units = [("k", 0), ("k", 1), ("q", 0), ("q", 1), ("v", 0), ("v", 1)]
                for ui, (wname, cc) in enumerate(units):
                    guid = half * 6 + ui
                    wpar = guid % 2

                    @block.gpsimd
                    def _(g, wname=wname, cc=cc, wpar=wpar, guid=guid):
                        P.w(g, "gpsimd", ev_unit_mm.get(guid - 2))
                        bx["e"] = P.inc(
                            g.dma_start(Wbuf[wpar][:], wd_map[wname][cc, :, :]),
                            f"LW{wpar}", 16)
                    ev_W = bx["e"]

                    for s, tt in enumerate(tiles):
                        uid = guse[0]
                        guse[0] += 1
                        sl_ap = SLICES[uid % 6]

                        @block.tensor
                        def _(t, s=s, wpar=wpar, sl_ap=sl_ap, uid=uid, guid=guid,
                              last=(s == 7)):
                            P.w(t, "tensor", ev_W, ev_xnT_done)
                            if uid == 4:
                                P.w(t, "tensor", ev_sclrep)
                            for fev in ev_slice_free.pop(uid - 6, []):
                                P.w(t, "tensor", fev)
                            for dc in range(16):
                                ins = t.matmul(sl_ap,
                                         xnT[:, s * 2048 + dc * 128:s * 2048 + (dc + 1) * 128],
                                         Wbuf[wpar][:, dc * 512:(dc + 1) * 512],
                                         start=(dc == 0), stop=(dc == 15))
                            ev = P.inc(ins, "PE")
                            bx["e"] = ev
                            if last:
                                ev_unit_mm[guid] = ev
                        ev_mm = bx["e"]

                        frees = []
                        if wname == "v":

                            @block.vector
                            def _(v, tt=tt, cc=cc, sl_ap=sl_ap):
                                P.w(v, "vector", ev_mm)
                                bx["e"] = P.inc(v.tensor_copy(
                                    Vsb[:, tt * 1024 + cc * 512:tt * 1024 + (cc + 1) * 512],
                                    sl_ap), "DV")
                            frees.append(bx["e"])
                        else:

                            @block.scalar
                            def _(sc, sl_ap=sl_ap):
                                P.w(sc, "scalar", ev_mm, ev_evac_prev[0])
                                for lh in range(4):
                                    sc.activation(sqd[:, 0:128],
                                                  sl_ap[:, lh * 128:(lh + 1) * 128],
                                                  AF.Square, accum_out=sst[:, lh:lh + 1])
                                sc.activation(sst[:, 4:8], sst[:, 0:4], AF.Sqrt,
                                              scale=cst[:, 3:4], bias=cst[:, 4:5])
                                bx["e"] = P.inc(
                                    sc.activation(dmy[:, 0:256], stab[:, 0:256], AF.Copy), "AC")
                            ev_ss = bx["e"]
                            frees.append(ev_ss)

                            spar = (kqc[0] % 2) * 1536
                            kqc[0] += 1

                            @block.vector
                            def _(v, tt=tt, cc=cc, wname=wname, sl_ap=sl_ap, spar=spar):
                                P.w(v, "vector", ev_ss)
                                v.reciprocal(sst[:, 8:12], sst[:, 4:8])
                                v.tensor_copy(dmy[:], stab[:, 0:512])
                                if wname == "q":
                                    P.w(v, "vector", ev_lps)
                                    v.tensor_tensor(sst[:, 12:16], sst[:, 8:12],
                                                    lps[:, tt * 8 + cc * 4:tt * 8 + cc * 4 + 4],
                                                    op=OP.mult)
                                    v.tensor_copy(dmy[:], stab[:, 0:512])
                                    scol = 12
                                else:
                                    scol = 8
                                P.w(v, "vector", ev_rot_prev[0])
                                for lh in range(4):
                                    v.tensor_scalar_mul(stg[:, spar + lh * 128:spar + (lh + 1) * 128],
                                                        sl_ap[:, lh * 128:(lh + 1) * 128],
                                                        sst[:, scol + lh:scol + lh + 1])
                                bx["e"] = P.inc(v.tensor_copy(dmy[:], stab[:, 0:512]), "DV")
                            ev_evac = bx["e"]
                            ev_evac_prev[0] = ev_evac
                            frees.append(ev_evac)

                            @block.gpsimd
                            def _(g, tt=tt, spar=spar):
                                P.w(g, "gpsimd", ev_evac, ev_rt_prev[0])
                                c3 = ctab[:, tt * 128:(tt + 1) * 128].unsqueeze(1) \
                                    .broadcast_to([128, 4, 128])
                                s3 = stab[:, tt * 128:(tt + 1) * 128].unsqueeze(1) \
                                    .broadcast_to([128, 4, 128])
                                q3 = stg[:, spar:spar + 512].rearrange("p (h d) -> p h d", h=4)
                                sw = bass.AP(stg, spar + 64, [[3072, 128], [128, 4], [-64, 2], [1, 64]])
                                rotA = stg[:, spar + 512:spar + 1024].rearrange("p (h d) -> p h d", h=4)
                                rotT = stg[:, spar + 1024:spar + 1536].rearrange("p (h d) -> p h d", h=4)
                                g.tensor_tensor(rotA, q3, c3, op=OP.mult)
                                g.tensor_tensor(rotT, sw, s3, op=OP.mult)
                                ev = P.inc(g.tensor_tensor(
                                    stg[:, spar + 512:spar + 1024], stg[:, spar + 512:spar + 1024],
                                    stg[:, spar + 1024:spar + 1536], op=OP.add), "GP")
                                bx["e"] = ev
                                ev_rot_prev[0] = ev
                            ev_rot = bx["e"]

                            def mk_rot_emitter(ev_rot=ev_rot, tt=tt, cc=cc, wname=wname,
                                               spar=spar):
                                def emit(hsel):
                                    @block.tensor
                                    def _(t):
                                        P.w(t, "tensor", ev_rot, ev_ptA_copy[hsel])
                                        for lh in range(4):
                                            ins = t.transpose(
                                                ptA[:, hsel * 1024 + lh * 128:hsel * 1024 + (lh + 1) * 128],
                                                stg[:, spar + 512 + lh * 128:spar + 512 + (lh + 1) * 128],
                                                identb[:])
                                        ev = P.inc(ins, "PE")
                                        bx["e"] = ev
                                        ev_rt_prev[0] = ev
                                    ev_rt = bx["e"]

                                    @block.vector
                                    def _(v):
                                        P.w(v, "vector", ev_rt)
                                        dstt = KT if wname == "k" else QT
                                        dst = bass.AP(dstt, cc * 4 * 2048 + tt * 128,
                                                      [[16384, 128], [2048, 4], [1, 128]])
                                        srcap = ptA[:, hsel * 1024:hsel * 1024 + 512].rearrange(
                                            "p (h d) -> p h d", h=4)
                                        ev = P.inc(v.tensor_copy(dst, srcap), "DV")
                                        bx["e"] = ev
                                        ev_ptA_copy[hsel] = ev
                                    return bx["e"]
                                return emit
                            pend_rot.append(mk_rot_emitter())
                        ev_slice_free[uid] = frees
                        if len(pend_rot) > 1:
                            em = pend_rot.pop(0)
                            ev_slice_free[uid].append(em(rot_hsel[0] % 2))
                            rot_hsel[0] += 1

            while pend_rot:
                em = pend_rot.pop(0)
                em(rot_hsel[0] % 2)
                rot_hsel[0] += 1

            ev_A_done_dv = ("DV", P.cnt["DV"])
            ev_A_done_ac = ("AC", P.cnt["AC"])

            if dbg:
                @block.sync
                def _(sy):
                    P.w(sy, "sync", ev_A_done_dv, ev_A_done_ac)
                    P.inc(sy.dma_start(dKT[:, :], KT[:]), "SP2", 16)
                    P.inc(sy.dma_start(dSCL[:, :], sclrep[:]), "SP2", 16)
                    P.inc(sy.dma_start(dSST[:, :], sst[:]), "SP2", 16)
                    P.inc(sy.dma_start(dQT[:, :], QT[:]), "SP2", 16)
                    P.inc(sy.dma_start(dV[:, :], Vsb[:]), "SP2", 16)
                    P.inc(sy.dma_start(dXNT[:, :], xnT[:]), "SP2", 16)

        with ExitStack() as esB:
            EB = esB.enter_context
            E1 = [EB(nc.sbuf_tensor(f"E1{i}_s", [128, 2048], F32)) for i in range(2)]
            E2 = [EB(nc.sbuf_tensor(f"E2{i}_s", [128, 2048], F32)) for i in range(2)]
            Ph = [EB(nc.sbuf_tensor(f"Ph{i}_s", [128, 2048], BF16)) for i in range(2)]
            PhT = [EB(nc.sbuf_tensor(f"PhT{i}_s", [128, 2048], BF16)) for i in range(2)]
            ysb = [EB(nc.sbuf_tensor(f"ysb{i}_s", [128, 1024], BF16)) for i in range(2)]

            @block.vector
            def _(v):
                P.w(v, "vector", ev_tabs)
                v.tensor_tensor(lsc[:, 0:64], lam[:, 0:64], lam[:, 64:128], op=OP.mult)
                v.tensor_copy(dmy[:], stab[:, 0:512])
                v.reduce_sum(lsc[:, 64:65], lsc[:, 0:64], axis=AX.X)
                v.tensor_tensor(lsc[:, 0:64], lam[:, 128:192], lam[:, 192:256], op=OP.mult)
                v.tensor_copy(dmy[:], stab[:, 0:512])
                v.reduce_sum(lsc[:, 65:66], lsc[:, 0:64], axis=AX.X)
                bx["e"] = P.inc(v.tensor_copy(dmy[:], stab[:, 0:512]), "DV")
            ev_lam = bx["e"]

            @block.scalar
            def _(sc):
                P.w(sc, "scalar", ev_lam)
                sc.activation(lsc[:, 66:68], lsc[:, 64:66], AF.Exp)
                bx["e"] = P.inc(sc.activation(dmy[:, 0:256], stab[:, 0:256], AF.Copy), "AC")
            ev_lexp = bx["e"]

            @block.vector
            def _(v):
                P.w(v, "vector", ev_lexp)
                v.tensor_tensor(lsc[:, 68:69], lsc[:, 67:68], lsc[:, 66:67], op=OP.subtract)
                v.tensor_copy(dmy[:], stab[:, 0:512])
                v.tensor_tensor(lsc[:, 69:70], lsc[:, 68:69], cst[0:1, 5:6], op=OP.add)
                bx["e"] = P.inc(v.tensor_copy(dmy[:], stab[:, 0:512]), "DV")
            ev_negl = bx["e"]

            @block.tensor
            def _(t):
                P.w(t, "tensor", ev_negl, ev_A_done_dv)
                ins = t.matmul(psS2[:, 0:1], onesf[:, :], lsc[:, 69:70], start=True, stop=True)
                bx["e"] = P.inc(ins, "PE")
            ev_neglmm = bx["e"]

            @block.vector
            def _(v):
                P.w(v, "vector", ev_neglmm)
                v.tensor_copy(neglb[:], psS2[:, 0:1])
                bx["e"] = P.inc(v.tensor_copy(dmy[:], stab[:, 0:512]), "DV")
            ev_neglb = bx["e"]

            ev_exp1 = {}
            ev_exp2a = {}
            ev_exp2b = {}
            ev_yevac = {}
            ev_ypark = {}
            ev_phtcopy = {}
            ev_ph = {}
            ev_pv = {}
            ev_trlast = {}
            phsel = [0]

            def emit_head(p):
                r, lv = p // 4, p % 4
                L = (r + 1) * 128
                L2a = min(L, 512)
                L2b = L - L2a
                b0 = 2560 - L
                lh1, lh2 = lv, 4 + lv
                zb = 12 * (p % 3)

                @block.tensor
                def _(t, r=r, L=L, lh1=lh1, p=p):
                    P.w(t, "tensor", ev_exp1.get(p - 1), ev_exp2b.get(p - 1),
                        ev_A_done_dv, ev_A_done_ac)
                    if dbg:
                        P.w(t, "tensor", ("SP2", P.cnt["SP2"]))
                    for (c0, w_) in chunks_aligned(0, L):
                        ins = t.matmul(psS[:, c0:c0 + w_],
                                 QT[:, lh1 * 2048 + r * 128:lh1 * 2048 + (r + 1) * 128],
                                 KT[:, lh1 * 2048 + c0:lh1 * 2048 + c0 + w_],
                                 start=True, stop=True)
                    bx["e"] = P.inc(ins, "PE")
                ev_s1 = bx["e"]

                @block.vector
                def _(v, L=L, zb=zb):
                    P.w(v, "vector", ev_s1)
                    v.tensor_tensor(psS[:, L - 128:L], psS[:, L - 128:L], maskd[:],
                                    op=OP.add)
                    v.tensor_reduce(zst[:, zb:zb + 1], psS[:, 0:L],
                                    axis=AX.X, op=OP.max, negate=True)
                    bx["e"] = P.inc(v.tensor_copy(dmy[:], stab[:, 0:512]), "DV")
                ev_m1 = bx["e"]

                @block.scalar
                def _(sc, L=L, p=p, zb=zb):
                    P.w(sc, "scalar", ev_m1, ev_ph.get(p - 2))
                    ev = P.inc(sc.activation(E1[p % 2][:, 0:L], psS[:, 0:L], AF.Exp,
                                             bias=zst[:, zb:zb + 1],
                                             accum_out=zst[:, zb + 2:zb + 3]), "AC")
                    ev_exp1[p] = ev

                @block.tensor
                def _(t, r=r, L=L, L2a=L2a, L2b=L2b, b0=b0, lh2=lh2, p=p):
                    P.w(t, "tensor", ev_exp2a.get(p - 1), ev_neglb, ev_A_done_ac)
                    for (c0, w_) in chunks_aligned(0, L2a):
                        ins = t.matmul(psS2[:, c0:c0 + w_],
                                 QT[:, lh2 * 2048 + r * 128:lh2 * 2048 + (r + 1) * 128],
                                 KT[:, lh2 * 2048 + c0:lh2 * 2048 + c0 + w_],
                                 start=True, stop=True)
                    ev = P.inc(ins, "PE")
                    bx["ea"] = ev
                    if L2b:
                        if L >= 1152:
                            P.w(t, "tensor", ev_exp1[p])
                        P.w(t, "tensor", ev_exp2b.get(p - 1))
                        for (c0, w_) in chunks_aligned(b0, 2048):
                            kk = c0 - b0 + 512
                            ins = t.matmul(psS[:, c0:c0 + w_],
                                     QT[:, lh2 * 2048 + r * 128:lh2 * 2048 + (r + 1) * 128],
                                     KT[:, lh2 * 2048 + kk:lh2 * 2048 + kk + w_],
                                     start=True, stop=True)
                        ev = P.inc(ins, "PE")
                    bx["eb"] = ev
                ev_s2a, ev_s2b = bx["ea"], bx["eb"]

                @block.vector
                def _(v, L=L, L2a=L2a, L2b=L2b, b0=b0, zb=zb):
                    P.w(v, "vector", ev_s2a, ev_s2b)
                    if L2b:
                        v.tensor_tensor(psS[:, 1920:2048], psS[:, 1920:2048],
                                        maskd[:], op=OP.add)
                        v.tensor_reduce(zst[:, zb + 8:zb + 9], psS2[:, 0:L2a], axis=AX.X,
                                        op=OP.max)
                        v.tensor_reduce(zst[:, zb + 9:zb + 10], psS[:, b0:2048], axis=AX.X,
                                        op=OP.max)
                        v.tensor_copy(dmy[:], stab[:, 0:512])
                        v.tensor_reduce(zst[:, zb + 1:zb + 2],
                                        zst[:, zb + 8:zb + 10],
                                        axis=AX.X, op=OP.max, negate=True)
                        bx["e"] = P.inc(v.tensor_copy(dmy[:], stab[:, 0:512]), "DV")
                    else:
                        v.tensor_tensor(psS2[:, L - 128:L], psS2[:, L - 128:L],
                                        maskd[:], op=OP.add)
                        v.tensor_reduce(zst[:, zb + 1:zb + 2], psS2[:, 0:L2a],
                                        axis=AX.X, op=OP.max, negate=True)
                        bx["e"] = P.inc(v.tensor_copy(dmy[:], stab[:, 0:512]), "DV")
                ev_m2 = bx["e"]

                @block.scalar
                def _(sc, L=L, L2a=L2a, L2b=L2b, b0=b0, p=p, zb=zb):
                    P.w(sc, "scalar", ev_m2, ev_ph.get(p - 2))
                    ev = P.inc(sc.activation(E2[p % 2][:, 0:L2a], psS2[:, 0:L2a], AF.Exp,
                                             bias=zst[:, zb + 1:zb + 2],
                                             accum_out=zst[:, zb + 3:zb + 4]), "AC")
                    ev_exp2a[p] = ev
                    if L2b:
                        ev = P.inc(sc.activation(E2[p % 2][:, 512:L], psS[:, b0:2048], AF.Exp,
                                                 bias=zst[:, zb + 1:zb + 2],
                                                 accum_out=zst[:, zb + 4:zb + 5]), "AC")
                    ev_exp2b[p] = ev
                    bx["e"] = ev
                ev_e2done = bx["e"]

                @block.vector
                def _(v, L2b=L2b, p=p, zb=zb):
                    P.w(v, "vector", ev_exp1[p], ev_e2done, ev_neglb)
                    if L2b:
                        v.tensor_tensor(zst[:, zb + 3:zb + 4], zst[:, zb + 3:zb + 4],
                                        zst[:, zb + 4:zb + 5], op=OP.add)
                    v.reciprocal(zst[:, zb + 7:zb + 8], zst[:, zb + 2:zb + 3])
                    v.tensor_tensor(zst[:, zb + 5:zb + 6], zst[:, zb + 2:zb + 3],
                                    neglb[:], op=OP.mult)
                    v.reciprocal(zst[:, zb + 6:zb + 7], zst[:, zb + 3:zb + 4])
                    v.tensor_copy(dmy[:], stab[:, 0:512])
                    v.tensor_tensor(zst[:, zb + 5:zb + 6], zst[:, zb + 5:zb + 6],
                                    zst[:, zb + 6:zb + 7], op=OP.mult)
                    bx["e"] = P.inc(v.tensor_copy(dmy[:], stab[:, 0:512]), "DV")
                ev_c2 = bx["e"]
                ev_c2d[p] = ev_c2

                @block.gpsimd
                def _(g, L=L, p=p, zb=zb):
                    P.w(g, "gpsimd", ev_c2, ev_trlast.get(p - 2))
                    g.tensor_tensor(Ph[p % 2][:, 0:L], E2[p % 2][:, 0:L],
                                    zst[:, zb + 5:zb + 6].broadcast_to([128, L]), op=OP.mult)
                    ev = P.inc(g.tensor_tensor(Ph[p % 2][:, 0:L], Ph[p % 2][:, 0:L],
                                               E1[p % 2][:, 0:L], op=OP.add), "GP")
                    ev_ph[p] = ev

            def emit_tail(p):
                r, lv = p // 4, p % 4
                L = (r + 1) * 128
                rpar = r % 2
                zb = 12 * (p % 3)
                ngr = (L + 511) // 512
                for gi2 in range(ngr):
                    w_ = min(512, L - gi2 * 512)
                    ntr = (w_ + 127) // 128
                    hsel = phsel[0] % 2
                    phsel[0] += 1

                    @block.tensor
                    def _(t, gi2=gi2, ntr=ntr, hsel=hsel, p=p, last=(gi2 == ngr - 1)):
                        P.w(t, "tensor", ev_ph[p], ev_ptA_copy[hsel])
                        for ti in range(ntr):
                            ins = t.transpose(
                                ptA[:, hsel * 1024 + ti * 128:hsel * 1024 + (ti + 1) * 128],
                                Ph[p % 2][:, gi2 * 512 + ti * 128:gi2 * 512 + (ti + 1) * 128],
                                identb[:])
                        ev = P.inc(ins, "PE")
                        bx["e"] = ev
                        if last:
                            ev_trlast[p] = ev
                    ev_tr2 = bx["e"]

                    @block.vector
                    def _(v, gi2=gi2, w_=w_, hsel=hsel, p=p):
                        P.w(v, "vector", ev_tr2, ev_pv.get(p - 2))
                        ev = P.inc(v.tensor_copy(PhT[p % 2][:, gi2 * 512:gi2 * 512 + w_],
                                                 ptA[:, hsel * 1024:hsel * 1024 + w_]),
                                   "DV")
                        bx["e"] = ev
                        ev_ptA_copy[hsel] = ev
                        ev_phtcopy[(p, gi2)] = ev

                @block.tensor
                def _(t, L=L, lv=lv, p=p):
                    P.w(t, "tensor", ev_yevac.get(p - 1), ev_neglb)
                    nkc = L // 128
                    for kc in range(nkc):
                        P.w(t, "tensor", ev_phtcopy[(p, kc // 4)])
                        ins = t.matmul(pvout[:, 0:256],
                                 PhT[p % 2][:, kc * 128:(kc + 1) * 128],
                                 Vsb[:, kc * 1024 + lv * 256:kc * 1024 + (lv + 1) * 256],
                                 start=(kc == 0), stop=(kc == nkc - 1))
                    ev = P.inc(ins, "PE")
                    ev_pv[p] = ev
                    bx["e"] = ev
                ev_pvv = bx["e"]

                @block.scalar
                def _(sc, lv=lv, rpar=rpar, p=p, r=r, zb=zb):
                    P.w(sc, "scalar", ev_pvv, ev_c2d[p])
                    if lv == 0:
                        P.w(sc, "scalar", ev_ypark.get(r - 2))
                    ev = P.inc(sc.activation(ysb[rpar][:, lv * 256:(lv + 1) * 256],
                                             pvout[:, 0:256], AF.Copy,
                                             scale=zst[:, zb + 7:zb + 8]), "AC")
                    ev_yevac[p] = ev

                if lv == 3:
                    @block.sync
                    def _(sy, r=r, rpar=rpar):
                        P.w(sy, "sync", ev_yevac[r * 4 + 3])
                        ev = P.inc(sy.dma_start(ybuf_d[r * 128:(r + 1) * 128, :],
                                                ysb[rpar][:]), f"SY{r % 2}", 16)
                        ev_ypark[r] = ev

            ev_c2d = {}
            for p in range(64):
                if p >= 2:
                    emit_tail(p - 2)
                emit_head(p)
            emit_tail(62)
            emit_tail(63)

            if dbg:
                @block.sync
                def _(sy):
                    P.w(sy, "sync", ev_yevac[63])
                    P.inc(sy.dma_start(dE1[:, :], E1[1][:]), "SP2", 16)
                    P.inc(sy.dma_start(dE2[:, :], E2[1][:]), "SP2", 16)
                    P.inc(sy.dma_start(dPhT[:, :], PhT[1][:]), "SP2", 16)
                    P.inc(sy.dma_start(dZ[:, :], zst[:]), "SP2", 16)
                    P.inc(sy.dma_start(dNL[:, :], neglb[:]), "SP2", 16)
                    P.inc(sy.dma_start(dLS[:, :], lsc[:]), "SP2", 16)

            @block.gpsimd
            def _(g):
                P.w(g, "gpsimd", ("SY0", P.cnt["SY0"]), ("SY1", P.cnt["SY1"]))
                if dbg:
                    P.w(g, "gpsimd", ("SP2", P.cnt["SP2"]))
                bx["e"] = P.inc(g.collective_compute(
                    "AllGather", OP.bypass,
                    replica_groups=[[0, 1], [2, 3], [4, 5], [6, 7]],
                    ins=[ybuf_d.ap().opt()], outs=[ygath_d.ap().opt()]), "CC")
            ev_cc = bx["e"]
            ev_B_done_dv = ("DV", P.cnt["DV"])
            ev_B_done_ac = ("AC", P.cnt["AC"])
            ev_B_done_pe = ("PE", P.cnt["PE"])
        esAB.close()

        with ExitStack() as esC:
            EC = esC.enter_context
            xn2T = EC(nc.sbuf_tensor("xn2T_s", [128, 16384], BF16))
            fuse = EC(nc.sbuf_tensor("fuse_s", [128, 32768], BF16))
            Wf = [EC(nc.sbuf_tensor(f"Wf{i}_s", [128, 8192], BF16)) for i in range(2)]
            gatesb = EC(nc.sbuf_tensor("gatesb_s", [128, 2048], BF16))
            upsb = EC(nc.sbuf_tensor("upsb_s", [128, 512], BF16))
            outtm = EC(nc.sbuf_tensor("outtm_s", [128, 8192], F32))
            x2t = EC(nc.sbuf_tensor("x2t_s", [128, 2048], F32))
            yc = EC(nc.sbuf_tensor("yc_s", [128, 2048], BF16))
            xn2 = EC(nc.sbuf_tensor("xn2_s", [128, 2048], BF16))
            dwn = EC(nc.sbuf_tensor("dwn_s", [128, 512], F32))

            ev_park = {}
            ev_x2done = {}
            for tp in range(8):

                @block.gpsimd
                def _(g, tp=tp):
                    P.w(g, "gpsimd", ev_cc, ev_tabs, ev_x2done.get(tp - 1))
                    P.inc(g.indirect_dma_start(
                        yc[:, 0:1024], None, ygath_d[:, :],
                        IndirectOffsetOnAxis(idxg[:, tp:tp + 1], 0)), "LYC", 16)
                    bx["e"] = P.inc(g.indirect_dma_start(
                        yc[:, 1024:2048], None, ygath_d[:, :],
                        IndirectOffsetOnAxis(idxg[:, 8 + tp:9 + tp], 0)), "LYC", 16)
                ev_yc = bx["e"]

                @block.sync
                def _(sy, tp=tp):
                    P.w(sy, "sync", ev_x2done.get(tp - 1))
                    bx["e"] = P.inc(
                        sy.dma_start(x2t[:], xown_d[tp * 128:(tp + 1) * 128, :]),
                        "LXO", 16)
                ev_xo = bx["e"]

                @block.vector
                def _(v):
                    P.w(v, "vector", ev_yc, ev_xo)
                    bx["e"] = P.inc(v.tensor_tensor(x2t[:], x2t[:], yc[:], op=OP.add),
                                    "DV")
                ev_x2 = bx["e"]

                @block.sync
                def _(sy, tp=tp):
                    P.w(sy, "sync", ev_x2)
                    ev = P.inc(sy.dma_start(out_d[tp * 128:(tp + 1) * 128, :], x2t[:]),
                               "SP2", 16)
                    ev_park[tp] = ev

                @block.scalar
                def _(sc):
                    P.w(sc, "scalar", ev_x2)
                    bx["e"] = P.inc(sc.activation(sqd[:], x2t[:], AF.Square,
                                                  accum_out=lnst[:, 0:1]), "AC")
                ev_sq2 = bx["e"]

                @block.vector
                def _(v):
                    P.w(v, "vector", ev_sq2)
                    v.reduce_sum(lnst[:, 1:2], x2t[:], axis=AX.X)
                    v.tensor_copy(dmy[:], stab[:, 0:512])
                    v.tensor_scalar_mul(lnst[:, 2:3], lnst[:, 1:2], cst[:, 0:1])
                    bx["e"] = P.inc(v.tensor_copy(dmy[:], stab[:, 0:512]), "DV")
                ev_mean2 = bx["e"]

                @block.scalar
                def _(sc):
                    P.w(sc, "scalar", ev_mean2)
                    sc.activation(lnst[:, 3:4], lnst[:, 2:3], AF.Square)
                    bx["e"] = P.inc(sc.activation(dmy[:, 0:256], stab[:, 0:256], AF.Copy), "AC")
                ev_msq2 = bx["e"]

                @block.vector
                def _(v):
                    P.w(v, "vector", ev_msq2)
                    v.tensor_scalar(lnst[:, 4:5], lnst[:, 3:4], cst[:, 1:2], cst[:, 2:3],
                                    op0=OP.mult, op1=OP.add)
                    bx["e"] = P.inc(v.tensor_copy(dmy[:], stab[:, 0:512]), "DV")
                ev_st2 = bx["e"]

                @block.scalar
                def _(sc):
                    P.w(sc, "scalar", ev_st2)
                    sc.activation(lnst[:, 5:6], lnst[:, 0:1], AF.Sqrt,
                                  scale=cst[:, 0:1], bias=lnst[:, 4:5])
                    bx["e"] = P.inc(sc.activation(dmy[:, 0:256], stab[:, 0:256], AF.Copy), "AC")
                ev_sd2 = bx["e"]

                @block.vector
                def _(v, tp=tp):
                    P.w(v, "vector", ev_sd2)
                    v.reciprocal(lnst[:, 6:7], lnst[:, 5:6])
                    v.tensor_copy(dmy[:], stab[:, 0:512])
                    ev = P.inc(v.tensor_scalar(xn2[:], x2t[:], lnst[:, 2:3], lnst[:, 6:7],
                                               op0=OP.subtract, op1=OP.mult), "DV")
                    bx["e"] = ev
                    ev_x2done[tp] = ev
                ev_xn2 = bx["e"]

                for g2 in range(4):
                    hsel = g2 % 2

                    @block.tensor
                    def _(t, g2=g2, hsel=hsel):
                        P.w(t, "tensor", ev_xn2, ev_ptA_copy[hsel], ev_B_done_dv)
                        for d4 in range(4):
                            dc = g2 * 4 + d4
                            ins = t.transpose(
                                ptA[:, hsel * 1024 + d4 * 128:hsel * 1024 + (d4 + 1) * 128],
                                xn2[:, dc * 128:(dc + 1) * 128], identb[:])
                        bx["e"] = P.inc(ins, "PE")
                    ev_t2 = bx["e"]

                    @block.vector
                    def _(v, g2=g2, hsel=hsel, tp=tp):
                        P.w(v, "vector", ev_t2)
                        dst = bass.AP(xn2T, (g2 * 4) * 1024 + tp * 128,
                                      [[16384, 128], [1024, 4], [1, 128]])
                        src = ptA[:, hsel * 1024:hsel * 1024 + 512].rearrange(
                            "p (h d) -> p h d", h=4)
                        ev = P.inc(v.tensor_copy(dst, src), "DV")
                        bx["e"] = ev
                        ev_ptA_copy[hsel] = ev
            ev_xn2T_done = ("DV", P.cnt["DV"])

            GP_SL = [psS2[:, 0:512], psS[:, 0:512]]
            UP_SL = [psS[:, 512:1024], pvout[:, 0:512]]
            ev_silu = {}
            ev_upev = {}
            ev_fuse_ev = {}
            ev_dts = {}
            ev_dcopy = {}
            ev_acc = {}
            ev_gmm = {}
            ev_umm = {}
            ev_dmm = {}
            ev_dtr_prev = [None]

            for g in range(2):
                for f_i in range(16):

                    @block.gpsimd
                    def _(gp, g=g, f_i=f_i):
                        P.w(gp, "gpsimd", ev_gmm.get((g, f_i - 1)),
                            ev_dmm.get((g - 1, 15)))
                        bx["e0"] = P.inc(gp.dma_start(Wf[0][:], wg_d[f_i, :, :]), "LW0", 16)
                        P.w(gp, "gpsimd", ev_umm.get((g, f_i - 1)))
                        bx["e"] = P.inc(gp.dma_start(Wf[1][:], wu_d[f_i, :, :]), "LW1", 16)
                    ev_Wg0 = bx["e0"]
                    ev_Wgu = bx["e"]

                    for f128 in range(4):
                        ic = f_i * 4 + f128
                        gsl = GP_SL[ic % 2]
                        usl = UP_SL[ic % 2]

                        @block.tensor
                        def _(t, g=g, f_i=f_i, f128=f128, ic=ic, gsl=gsl, usl=usl):
                            P.w(t, "tensor", ev_Wg0, ev_Wgu, ev_xn2T_done,
                                ev_B_done_ac)
                            P.w(t, "tensor", ev_silu.get((g, ic - 2)))
                            if g > 0 and ic < 2:
                                P.w(t, "tensor", ev_dts.get((g - 1, 12 + ic)))
                            for kc in range(16):
                                ins = t.matmul(gsl,
                                         Wf[0][:, kc * 512 + f128 * 128:kc * 512 + (f128 + 1) * 128],
                                         xn2T[:, kc * 1024 + g * 512:kc * 1024 + (g + 1) * 512],
                                         start=(kc == 0), stop=(kc == 15))
                            ev = P.inc(ins, "PE")
                            ev_gmm[(g, f_i)] = ev
                            bx["eg"] = ev
                            P.w(t, "tensor", ev_upev.get((g, ic - 2)))
                            if g > 0 and ic < 2:
                                P.w(t, "tensor", ev_dts.get((g - 1, 13 + ic)))
                            for kc in range(16):
                                ins = t.matmul(usl,
                                         Wf[1][:, kc * 512 + f128 * 128:kc * 512 + (f128 + 1) * 128],
                                         xn2T[:, kc * 1024 + g * 512:kc * 1024 + (g + 1) * 512],
                                         start=(kc == 0), stop=(kc == 15))
                            ev = P.inc(ins, "PE")
                            ev_umm[(g, f_i)] = ev
                            bx["eu"] = ev
                        evg, evu = bx["eg"], bx["eu"]

                        @block.scalar
                        def _(sc, g=g, ic=ic, f128=f128, gsl=gsl):
                            P.w(sc, "scalar", evg, ev_fuse_ev.get((g, ic - 4)))
                            ev = P.inc(sc.activation(gatesb[:, f128 * 512:(f128 + 1) * 512],
                                                     gsl, AF.Silu,
                                                     scale=sgt[:, ic:ic + 1]), "AC")
                            ev_silu[(g, ic)] = ev
                            bx["e"] = ev
                        evsi = bx["e"]

                        @block.vector
                        def _(v, g=g, ic=ic, usl=usl):
                            P.w(v, "vector", evu, ev_fuse_ev.get((g, ic - 1)))
                            ev = P.inc(v.tensor_scalar_mul(upsb[:], usl,
                                                           sut[:, ic:ic + 1]), "DV")
                            ev_upev[(g, ic)] = ev
                            bx["e"] = ev
                        evup = bx["e"]

                        @block.gpsimd
                        def _(gp, g=g, ic=ic, f128=f128):
                            P.w(gp, "gpsimd", evsi, evup)
                            if ic == 0:
                                P.w(gp, "gpsimd", ev_dmm.get((g - 1, 15)))
                            ev = P.inc(gp.tensor_tensor(
                                fuse[:, ic * 512:(ic + 1) * 512],
                                gatesb[:, f128 * 512:(f128 + 1) * 512],
                                upsb[:], op=OP.mult), "GP")
                            ev_fuse_ev[(g, ic)] = ev

                ev_fuse_all = ("GP", P.cnt["GP"])
                for fc in range(16):

                    @block.gpsimd
                    def _(gp, g=g, fc=fc):
                        P.w(gp, "gpsimd", ev_dmm.get((g, fc - 2)))
                        if fc < 2:
                            P.w(gp, "gpsimd", ev_umm[(g, 15)])
                        bx["e"] = P.inc(gp.dma_start(Wf[fc % 2][:], wd_d[fc, :, :]),
                                        f"LW{fc % 2}", 16)
                    ev_Wd = bx["e"]

                    @block.tensor
                    def _(t, g=g, fc=fc):
                        P.w(t, "tensor", ev_Wd, ev_fuse_all)
                        P.w(t, "tensor", ev_dts.get((g, fc - 4)))
                        dsl = psS[:, (fc % 4) * 512:((fc % 4) + 1) * 512]
                        for ic in range(64):
                            ins = t.matmul(dsl,
                                     Wf[fc % 2][:, ic * 128:(ic + 1) * 128],
                                     fuse[:, ic * 512:(ic + 1) * 512],
                                     start=(ic == 0), stop=(ic == 63))
                        ev = P.inc(ins, "PE")
                        ev_dmm[(g, fc)] = ev
                        bx["e"] = ev
                    ev_dm = bx["e"]

                    @block.vector
                    def _(v, g=g, fc=fc):
                        P.w(v, "vector", ev_dm, ev_dtr_prev[0])
                        dsl = psS[:, (fc % 4) * 512:((fc % 4) + 1) * 512]
                        ev = P.inc(v.tensor_scalar_mul(dwn[:], dsl, sdt[:, fc:fc + 1]),
                                   "DV")
                        ev_dts[(g, fc)] = ev
                        bx["e"] = ev
                    ev_dt = bx["e"]

                    @block.tensor
                    def _(t, g=g, fc=fc):
                        P.w(t, "tensor", ev_dt, ev_dcopy.get((g, fc - 1)))
                        for tl in range(4):
                            ins = t.transpose(pvout[:, tl * 128:(tl + 1) * 128],
                                        dwn[:, tl * 128:(tl + 1) * 128], identf[:])
                        ev = P.inc(ins, "PE")
                        ev_dtr_prev[0] = ev
                        bx["e"] = ev
                    ev_dtr = bx["e"]

                    @block.vector
                    def _(v, g=g, fc=fc):
                        P.w(v, "vector", ev_dtr)
                        if fc == 0:
                            P.w(v, "vector", ev_acc.get(g - 1))
                        dst = bass.AP(outtm, fc * 128, [[8192, 128], [2048, 4], [1, 128]])
                        src = pvout[:, 0:512].rearrange("p (h d) -> p h d", h=4)
                        ev = P.inc(v.tensor_copy(dst, src), "DV")
                        ev_dcopy[(g, fc)] = ev

                for tl in range(4):
                    tp = g * 4 + tl

                    @block.gpsimd
                    def _(gp, g=g, tl=tl, tp=tp):
                        P.w(gp, "gpsimd", ev_dcopy[(g, 15)])
                        P.w(gp, "gpsimd", ("SP2", P.cnt["SP2"]))
                        ev = P.inc(gp.dma_start(out_d[tp * 128:(tp + 1) * 128, :],
                                                outtm[:, tl * 2048:(tl + 1) * 2048],
                                                accum_op=OP.add), "SG", 16)
                        bx["e"] = ev
                ev_acc[g] = ("SG", P.cnt["SG"])

            @block.gpsimd
            def _(gp):
                gp.wait_ge(sems["SG"], P.cnt["SG"])
                gp.wait_ge(sems["SP2"], P.cnt["SP2"])
                gp.wait_ge(sems["SY0"], P.cnt["SY0"])
                gp.wait_ge(sems["SY1"], P.cnt["SY1"])

            @block.sync
            def _(sy):
                sy.wait_ge(sems["SP2"], P.cnt["SP2"])

    return nc


_NC = None


def _get_nc():
    global _NC
    if _NC is None:
        _NC = build_program()
    return _NC


def _host_tables():
    i = np.arange(64, dtype=np.float64)
    inv_freq = 1.0 / (10000.0 ** (2 * i / HD))
    t = np.arange(T, dtype=np.float64)
    fr = t[:, None] * inv_freq[None, :]
    cos = np.cos(fr).astype(np.float32)
    sin = np.sin(fr).astype(np.float32)
    C = np.concatenate([cos, cos], axis=1)
    S = np.concatenate([sin, -sin], axis=1)
    ctab = np.ascontiguousarray(
        C.reshape(16, 128, 128).transpose(1, 0, 2).reshape(128, 2048)).astype(bf16)
    stab = np.ascontiguousarray(
        S.reshape(16, 128, 128).transpose(1, 0, 2).reshape(128, 2048)).astype(bf16)
    logpos = np.log(np.arange(1, T + 1, dtype=np.float32))
    logpt = np.ascontiguousarray(logpos.reshape(16, 128).T).astype(np.float32)
    p = np.arange(128)
    ii = np.arange(128)
    maskd = np.where(ii[None, :] <= p[:, None], 0.0, NEG).astype(np.float32).astype(bf16)
    identb = np.eye(128, dtype=np.float32).astype(bf16)
    identf = np.eye(128, dtype=np.float32)
    onesf = np.ones((1, 128), np.float32)
    return ctab, stab, logpt, maskd, identb, identf, onesf


def make_inmaps(L0_attn_x, L0_attn_Wq, L0_attn_Wk, L0_attn_Wv,
                L0_attn_lambda_q1, L0_attn_lambda_k1, L0_attn_lambda_q2, L0_attn_lambda_k2,
                L0_attn_softmax_scaler, L0_ffn_w_gate, L0_ffn_w_up, L0_ffn_w_down,
                L0_ffn_s_gate, L0_ffn_s_up, L0_ffn_s_down):
    x = np.asarray(L0_attn_x, dtype=np.float32)
    Wq = np.asarray(L0_attn_Wq, dtype=np.float32)
    Wk = np.asarray(L0_attn_Wk, dtype=np.float32)
    Wv = np.asarray(L0_attn_Wv, dtype=np.float32)
    Wg = np.asarray(L0_ffn_w_gate)
    Wu = np.asarray(L0_ffn_w_up)
    Wd = np.asarray(L0_ffn_w_down)
    sg = np.asarray(L0_ffn_s_gate, dtype=np.float32)
    su = np.asarray(L0_ffn_s_up, dtype=np.float32)
    sd = np.asarray(L0_ffn_s_down, dtype=np.float32)
    scaler = np.asarray(L0_attn_softmax_scaler, dtype=np.float32).reshape(16)
    lam = np.concatenate([np.asarray(v, np.float32).reshape(64) for v in
                          (L0_attn_lambda_q1, L0_attn_lambda_k1,
                           L0_attn_lambda_q2, L0_attn_lambda_k2)]).reshape(1, 256)

    ctab, stab, logpt, maskd, identb, identf, onesf = _host_tables()
    cstv = np.broadcast_to(np.array([1.0 / D, -1.0, EPS_LN, 1.0 / HD, EPS_RMS,
                                     -LAMBDA_INIT, 1.0 / math.sqrt(HD), -0.5],
                                    np.float32), (128, 8)).copy()
    wg_r = np.ascontiguousarray(
        Wg.reshape(16, 128, 16, 512).transpose(2, 1, 0, 3).reshape(16, 128, 8192))
    wu_r = np.ascontiguousarray(
        Wu.reshape(16, 128, 16, 512).transpose(2, 1, 0, 3).reshape(16, 128, 8192))
    wd_r = np.ascontiguousarray(
        Wd.reshape(64, 128, 16, 128).transpose(2, 1, 0, 3).reshape(16, 128, 8192))
    sgt = np.ascontiguousarray(sg.reshape(64, 128).T)
    sut = np.ascontiguousarray(su.reshape(64, 128).T)
    sdt = np.ascontiguousarray(sd.reshape(16, 128).T)

    in_maps = []
    for c in range(8):
        b, j = c // 2, c % 2
        qheads = [4 * j + i for i in range(4)] + [8 + 4 * j + i for i in range(4)]
        qcols = np.concatenate([np.arange(h * 128, (h + 1) * 128) for h in qheads])
        vcols = np.arange(4 * j * 256, 4 * j * 256 + 1024)

        def tile_w(W, cols):
            Ws = np.ascontiguousarray(W[:, cols])
            return np.ascontiguousarray(
                Ws.reshape(16, 128, 2, 512).transpose(2, 1, 0, 3).reshape(2, 128, 8192))

        pp = np.arange(128, dtype=np.int64)
        idxg = np.zeros((128, 16), np.int32)
        for tp in range(8):
            idxg[:, tp] = j * 1024 + tp * 128 + pp
            idxg[:, 8 + tp] = 2048 + j * 1024 + tp * 128 + pp

        in_maps.append({
            "x": np.ascontiguousarray(x[b]),
            "xown": np.ascontiguousarray(x[b, j * 1024:(j + 1) * 1024]),
            "wq": tile_w(Wq, qcols),
            "wk": tile_w(Wk, qcols),
            "wv": tile_w(Wv, vcols),
            "wg": wg_r, "wu": wu_r, "wd": wd_r,
            "sgt": sgt, "sut": sut, "sdt": sdt,
            "lam": lam,
            "scl": np.ascontiguousarray(scaler[qheads]).reshape(1, 8),
            "ctab": ctab, "stab": stab, "logpt": logpt, "maskd": maskd,
            "identb": identb, "identf": identf, "onesf": onesf,
            "idxg": idxg, "cst": cstv,
        })

    return in_maps


def kernel(**inputs):
    in_maps = make_inmaps(**inputs)
    nc = _get_nc()
    res = run_bass_kernel_spmd(nc, in_maps, core_ids=list(range(8)))
    out = np.empty((4, T, D), np.float32)
    for c in range(8):
        b, j = c // 2, c % 2
        out[b, j * 1024:(j + 1) * 1024, :] = np.asarray(res.results[c]["out"])
    return out


# version 04 (final)
# speedup vs baseline: 1.2945x; 1.0316x over previous
import math
from contextlib import ExitStack

import numpy as np
import ml_dtypes
import concourse.bass as bass
import concourse.mybir as mybir
from concourse.bass_core_rust import IndirectOffsetOnAxis
from concourse.bass_utils import run_bass_kernel_spmd

F32 = mybir.dt.float32
BF16 = mybir.dt.bfloat16
I8 = mybir.dt.int8
I32 = mybir.dt.int32
AF = mybir.ActivationFunctionType
OP = mybir.AluOpType
AX = mybir.AxisListType
bf16 = ml_dtypes.bfloat16

T = 2048
D = 2048
HD = 128
INTER = 8192
EPS_RMS = float(np.finfo(np.float32).eps)
EPS_LN = 1e-5
LAMBDA_INIT = 0.2
NEG = -1e30


class Prog:

    def __init__(self, nc, sems, serial=False):
        self.nc = nc
        self.sem = sems
        self.serial = serial
        self.cnt = {k: 0 for k in sems}
        self.lastw = {e: {k: 0 for k in sems} for e in
                      ("tensor", "vector", "scalar", "gpsimd", "sync")}
        self.own = {"tensor": "PE", "vector": "DV", "scalar": "AC", "gpsimd": "GP"}

    def inc(self, instr, s, n=1):
        instr.then_inc(self.sem[s], n)
        self.cnt[s] += n
        return (s, self.cnt[s])

    def w(self, eng, engname, *evs):
        if self.serial:
            evs = tuple(evs) + tuple((s, self.cnt[s]) for s in self.cnt
                                     if s not in ("CC",))
        for ev in evs:
            if ev is None:
                continue
            s, v = ev
            if v <= 0:
                continue
            if self.own.get(engname) == s:
                continue
            if self.lastw[engname][s] >= v:
                continue
            eng.wait_ge(self.sem[s], v)
            self.lastw[engname][s] = v


def chunks_aligned(c0, c1):
    out = []
    c = c0
    while c < c1:
        nxt = min(c1, (c // 512 + 1) * 512)
        out.append((c, nxt - c))
        c = nxt
    return out


DBG = False


def build_program(dbg=False, serial=False):
    nc = bass.Bass("TRN2", target_bir_lowering=False, detect_race_conditions=False)

    x_d = nc.dram_tensor("x", [T, D], F32, kind="ExternalInput")
    xown_d = nc.dram_tensor("xown", [1024, D], F32, kind="ExternalInput")
    wq_d = nc.dram_tensor("wq", [2, 128, 8192], F32, kind="ExternalInput")
    wk_d = nc.dram_tensor("wk", [2, 128, 8192], F32, kind="ExternalInput")
    wv_d = nc.dram_tensor("wv", [2, 128, 8192], F32, kind="ExternalInput")
    wg_d = nc.dram_tensor("wg", [16, 128, 8192], I8, kind="ExternalInput")
    wu_d = nc.dram_tensor("wu", [16, 128, 8192], I8, kind="ExternalInput")
    wd_d = nc.dram_tensor("wd", [16, 128, 8192], I8, kind="ExternalInput")
    sgt_d = nc.dram_tensor("sgt", [128, 64], F32, kind="ExternalInput")
    sut_d = nc.dram_tensor("sut", [128, 64], F32, kind="ExternalInput")
    sdt_d = nc.dram_tensor("sdt", [128, 16], F32, kind="ExternalInput")
    lam_d = nc.dram_tensor("lam", [1, 256], F32, kind="ExternalInput")
    scl_d = nc.dram_tensor("scl", [1, 8], F32, kind="ExternalInput")
    ctab_d = nc.dram_tensor("ctab", [128, 2048], BF16, kind="ExternalInput")
    stab_d = nc.dram_tensor("stab", [128, 2048], BF16, kind="ExternalInput")
    logpt_d = nc.dram_tensor("logpt", [128, 16], F32, kind="ExternalInput")
    maskd_d = nc.dram_tensor("maskd", [128, 128], BF16, kind="ExternalInput")
    identb_d = nc.dram_tensor("identb", [128, 128], BF16, kind="ExternalInput")
    identf_d = nc.dram_tensor("identf", [128, 128], F32, kind="ExternalInput")
    onesf_d = nc.dram_tensor("onesf", [1, 128], F32, kind="ExternalInput")
    idxg_d = nc.dram_tensor("idxg", [128, 16], I32, kind="ExternalInput")
    cst_d = nc.dram_tensor("cst", [128, 8], F32, kind="ExternalInput")
    out_d = nc.dram_tensor("out", [1024, D], F32, kind="ExternalOutput")
    ybuf_d = nc.dram_tensor("ybuf", [T, 1024], BF16)
    ygath_d = nc.dram_tensor("ygath", [2 * T, 1024], BF16)
    if dbg:
        dKT = nc.dram_tensor("dKT", [128, 16384], BF16, kind="ExternalOutput")
        dQT = nc.dram_tensor("dQT", [128, 16384], BF16, kind="ExternalOutput")
        dV = nc.dram_tensor("dV", [128, 16384], BF16, kind="ExternalOutput")
        dXNT = nc.dram_tensor("dXNT", [128, 16384], BF16, kind="ExternalOutput")
        dXNT0 = nc.dram_tensor("dXNT0", [128, 16384], BF16, kind="ExternalOutput")
        dXT0 = nc.dram_tensor("dXT0", [128, 2048], F32, kind="ExternalOutput")
        dLN = nc.dram_tensor("dLN", [128, 8], F32, kind="ExternalOutput")
        dSCL = nc.dram_tensor("dSCL", [128, 8], F32, kind="ExternalOutput")
        dSST = nc.dram_tensor("dSST", [128, 16], F32, kind="ExternalOutput")
        dE1 = nc.dram_tensor("dE1", [128, 2048], F32, kind="ExternalOutput")
        dE2 = nc.dram_tensor("dE2", [128, 2048], F32, kind="ExternalOutput")
        dPhT = nc.dram_tensor("dPhT", [128, 2048], BF16, kind="ExternalOutput")
        dZ = nc.dram_tensor("dZ", [128, 36], F32, kind="ExternalOutput")
        dNL = nc.dram_tensor("dNL", [128, 1], F32, kind="ExternalOutput")
        dLS = nc.dram_tensor("dLS", [1, 80], F32, kind="ExternalOutput")

    bx = {}

    es = ExitStack()
    E = es.enter_context
    with es:
        sems = {}
        for s in ("LT", "LX0", "LX1", "LXO", "LW0", "LW1", "LYC", "SY0", "SY1",
                  "SP2", "SG", "CC", "PE", "DV", "AC", "GP"):
            sems[s] = E(nc.semaphore(s))
        block = E(nc.Block())
        P = Prog(nc, sems, serial=serial)

        psS = E(nc.psum_tensor("psS", [128, 2048], F32))
        psS2 = E(nc.psum_tensor("psS2", [128, 512], F32))
        pvout = E(nc.psum_tensor("pvout", [128, 512], F32))
        ptA = E(nc.psum_tensor("ptA", [128, 2048], BF16))

        ctab = E(nc.sbuf_tensor("ctab_s", [128, 2048], BF16))
        stab = E(nc.sbuf_tensor("stab_s", [128, 2048], BF16))
        logpt = E(nc.sbuf_tensor("logpt_s", [128, 16], F32))
        maskd = E(nc.sbuf_tensor("maskd_s", [128, 128], BF16))
        identb = E(nc.sbuf_tensor("identb_s", [128, 128], BF16))
        identf = E(nc.sbuf_tensor("identf_s", [128, 128], F32))
        onesf = E(nc.sbuf_tensor("onesf_s", [1, 128], F32))
        scl = E(nc.sbuf_tensor("scl_s", [1, 8], F32))
        sclrep = E(nc.sbuf_tensor("sclrep_s", [128, 8], F32))
        lps = E(nc.sbuf_tensor("lps_s", [128, 128], F32))
        lam = E(nc.sbuf_tensor("lam_s", [1, 256], F32))
        lsc = E(nc.sbuf_tensor("lsc_s", [1, 80], F32))
        neglb = E(nc.sbuf_tensor("neglb_s", [128, 1], F32))
        sgt = E(nc.sbuf_tensor("sgt_s", [128, 64], F32))
        sut = E(nc.sbuf_tensor("sut_s", [128, 64], F32))
        sdt = E(nc.sbuf_tensor("sdt_s", [128, 16], F32))
        idxg = E(nc.sbuf_tensor("idxg_s", [128, 16], I32))
        cst = E(nc.sbuf_tensor("cst_s", [128, 8], F32))
        sqd = E(nc.sbuf_tensor("sqd_s", [128, 2048], BF16))
        lnst = E(nc.sbuf_tensor("lnst_s", [128, 8], F32))
        zst = E(nc.sbuf_tensor("zst_s", [128, 36], F32))
        dmy = E(nc.sbuf_tensor("dmy_s", [128, 512], BF16))

        @block.sync
        def _(sy):
            for dst, src in ((ctab, ctab_d), (stab, stab_d), (logpt, logpt_d),
                             (maskd, maskd_d), (identb, identb_d), (identf, identf_d),
                             (onesf, onesf_d), (scl, scl_d), (lam, lam_d),
                             (sgt, sgt_d), (sut, sut_d), (sdt, sdt_d), (idxg, idxg_d),
                             (cst, cst_d)):
                P.inc(sy.dma_start(dst[:], src[:, :]), "LT", 16)
        ev_tabs = ("LT", P.cnt["LT"])

        @block.tensor
        def _(t):
            P.w(t, "tensor", ev_tabs)
            ins = t.matmul(psS2[:, 0:8], onesf[:, :], scl[:, :], start=True, stop=True)
            bx["e"] = P.inc(ins, "PE")
        ev_sclmm = bx["e"]

        @block.vector
        def _(v):
            P.w(v, "vector", ev_sclmm, ev_tabs)
            v.tensor_scalar_mul(sclrep[:], psS2[:, 0:8], cst[:, 6:7])
            bx["e"] = P.inc(v.tensor_copy(dmy[:], stab[:, 0:512]), "DV")
        ev_sclrep = bx["e"]

        @block.vector
        def _(v):
            lp3 = logpt[:, :].unsqueeze(2).broadcast_to([128, 16, 8])
            sc3 = sclrep[:, :].unsqueeze(1).broadcast_to([128, 16, 8])
            o3 = lps[:, :].rearrange("p (t h) -> p t h", t=16)
            bx["e"] = P.inc(v.tensor_tensor(o3, lp3, sc3, op=OP.mult), "DV")
        ev_lps = bx["e"]

        esAB = ExitStack()
        KT = esAB.enter_context(nc.sbuf_tensor("KT_s", [128, 16384], BF16))
        QT = esAB.enter_context(nc.sbuf_tensor("QT_s", [128, 16384], BF16))
        Vsb = esAB.enter_context(nc.sbuf_tensor("Vsb_s", [128, 16384], BF16))

        ev_ptA_copy = [None, None]

        with ExitStack() as esA:
            EA = esA.enter_context
            xnT = EA(nc.sbuf_tensor("xnT_s", [128, 16384], BF16))
            xt = [EA(nc.sbuf_tensor(f"xt{i}_s", [128, 2048], F32)) for i in range(2)]
            xnb = EA(nc.sbuf_tensor("xnb_s", [128, 2048], BF16))
            Wbuf = [EA(nc.sbuf_tensor(f"Wb{i}_s", [128, 8192], BF16)) for i in range(2)]
            stg = EA(nc.sbuf_tensor("stg_s", [128, 2 * 3 * 512], BF16))
            sst = EA(nc.sbuf_tensor("sst_s", [128, 16], F32))

            ev_xt_free = {}
            ev_xnT_tile = {}
            ev_xnb_free = {}
            ev_unit_mm = {}
            ev_slice_free = {}
            ev_rot_prev = [None]
            pend_rot = []
            rot_hsel = [0]
            kqc = [0]
            ev_rt_prev = [None]
            ev_evac_prev = [None]
            guse = [0]
            SLICES = [psS[:, 0:512], psS[:, 512:1024], psS[:, 1024:1536],
                      psS[:, 1536:2048], psS2[:, 0:512], pvout[:, 0:512]]
            wd_map = {"k": wk_d, "q": wq_d, "v": wv_d}

            for half in range(2):
                tiles = list(range(8 * half, 8 * half + 8))
                for s, tt in enumerate(tiles):
                    gi = half * 8 + s
                    par = gi % 2

                    @block.sync
                    def _(sy, tt=tt, par=par, gi=gi):
                        P.w(sy, "sync", ev_xt_free.get(gi - 2))
                        if dbg and gi == 8:
                            P.w(sy, "sync", ("SP2", P.cnt["SP2"]))
                        bx["e"] = P.inc(
                            sy.dma_start(xt[par][:], x_d[tt * 128:(tt + 1) * 128, :]),
                            f"LX{gi % 2}", 16)
                    ev_x = bx["e"]

                    @block.scalar
                    def _(sc, par=par):
                        P.w(sc, "scalar", ev_x)
                        bx["e"] = P.inc(sc.activation(sqd[:], xt[par][:], AF.Square,
                                                      accum_out=lnst[:, 0:1]), "AC")
                    ev_sq = bx["e"]

                    @block.vector
                    def _(v, par=par):
                        P.w(v, "vector", ev_x, ev_sq)
                        v.reduce_sum(lnst[:, 1:2], xt[par][:], axis=AX.X)
                        v.tensor_copy(dmy[:], stab[:, 0:512])
                        v.tensor_scalar_mul(lnst[:, 2:3], lnst[:, 1:2], cst[:, 0:1])
                        bx["e"] = P.inc(v.tensor_copy(dmy[:], stab[:, 0:512]), "DV")
                    ev_mean = bx["e"]

                    @block.scalar
                    def _(sc):
                        P.w(sc, "scalar", ev_mean)
                        sc.activation(lnst[:, 3:4], lnst[:, 2:3], AF.Square)
                        bx["e"] = P.inc(sc.activation(dmy[:, 0:256], stab[:, 0:256], AF.Copy), "AC")
                    ev_msq = bx["e"]

                    @block.vector
                    def _(v):
                        P.w(v, "vector", ev_msq)
                        v.tensor_scalar(lnst[:, 4:5], lnst[:, 3:4], cst[:, 1:2], cst[:, 2:3],
                                        op0=OP.mult, op1=OP.add)
                        bx["e"] = P.inc(v.tensor_copy(dmy[:], stab[:, 0:512]), "DV")
                    ev_stats = bx["e"]

                    @block.scalar
                    def _(sc):
                        P.w(sc, "scalar", ev_stats)
                        sc.activation(lnst[:, 5:6], lnst[:, 0:1], AF.Sqrt,
                                      scale=cst[:, 0:1], bias=lnst[:, 4:5])
                        bx["e"] = P.inc(sc.activation(dmy[:, 0:256], stab[:, 0:256], AF.Copy), "AC")
                    ev_std = bx["e"]

                    @block.vector
                    def _(v, par=par, gi=gi):
                        P.w(v, "vector", ev_std, ev_xnb_free.get(gi - 1))
                        v.reciprocal(lnst[:, 6:7], lnst[:, 5:6])
                        v.tensor_copy(dmy[:], stab[:, 0:512])
                        ev = P.inc(v.tensor_scalar(xnb[:], xt[par][:], lnst[:, 2:3],
                                                   lnst[:, 6:7], op0=OP.subtract,
                                                   op1=OP.mult), "DV")
                        bx["e"] = ev
                        ev_xt_free[gi] = ev
                    ev_xn = bx["e"]

                    for g in range(4):
                        hsel = g % 2

                        @block.tensor
                        def _(t, g=g, hsel=hsel, gi=gi):
                            P.w(t, "tensor", ev_xn, ev_ptA_copy[hsel])
                            for d4 in range(4):
                                dc = g * 4 + d4
                                ins = t.transpose(
                                    ptA[:, hsel * 1024 + d4 * 128:hsel * 1024 + (d4 + 1) * 128],
                                    xnb[:, dc * 128:(dc + 1) * 128], identb[:])
                            ev = P.inc(ins, "PE")
                            bx["e"] = ev
                            if g == 3:
                                ev_xnb_free[gi] = ev
                        ev_tr = bx["e"]

                        @block.vector
                        def _(v, g=g, hsel=hsel, s=s, gi=gi):
                            P.w(v, "vector", ev_tr)
                            ev = P.inc(v.tensor_copy(
                                xnT[:, s * 2048 + g * 512:s * 2048 + (g + 1) * 512],
                                ptA[:, hsel * 1024:hsel * 1024 + 512]), "DV")
                            bx["e"] = ev
                            ev_ptA_copy[hsel] = ev
                            if g == 3:
                                ev_xnT_tile[gi] = ev
                ev_xnT_done = ("DV", P.cnt["DV"])

                if dbg and half == 0:
                    @block.sync
                    def _(sy):
                        P.w(sy, "sync", ev_xnT_done)
                        P.inc(sy.dma_start(dXNT0[:, :], xnT[:]), "SP2", 16)
                        P.inc(sy.dma_start(dLN[:, :], lnst[:]), "SP2", 16)
                        P.inc(sy.dma_start(dXT0[:, :], xt[1][:]), "SP2", 16)

                units = [("k", 0), ("k", 1), ("q", 0), ("q", 1), ("v", 0), ("v", 1)]
                for ui, (wname, cc) in enumerate(units):
                    guid = half * 6 + ui
                    wpar = guid % 2

                    @block.gpsimd
                    def _(g, wname=wname, cc=cc, wpar=wpar, guid=guid):
                        P.w(g, "gpsimd", ev_unit_mm.get(guid - 2))
                        bx["e"] = P.inc(
                            g.dma_start(Wbuf[wpar][:], wd_map[wname][cc, :, :]),
                            f"LW{wpar}", 16)
                    ev_W = bx["e"]

                    for s, tt in enumerate(tiles):
                        uid = guse[0]
                        guse[0] += 1
                        sl_ap = SLICES[uid % 6]

                        @block.tensor
                        def _(t, s=s, wpar=wpar, sl_ap=sl_ap, uid=uid, guid=guid,
                              last=(s == 7)):
                            P.w(t, "tensor", ev_W, ev_xnT_tile[half * 8 + s])
                            if uid == 4:
                                P.w(t, "tensor", ev_sclrep)
                            for fev in ev_slice_free.pop(uid - 6, []):
                                P.w(t, "tensor", fev)
                            for dc in range(16):
                                ins = t.matmul(sl_ap,
                                         xnT[:, s * 2048 + dc * 128:s * 2048 + (dc + 1) * 128],
                                         Wbuf[wpar][:, dc * 512:(dc + 1) * 512],
                                         start=(dc == 0), stop=(dc == 15))
                            ev = P.inc(ins, "PE")
                            bx["e"] = ev
                            if last:
                                ev_unit_mm[guid] = ev
                        ev_mm = bx["e"]

                        frees = []
                        if wname == "v":

                            @block.vector
                            def _(v, tt=tt, cc=cc, sl_ap=sl_ap):
                                P.w(v, "vector", ev_mm)
                                bx["e"] = P.inc(v.tensor_copy(
                                    Vsb[:, tt * 1024 + cc * 512:tt * 1024 + (cc + 1) * 512],
                                    sl_ap), "DV")
                            frees.append(bx["e"])
                        else:

                            @block.scalar
                            def _(sc, sl_ap=sl_ap):
                                P.w(sc, "scalar", ev_mm, ev_evac_prev[0])
                                for lh in range(4):
                                    sc.activation(sqd[:, 0:128],
                                                  sl_ap[:, lh * 128:(lh + 1) * 128],
                                                  AF.Square, accum_out=sst[:, lh:lh + 1])
                                sc.activation(sst[:, 4:8], sst[:, 0:4], AF.Sqrt,
                                              scale=cst[:, 3:4], bias=cst[:, 4:5])
                                bx["e"] = P.inc(
                                    sc.activation(dmy[:, 0:256], stab[:, 0:256], AF.Copy), "AC")
                            ev_ss = bx["e"]
                            frees.append(ev_ss)

                            spar = (kqc[0] % 2) * 1536
                            kqc[0] += 1

                            @block.vector
                            def _(v, tt=tt, cc=cc, wname=wname, sl_ap=sl_ap, spar=spar):
                                P.w(v, "vector", ev_ss)
                                v.reciprocal(sst[:, 8:12], sst[:, 4:8])
                                v.tensor_copy(dmy[:], stab[:, 0:512])
                                if wname == "q":
                                    P.w(v, "vector", ev_lps)
                                    v.tensor_tensor(sst[:, 12:16], sst[:, 8:12],
                                                    lps[:, tt * 8 + cc * 4:tt * 8 + cc * 4 + 4],
                                                    op=OP.mult)
                                    v.tensor_copy(dmy[:], stab[:, 0:512])
                                    scol = 12
                                else:
                                    scol = 8
                                P.w(v, "vector", ev_rot_prev[0])
                                for lh in range(4):
                                    v.tensor_scalar_mul(stg[:, spar + lh * 128:spar + (lh + 1) * 128],
                                                        sl_ap[:, lh * 128:(lh + 1) * 128],
                                                        sst[:, scol + lh:scol + lh + 1])
                                bx["e"] = P.inc(v.tensor_copy(dmy[:], stab[:, 0:512]), "DV")
                            ev_evac = bx["e"]
                            ev_evac_prev[0] = ev_evac
                            frees.append(ev_evac)

                            @block.gpsimd
                            def _(g, tt=tt, spar=spar):
                                P.w(g, "gpsimd", ev_evac, ev_rt_prev[0])
                                c3 = ctab[:, tt * 128:(tt + 1) * 128].unsqueeze(1) \
                                    .broadcast_to([128, 4, 128])
                                s3 = stab[:, tt * 128:(tt + 1) * 128].unsqueeze(1) \
                                    .broadcast_to([128, 4, 128])
                                q3 = stg[:, spar:spar + 512].rearrange("p (h d) -> p h d", h=4)
                                sw = bass.AP(stg, spar + 64, [[3072, 128], [128, 4], [-64, 2], [1, 64]])
                                rotA = stg[:, spar + 512:spar + 1024].rearrange("p (h d) -> p h d", h=4)
                                rotT = stg[:, spar + 1024:spar + 1536].rearrange("p (h d) -> p h d", h=4)
                                g.tensor_tensor(rotA, q3, c3, op=OP.mult)
                                g.tensor_tensor(rotT, sw, s3, op=OP.mult)
                                ev = P.inc(g.tensor_tensor(
                                    stg[:, spar + 512:spar + 1024], stg[:, spar + 512:spar + 1024],
                                    stg[:, spar + 1024:spar + 1536], op=OP.add), "GP")
                                bx["e"] = ev
                                ev_rot_prev[0] = ev
                            ev_rot = bx["e"]

                            def mk_rot_emitter(ev_rot=ev_rot, tt=tt, cc=cc, wname=wname,
                                               spar=spar):
                                def emit(hsel):
                                    @block.tensor
                                    def _(t):
                                        P.w(t, "tensor", ev_rot, ev_ptA_copy[hsel])
                                        for lh in range(4):
                                            ins = t.transpose(
                                                ptA[:, hsel * 1024 + lh * 128:hsel * 1024 + (lh + 1) * 128],
                                                stg[:, spar + 512 + lh * 128:spar + 512 + (lh + 1) * 128],
                                                identb[:])
                                        ev = P.inc(ins, "PE")
                                        bx["e"] = ev
                                        ev_rt_prev[0] = ev
                                    ev_rt = bx["e"]

                                    @block.vector
                                    def _(v):
                                        P.w(v, "vector", ev_rt)
                                        dstt = KT if wname == "k" else QT
                                        dst = bass.AP(dstt, cc * 4 * 2048 + tt * 128,
                                                      [[16384, 128], [2048, 4], [1, 128]])
                                        srcap = ptA[:, hsel * 1024:hsel * 1024 + 512].rearrange(
                                            "p (h d) -> p h d", h=4)
                                        ev = P.inc(v.tensor_copy(dst, srcap), "DV")
                                        bx["e"] = ev
                                        ev_ptA_copy[hsel] = ev
                                    return bx["e"]
                                return emit
                            pend_rot.append(mk_rot_emitter())
                        ev_slice_free[uid] = frees
                        if len(pend_rot) > 1:
                            em = pend_rot.pop(0)
                            ev_slice_free[uid].append(em(rot_hsel[0] % 2))
                            rot_hsel[0] += 1

            while pend_rot:
                em = pend_rot.pop(0)
                em(rot_hsel[0] % 2)
                rot_hsel[0] += 1

            ev_A_done_dv = ("DV", P.cnt["DV"])
            ev_A_done_ac = ("AC", P.cnt["AC"])

            if dbg:
                @block.sync
                def _(sy):
                    P.w(sy, "sync", ev_A_done_dv, ev_A_done_ac)
                    P.inc(sy.dma_start(dKT[:, :], KT[:]), "SP2", 16)
                    P.inc(sy.dma_start(dSCL[:, :], sclrep[:]), "SP2", 16)
                    P.inc(sy.dma_start(dSST[:, :], sst[:]), "SP2", 16)
                    P.inc(sy.dma_start(dQT[:, :], QT[:]), "SP2", 16)
                    P.inc(sy.dma_start(dV[:, :], Vsb[:]), "SP2", 16)
                    P.inc(sy.dma_start(dXNT[:, :], xnT[:]), "SP2", 16)

        with ExitStack() as esB:
            EB = esB.enter_context
            E1 = [EB(nc.sbuf_tensor(f"E1{i}_s", [128, 2048], F32)) for i in range(2)]
            E2 = [EB(nc.sbuf_tensor(f"E2{i}_s", [128, 2048], F32)) for i in range(2)]
            Ph = [EB(nc.sbuf_tensor(f"Ph{i}_s", [128, 2048], BF16)) for i in range(2)]
            PhT = [EB(nc.sbuf_tensor(f"PhT{i}_s", [128, 2048], BF16)) for i in range(2)]
            ysb = [EB(nc.sbuf_tensor(f"ysb{i}_s", [128, 1024], BF16)) for i in range(2)]

            @block.vector
            def _(v):
                P.w(v, "vector", ev_tabs)
                v.tensor_tensor(lsc[:, 0:64], lam[:, 0:64], lam[:, 64:128], op=OP.mult)
                v.tensor_copy(dmy[:], stab[:, 0:512])
                v.reduce_sum(lsc[:, 64:65], lsc[:, 0:64], axis=AX.X)
                v.tensor_tensor(lsc[:, 0:64], lam[:, 128:192], lam[:, 192:256], op=OP.mult)
                v.tensor_copy(dmy[:], stab[:, 0:512])
                v.reduce_sum(lsc[:, 65:66], lsc[:, 0:64], axis=AX.X)
                bx["e"] = P.inc(v.tensor_copy(dmy[:], stab[:, 0:512]), "DV")
            ev_lam = bx["e"]

            @block.scalar
            def _(sc):
                P.w(sc, "scalar", ev_lam)
                sc.activation(lsc[:, 66:68], lsc[:, 64:66], AF.Exp)
                bx["e"] = P.inc(sc.activation(dmy[:, 0:256], stab[:, 0:256], AF.Copy), "AC")
            ev_lexp = bx["e"]

            @block.vector
            def _(v):
                P.w(v, "vector", ev_lexp)
                v.tensor_tensor(lsc[:, 68:69], lsc[:, 67:68], lsc[:, 66:67], op=OP.subtract)
                v.tensor_copy(dmy[:], stab[:, 0:512])
                v.tensor_tensor(lsc[:, 69:70], lsc[:, 68:69], cst[0:1, 5:6], op=OP.add)
                bx["e"] = P.inc(v.tensor_copy(dmy[:], stab[:, 0:512]), "DV")
            ev_negl = bx["e"]

            @block.tensor
            def _(t):
                P.w(t, "tensor", ev_negl, ev_A_done_dv)
                ins = t.matmul(psS2[:, 0:1], onesf[:, :], lsc[:, 69:70], start=True, stop=True)
                bx["e"] = P.inc(ins, "PE")
            ev_neglmm = bx["e"]

            @block.vector
            def _(v):
                P.w(v, "vector", ev_neglmm)
                v.tensor_copy(neglb[:], psS2[:, 0:1])
                bx["e"] = P.inc(v.tensor_copy(dmy[:], stab[:, 0:512]), "DV")
            ev_neglb = bx["e"]

            ev_exp1 = {}
            ev_exp2a = {}
            ev_exp2b = {}
            ev_yevac = {}
            ev_ypark = {}
            ev_phtcopy = {}
            ev_ph = {}
            ev_pv = {}
            ev_trlast = {}
            phsel = [0]

            def emit_head(p):
                r, lv = p // 4, p % 4
                L = (r + 1) * 128
                L2a = min(L, 512)
                L2b = L - L2a
                b0 = 2560 - L
                lh1, lh2 = lv, 4 + lv
                zb = 12 * (p % 3)

                @block.tensor
                def _(t, r=r, L=L, lh1=lh1, p=p):
                    P.w(t, "tensor", ev_exp1.get(p - 1), ev_exp2b.get(p - 1),
                        ev_A_done_dv, ev_A_done_ac)
                    if dbg:
                        P.w(t, "tensor", ("SP2", P.cnt["SP2"]))
                    for (c0, w_) in chunks_aligned(0, L):
                        ins = t.matmul(psS[:, c0:c0 + w_],
                                 QT[:, lh1 * 2048 + r * 128:lh1 * 2048 + (r + 1) * 128],
                                 KT[:, lh1 * 2048 + c0:lh1 * 2048 + c0 + w_],
                                 start=True, stop=True)
                    bx["e"] = P.inc(ins, "PE")
                ev_s1 = bx["e"]

                @block.vector
                def _(v, L=L, zb=zb):
                    P.w(v, "vector", ev_s1)
                    v.tensor_tensor(psS[:, L - 128:L], psS[:, L - 128:L], maskd[:],
                                    op=OP.add)
                    v.tensor_reduce(zst[:, zb:zb + 1], psS[:, 0:L],
                                    axis=AX.X, op=OP.max, negate=True)
                    bx["e"] = P.inc(v.tensor_copy(dmy[:], stab[:, 0:512]), "DV")
                ev_m1 = bx["e"]

                @block.scalar
                def _(sc, L=L, p=p, zb=zb):
                    P.w(sc, "scalar", ev_m1, ev_ph.get(p - 2))
                    ev = P.inc(sc.activation(E1[p % 2][:, 0:L], psS[:, 0:L], AF.Exp,
                                             bias=zst[:, zb:zb + 1],
                                             accum_out=zst[:, zb + 2:zb + 3]), "AC")
                    ev_exp1[p] = ev

                @block.tensor
                def _(t, r=r, L=L, L2a=L2a, L2b=L2b, b0=b0, lh2=lh2, p=p):
                    P.w(t, "tensor", ev_exp2a.get(p - 1), ev_neglb, ev_A_done_ac)
                    for (c0, w_) in chunks_aligned(0, L2a):
                        ins = t.matmul(psS2[:, c0:c0 + w_],
                                 QT[:, lh2 * 2048 + r * 128:lh2 * 2048 + (r + 1) * 128],
                                 KT[:, lh2 * 2048 + c0:lh2 * 2048 + c0 + w_],
                                 start=True, stop=True)
                    ev = P.inc(ins, "PE")
                    bx["ea"] = ev
                    if L2b:
                        if L >= 1152:
                            P.w(t, "tensor", ev_exp1[p])
                        P.w(t, "tensor", ev_exp2b.get(p - 1))
                        for (c0, w_) in chunks_aligned(b0, 2048):
                            kk = c0 - b0 + 512
                            ins = t.matmul(psS[:, c0:c0 + w_],
                                     QT[:, lh2 * 2048 + r * 128:lh2 * 2048 + (r + 1) * 128],
                                     KT[:, lh2 * 2048 + kk:lh2 * 2048 + kk + w_],
                                     start=True, stop=True)
                        ev = P.inc(ins, "PE")
                    bx["eb"] = ev
                ev_s2a, ev_s2b = bx["ea"], bx["eb"]

                @block.vector
                def _(v, L=L, L2a=L2a, L2b=L2b, b0=b0, zb=zb):
                    P.w(v, "vector", ev_s2a, ev_s2b)
                    if L2b:
                        v.tensor_tensor(psS[:, 1920:2048], psS[:, 1920:2048],
                                        maskd[:], op=OP.add)
                        v.tensor_reduce(zst[:, zb + 8:zb + 9], psS2[:, 0:L2a], axis=AX.X,
                                        op=OP.max)
                        v.tensor_reduce(zst[:, zb + 9:zb + 10], psS[:, b0:2048], axis=AX.X,
                                        op=OP.max)
                        v.tensor_copy(dmy[:], stab[:, 0:512])
                        v.tensor_reduce(zst[:, zb + 1:zb + 2],
                                        zst[:, zb + 8:zb + 10],
                                        axis=AX.X, op=OP.max, negate=True)
                        bx["e"] = P.inc(v.tensor_copy(dmy[:], stab[:, 0:512]), "DV")
                    else:
                        v.tensor_tensor(psS2[:, L - 128:L], psS2[:, L - 128:L],
                                        maskd[:], op=OP.add)
                        v.tensor_reduce(zst[:, zb + 1:zb + 2], psS2[:, 0:L2a],
                                        axis=AX.X, op=OP.max, negate=True)
                        bx["e"] = P.inc(v.tensor_copy(dmy[:], stab[:, 0:512]), "DV")
                ev_m2 = bx["e"]

                @block.scalar
                def _(sc, L=L, L2a=L2a, L2b=L2b, b0=b0, p=p, zb=zb):
                    P.w(sc, "scalar", ev_m2, ev_ph.get(p - 2))
                    ev = P.inc(sc.activation(E2[p % 2][:, 0:L2a], psS2[:, 0:L2a], AF.Exp,
                                             bias=zst[:, zb + 1:zb + 2],
                                             accum_out=zst[:, zb + 3:zb + 4]), "AC")
                    ev_exp2a[p] = ev
                    if L2b:
                        ev = P.inc(sc.activation(E2[p % 2][:, 512:L], psS[:, b0:2048], AF.Exp,
                                                 bias=zst[:, zb + 1:zb + 2],
                                                 accum_out=zst[:, zb + 4:zb + 5]), "AC")
                    ev_exp2b[p] = ev
                    bx["e"] = ev
                ev_e2done = bx["e"]

                @block.vector
                def _(v, L2b=L2b, p=p, zb=zb):
                    P.w(v, "vector", ev_exp1[p], ev_e2done, ev_neglb)
                    if L2b:
                        v.tensor_tensor(zst[:, zb + 3:zb + 4], zst[:, zb + 3:zb + 4],
                                        zst[:, zb + 4:zb + 5], op=OP.add)
                    v.reciprocal(zst[:, zb + 7:zb + 8], zst[:, zb + 2:zb + 3])
                    v.tensor_tensor(zst[:, zb + 5:zb + 6], zst[:, zb + 2:zb + 3],
                                    neglb[:], op=OP.mult)
                    v.reciprocal(zst[:, zb + 6:zb + 7], zst[:, zb + 3:zb + 4])
                    v.tensor_copy(dmy[:], stab[:, 0:512])
                    v.tensor_tensor(zst[:, zb + 5:zb + 6], zst[:, zb + 5:zb + 6],
                                    zst[:, zb + 6:zb + 7], op=OP.mult)
                    bx["e"] = P.inc(v.tensor_copy(dmy[:], stab[:, 0:512]), "DV")
                ev_c2 = bx["e"]
                ev_c2d[p] = ev_c2

                @block.gpsimd
                def _(g, L=L, p=p, zb=zb):
                    P.w(g, "gpsimd", ev_c2, ev_trlast.get(p - 2))
                    g.tensor_tensor(Ph[p % 2][:, 0:L], E2[p % 2][:, 0:L],
                                    zst[:, zb + 5:zb + 6].broadcast_to([128, L]), op=OP.mult)
                    ev = P.inc(g.tensor_tensor(Ph[p % 2][:, 0:L], Ph[p % 2][:, 0:L],
                                               E1[p % 2][:, 0:L], op=OP.add), "GP")
                    ev_ph[p] = ev

            def emit_tail(p):
                r, lv = p // 4, p % 4
                L = (r + 1) * 128
                rpar = r % 2
                zb = 12 * (p % 3)
                ngr = (L + 511) // 512
                for gi2 in range(ngr):
                    w_ = min(512, L - gi2 * 512)
                    ntr = (w_ + 127) // 128
                    hsel = phsel[0] % 2
                    phsel[0] += 1

                    @block.tensor
                    def _(t, gi2=gi2, ntr=ntr, hsel=hsel, p=p, last=(gi2 == ngr - 1)):
                        P.w(t, "tensor", ev_ph[p], ev_ptA_copy[hsel])
                        for ti in range(ntr):
                            ins = t.transpose(
                                ptA[:, hsel * 1024 + ti * 128:hsel * 1024 + (ti + 1) * 128],
                                Ph[p % 2][:, gi2 * 512 + ti * 128:gi2 * 512 + (ti + 1) * 128],
                                identb[:])
                        ev = P.inc(ins, "PE")
                        bx["e"] = ev
                        if last:
                            ev_trlast[p] = ev
                    ev_tr2 = bx["e"]

                    @block.vector
                    def _(v, gi2=gi2, w_=w_, hsel=hsel, p=p):
                        P.w(v, "vector", ev_tr2, ev_pv.get(p - 2))
                        ev = P.inc(v.tensor_copy(PhT[p % 2][:, gi2 * 512:gi2 * 512 + w_],
                                                 ptA[:, hsel * 1024:hsel * 1024 + w_]),
                                   "DV")
                        bx["e"] = ev
                        ev_ptA_copy[hsel] = ev
                        ev_phtcopy[(p, gi2)] = ev

                @block.tensor
                def _(t, L=L, lv=lv, p=p):
                    P.w(t, "tensor", ev_yevac.get(p - 1), ev_neglb)
                    nkc = L // 128
                    for kc in range(nkc):
                        P.w(t, "tensor", ev_phtcopy[(p, kc // 4)])
                        ins = t.matmul(pvout[:, 0:256],
                                 PhT[p % 2][:, kc * 128:(kc + 1) * 128],
                                 Vsb[:, kc * 1024 + lv * 256:kc * 1024 + (lv + 1) * 256],
                                 start=(kc == 0), stop=(kc == nkc - 1))
                    ev = P.inc(ins, "PE")
                    ev_pv[p] = ev
                    bx["e"] = ev
                ev_pvv = bx["e"]

                @block.scalar
                def _(sc, lv=lv, rpar=rpar, p=p, r=r, zb=zb):
                    P.w(sc, "scalar", ev_pvv, ev_c2d[p])
                    if lv == 0:
                        P.w(sc, "scalar", ev_ypark.get(r - 2))
                    ev = P.inc(sc.activation(ysb[rpar][:, lv * 256:(lv + 1) * 256],
                                             pvout[:, 0:256], AF.Copy,
                                             scale=zst[:, zb + 7:zb + 8]), "AC")
                    ev_yevac[p] = ev

                if lv == 3:
                    @block.sync
                    def _(sy, r=r, rpar=rpar):
                        P.w(sy, "sync", ev_yevac[r * 4 + 3])
                        ev = P.inc(sy.dma_start(ybuf_d[r * 128:(r + 1) * 128, :],
                                                ysb[rpar][:]), f"SY{r % 2}", 16)
                        ev_ypark[r] = ev

            ev_c2d = {}
            for p in range(64):
                if p >= 2:
                    emit_tail(p - 2)
                emit_head(p)
            emit_tail(62)
            emit_tail(63)

            if dbg:
                @block.sync
                def _(sy):
                    P.w(sy, "sync", ev_yevac[63])
                    P.inc(sy.dma_start(dE1[:, :], E1[1][:]), "SP2", 16)
                    P.inc(sy.dma_start(dE2[:, :], E2[1][:]), "SP2", 16)
                    P.inc(sy.dma_start(dPhT[:, :], PhT[1][:]), "SP2", 16)
                    P.inc(sy.dma_start(dZ[:, :], zst[:]), "SP2", 16)
                    P.inc(sy.dma_start(dNL[:, :], neglb[:]), "SP2", 16)
                    P.inc(sy.dma_start(dLS[:, :], lsc[:]), "SP2", 16)

            @block.gpsimd
            def _(g):
                P.w(g, "gpsimd", ("SY0", P.cnt["SY0"]), ("SY1", P.cnt["SY1"]))
                if dbg:
                    P.w(g, "gpsimd", ("SP2", P.cnt["SP2"]))
                bx["e"] = P.inc(g.collective_compute(
                    "AllGather", OP.bypass,
                    replica_groups=[[0, 1], [2, 3], [4, 5], [6, 7]],
                    ins=[ybuf_d.ap().opt()], outs=[ygath_d.ap().opt()]), "CC")
            ev_cc = bx["e"]
            ev_B_done_dv = ("DV", P.cnt["DV"])
            ev_B_done_ac = ("AC", P.cnt["AC"])
            ev_B_done_pe = ("PE", P.cnt["PE"])
        esAB.close()

        with ExitStack() as esC:
            EC = esC.enter_context
            xn2T = EC(nc.sbuf_tensor("xn2T_s", [128, 16384], BF16))
            fuse = EC(nc.sbuf_tensor("fuse_s", [128, 32768], BF16))
            Wf = [EC(nc.sbuf_tensor(f"Wf{i}_s", [128, 8192], BF16)) for i in range(2)]
            gatesb = EC(nc.sbuf_tensor("gatesb_s", [128, 2048], BF16))
            upsb = EC(nc.sbuf_tensor("upsb_s", [128, 512], BF16))
            outtm = EC(nc.sbuf_tensor("outtm_s", [128, 8192], F32))
            x2t = [EC(nc.sbuf_tensor("x2t0_s", [128, 2048], F32))] * 2
            yc = [EC(nc.sbuf_tensor(f"yc{i}_s", [128, 2048], BF16)) for i in range(2)]
            xn2 = EC(nc.sbuf_tensor("xn2_s", [128, 2048], BF16))
            dwn = EC(nc.sbuf_tensor("dwn_s", [128, 512], F32))

            ev_park = {}
            ev_x2done = {}
            for tp in range(8):

                @block.gpsimd
                def _(g, tp=tp):
                    P.w(g, "gpsimd", ev_cc, ev_tabs, ev_x2done.get(tp - 2))
                    P.inc(g.indirect_dma_start(
                        yc[tp % 2][:, 0:1024], None, ygath_d[:, :],
                        IndirectOffsetOnAxis(idxg[:, tp:tp + 1], 0)), "LYC", 16)
                    bx["e"] = P.inc(g.indirect_dma_start(
                        yc[tp % 2][:, 1024:2048], None, ygath_d[:, :],
                        IndirectOffsetOnAxis(idxg[:, 8 + tp:9 + tp], 0)), "LYC", 16)
                ev_yc = bx["e"]

                @block.sync
                def _(sy, tp=tp):
                    P.w(sy, "sync", ev_x2done.get(tp - 1))
                    bx["e"] = P.inc(
                        sy.dma_start(x2t[tp % 2][:], xown_d[tp * 128:(tp + 1) * 128, :]),
                        "LXO", 16)
                ev_xo = bx["e"]

                @block.vector
                def _(v, tp=tp):
                    P.w(v, "vector", ev_yc, ev_xo)
                    bx["e"] = P.inc(v.tensor_tensor(x2t[tp % 2][:], x2t[tp % 2][:], yc[tp % 2][:], op=OP.add),
                                    "DV")
                ev_x2 = bx["e"]

                @block.sync
                def _(sy, tp=tp):
                    P.w(sy, "sync", ev_x2)
                    ev = P.inc(sy.dma_start(out_d[tp * 128:(tp + 1) * 128, :], x2t[tp % 2][:]),
                               "SP2", 16)
                    ev_park[tp] = ev

                @block.scalar
                def _(sc, tp=tp):
                    P.w(sc, "scalar", ev_x2)
                    bx["e"] = P.inc(sc.activation(sqd[:], x2t[tp % 2][:], AF.Square,
                                                  accum_out=lnst[:, 0:1]), "AC")
                ev_sq2 = bx["e"]

                @block.vector
                def _(v, tp=tp):
                    P.w(v, "vector", ev_sq2)
                    v.reduce_sum(lnst[:, 1:2], x2t[tp % 2][:], axis=AX.X)
                    v.tensor_copy(dmy[:], stab[:, 0:512])
                    v.tensor_scalar_mul(lnst[:, 2:3], lnst[:, 1:2], cst[:, 0:1])
                    bx["e"] = P.inc(v.tensor_copy(dmy[:], stab[:, 0:512]), "DV")
                ev_mean2 = bx["e"]

                @block.scalar
                def _(sc):
                    P.w(sc, "scalar", ev_mean2)
                    sc.activation(lnst[:, 3:4], lnst[:, 2:3], AF.Square)
                    bx["e"] = P.inc(sc.activation(dmy[:, 0:256], stab[:, 0:256], AF.Copy), "AC")
                ev_msq2 = bx["e"]

                @block.vector
                def _(v):
                    P.w(v, "vector", ev_msq2)
                    v.tensor_scalar(lnst[:, 4:5], lnst[:, 3:4], cst[:, 1:2], cst[:, 2:3],
                                    op0=OP.mult, op1=OP.add)
                    bx["e"] = P.inc(v.tensor_copy(dmy[:], stab[:, 0:512]), "DV")
                ev_st2 = bx["e"]

                @block.scalar
                def _(sc):
                    P.w(sc, "scalar", ev_st2)
                    sc.activation(lnst[:, 5:6], lnst[:, 0:1], AF.Sqrt,
                                  scale=cst[:, 0:1], bias=lnst[:, 4:5])
                    bx["e"] = P.inc(sc.activation(dmy[:, 0:256], stab[:, 0:256], AF.Copy), "AC")
                ev_sd2 = bx["e"]

                @block.vector
                def _(v, tp=tp):
                    P.w(v, "vector", ev_sd2)
                    v.reciprocal(lnst[:, 6:7], lnst[:, 5:6])
                    v.tensor_copy(dmy[:], stab[:, 0:512])
                    ev = P.inc(v.tensor_scalar(xn2[:], x2t[tp % 2][:], lnst[:, 2:3], lnst[:, 6:7],
                                               op0=OP.subtract, op1=OP.mult), "DV")
                    bx["e"] = ev
                    ev_x2done[tp] = ev
                ev_xn2 = bx["e"]

                for g2 in range(4):
                    hsel = g2 % 2

                    @block.tensor
                    def _(t, g2=g2, hsel=hsel):
                        P.w(t, "tensor", ev_xn2, ev_ptA_copy[hsel], ev_B_done_dv)
                        for d4 in range(4):
                            dc = g2 * 4 + d4
                            ins = t.transpose(
                                ptA[:, hsel * 1024 + d4 * 128:hsel * 1024 + (d4 + 1) * 128],
                                xn2[:, dc * 128:(dc + 1) * 128], identb[:])
                        bx["e"] = P.inc(ins, "PE")
                    ev_t2 = bx["e"]

                    @block.vector
                    def _(v, g2=g2, hsel=hsel, tp=tp):
                        P.w(v, "vector", ev_t2)
                        dst = bass.AP(xn2T, (g2 * 4) * 1024 + tp * 128,
                                      [[16384, 128], [1024, 4], [1, 128]])
                        src = ptA[:, hsel * 1024:hsel * 1024 + 512].rearrange(
                            "p (h d) -> p h d", h=4)
                        ev = P.inc(v.tensor_copy(dst, src), "DV")
                        bx["e"] = ev
                        ev_ptA_copy[hsel] = ev
            ev_xn2T_done = ("DV", P.cnt["DV"])

            GP_SL = [psS2[:, 0:512], psS[:, 0:512]]
            UP_SL = [psS[:, 512:1024], pvout[:, 0:512]]
            ev_silu = {}
            ev_upev = {}
            ev_fuse_ev = {}
            ev_dts = {}
            ev_dcopy = {}
            ev_acc = {}
            ev_gmm = {}
            ev_umm = {}
            ev_dmm = {}
            ev_dtr_prev = [None]

            for g in range(2):
                for f_i in range(16):

                    @block.gpsimd
                    def _(gp, g=g, f_i=f_i):
                        P.w(gp, "gpsimd", ev_gmm.get((g, f_i - 1)),
                            ev_dmm.get((g - 1, 15)))
                        bx["e0"] = P.inc(gp.dma_start(Wf[0][:], wg_d[f_i, :, :]), "LW0", 16)
                        P.w(gp, "gpsimd", ev_umm.get((g, f_i - 1)))
                        bx["e"] = P.inc(gp.dma_start(Wf[1][:], wu_d[f_i, :, :]), "LW1", 16)
                    ev_Wg0 = bx["e0"]
                    ev_Wgu = bx["e"]

                    for f128 in range(4):
                        ic = f_i * 4 + f128
                        gsl = GP_SL[ic % 2]

                        @block.tensor
                        def _(t, g=g, f_i=f_i, f128=f128, ic=ic, gsl=gsl):
                            P.w(t, "tensor", ev_Wg0, ev_xn2T_done, ev_B_done_ac)
                            P.w(t, "tensor", ev_silu.get((g, ic - 2)))
                            if g > 0 and ic < 2:
                                P.w(t, "tensor", ev_dts.get((g - 1, 12 + ic)))
                            for kc in range(16):
                                ins = t.matmul(gsl,
                                         Wf[0][:, kc * 512 + f128 * 128:kc * 512 + (f128 + 1) * 128],
                                         xn2T[:, kc * 1024 + g * 512:kc * 1024 + (g + 1) * 512],
                                         start=(kc == 0), stop=(kc == 15))
                            ev = P.inc(ins, "PE")
                            ev_gmm[(g, f_i)] = ev
                            bx["eg"] = ev
                        evg = bx["eg"]

                        @block.scalar
                        def _(sc, g=g, ic=ic, f128=f128, gsl=gsl, evg=evg):
                            P.w(sc, "scalar", evg, ev_fuse_ev.get((g, ic - 4)))
                            ev = P.inc(sc.activation(gatesb[:, f128 * 512:(f128 + 1) * 512],
                                                     gsl, AF.Silu,
                                                     scale=sgt[:, ic:ic + 1]), "AC")
                            ev_silu[(g, ic)] = ev

                    for f128 in range(4):
                        ic = f_i * 4 + f128
                        usl = UP_SL[ic % 2]

                        @block.tensor
                        def _(t, g=g, f_i=f_i, f128=f128, ic=ic, usl=usl):
                            P.w(t, "tensor", ev_Wgu)
                            P.w(t, "tensor", ev_upev.get((g, ic - 2)))
                            if g > 0 and ic < 2:
                                P.w(t, "tensor", ev_dts.get((g - 1, 13 + ic)))
                            for kc in range(16):
                                ins = t.matmul(usl,
                                         Wf[1][:, kc * 512 + f128 * 128:kc * 512 + (f128 + 1) * 128],
                                         xn2T[:, kc * 1024 + g * 512:kc * 1024 + (g + 1) * 512],
                                         start=(kc == 0), stop=(kc == 15))
                            ev = P.inc(ins, "PE")
                            ev_umm[(g, f_i)] = ev
                            bx["eu"] = ev
                        evu = bx["eu"]

                        @block.vector
                        def _(v, g=g, ic=ic, usl=usl, evu=evu):
                            P.w(v, "vector", evu, ev_fuse_ev.get((g, ic - 1)))
                            ev = P.inc(v.tensor_scalar_mul(upsb[:], usl,
                                                           sut[:, ic:ic + 1]), "DV")
                            ev_upev[(g, ic)] = ev

                        @block.gpsimd
                        def _(gp, g=g, ic=ic, f128=f128):
                            P.w(gp, "gpsimd", ev_silu[(g, ic)], ev_upev[(g, ic)])
                            if ic == 0:
                                P.w(gp, "gpsimd", ev_dmm.get((g - 1, 15)))
                            ev = P.inc(gp.tensor_tensor(
                                fuse[:, ic * 512:(ic + 1) * 512],
                                gatesb[:, f128 * 512:(f128 + 1) * 512],
                                upsb[:], op=OP.mult), "GP")
                            ev_fuse_ev[(g, ic)] = ev

                ev_fuse_all = ("GP", P.cnt["GP"])
                for fc in range(16):

                    @block.gpsimd
                    def _(gp, g=g, fc=fc):
                        P.w(gp, "gpsimd", ev_dmm.get((g, fc - 2)))
                        if fc < 2:
                            P.w(gp, "gpsimd", ev_umm[(g, 15)])
                        bx["e"] = P.inc(gp.dma_start(Wf[fc % 2][:], wd_d[fc, :, :]),
                                        f"LW{fc % 2}", 16)
                    ev_Wd = bx["e"]

                    @block.tensor
                    def _(t, g=g, fc=fc):
                        P.w(t, "tensor", ev_Wd, ev_fuse_all)
                        P.w(t, "tensor", ev_dts.get((g, fc - 4)))
                        dsl = psS[:, (fc % 4) * 512:((fc % 4) + 1) * 512]
                        for ic in range(64):
                            ins = t.matmul(dsl,
                                     Wf[fc % 2][:, ic * 128:(ic + 1) * 128],
                                     fuse[:, ic * 512:(ic + 1) * 512],
                                     start=(ic == 0), stop=(ic == 63))
                        ev = P.inc(ins, "PE")
                        ev_dmm[(g, fc)] = ev
                        bx["e"] = ev
                    ev_dm = bx["e"]

                    @block.vector
                    def _(v, g=g, fc=fc):
                        P.w(v, "vector", ev_dm, ev_dtr_prev[0])
                        dsl = psS[:, (fc % 4) * 512:((fc % 4) + 1) * 512]
                        ev = P.inc(v.tensor_scalar_mul(dwn[:], dsl, sdt[:, fc:fc + 1]),
                                   "DV")
                        ev_dts[(g, fc)] = ev
                        bx["e"] = ev
                    ev_dt = bx["e"]

                    @block.tensor
                    def _(t, g=g, fc=fc):
                        P.w(t, "tensor", ev_dt, ev_dcopy.get((g, fc - 1)))
                        for tl in range(4):
                            ins = t.transpose(pvout[:, tl * 128:(tl + 1) * 128],
                                        dwn[:, tl * 128:(tl + 1) * 128], identf[:])
                        ev = P.inc(ins, "PE")
                        ev_dtr_prev[0] = ev
                        bx["e"] = ev
                    ev_dtr = bx["e"]

                    @block.vector
                    def _(v, g=g, fc=fc):
                        P.w(v, "vector", ev_dtr)
                        if fc == 0:
                            P.w(v, "vector", ev_acc.get(g - 1))
                        dst = bass.AP(outtm, fc * 128, [[8192, 128], [2048, 4], [1, 128]])
                        src = pvout[:, 0:512].rearrange("p (h d) -> p h d", h=4)
                        ev = P.inc(v.tensor_copy(dst, src), "DV")
                        ev_dcopy[(g, fc)] = ev

                for tl in range(4):
                    tp = g * 4 + tl

                    @block.gpsimd
                    def _(gp, g=g, tl=tl, tp=tp):
                        P.w(gp, "gpsimd", ev_dcopy[(g, 15)])
                        P.w(gp, "gpsimd", ("SP2", P.cnt["SP2"]))
                        ev = P.inc(gp.dma_start(out_d[tp * 128:(tp + 1) * 128, :],
                                                outtm[:, tl * 2048:(tl + 1) * 2048],
                                                accum_op=OP.add), "SG", 16)
                        bx["e"] = ev
                ev_acc[g] = ("SG", P.cnt["SG"])

            @block.gpsimd
            def _(gp):
                gp.wait_ge(sems["SG"], P.cnt["SG"])
                gp.wait_ge(sems["SP2"], P.cnt["SP2"])
                gp.wait_ge(sems["SY0"], P.cnt["SY0"])
                gp.wait_ge(sems["SY1"], P.cnt["SY1"])

            @block.sync
            def _(sy):
                sy.wait_ge(sems["SP2"], P.cnt["SP2"])

    return nc


_NC = None


def _get_nc():
    global _NC
    if _NC is None:
        _NC = build_program()
    return _NC


def _host_tables():
    i = np.arange(64, dtype=np.float64)
    inv_freq = 1.0 / (10000.0 ** (2 * i / HD))
    t = np.arange(T, dtype=np.float64)
    fr = t[:, None] * inv_freq[None, :]
    cos = np.cos(fr).astype(np.float32)
    sin = np.sin(fr).astype(np.float32)
    C = np.concatenate([cos, cos], axis=1)
    S = np.concatenate([sin, -sin], axis=1)
    ctab = np.ascontiguousarray(
        C.reshape(16, 128, 128).transpose(1, 0, 2).reshape(128, 2048)).astype(bf16)
    stab = np.ascontiguousarray(
        S.reshape(16, 128, 128).transpose(1, 0, 2).reshape(128, 2048)).astype(bf16)
    logpos = np.log(np.arange(1, T + 1, dtype=np.float32))
    logpt = np.ascontiguousarray(logpos.reshape(16, 128).T).astype(np.float32)
    p = np.arange(128)
    ii = np.arange(128)
    maskd = np.where(ii[None, :] <= p[:, None], 0.0, NEG).astype(np.float32).astype(bf16)
    identb = np.eye(128, dtype=np.float32).astype(bf16)
    identf = np.eye(128, dtype=np.float32)
    onesf = np.ones((1, 128), np.float32)
    return ctab, stab, logpt, maskd, identb, identf, onesf


def make_inmaps(L0_attn_x, L0_attn_Wq, L0_attn_Wk, L0_attn_Wv,
                L0_attn_lambda_q1, L0_attn_lambda_k1, L0_attn_lambda_q2, L0_attn_lambda_k2,
                L0_attn_softmax_scaler, L0_ffn_w_gate, L0_ffn_w_up, L0_ffn_w_down,
                L0_ffn_s_gate, L0_ffn_s_up, L0_ffn_s_down):
    x = np.asarray(L0_attn_x, dtype=np.float32)
    Wq = np.asarray(L0_attn_Wq, dtype=np.float32)
    Wk = np.asarray(L0_attn_Wk, dtype=np.float32)
    Wv = np.asarray(L0_attn_Wv, dtype=np.float32)
    Wg = np.asarray(L0_ffn_w_gate)
    Wu = np.asarray(L0_ffn_w_up)
    Wd = np.asarray(L0_ffn_w_down)
    sg = np.asarray(L0_ffn_s_gate, dtype=np.float32)
    su = np.asarray(L0_ffn_s_up, dtype=np.float32)
    sd = np.asarray(L0_ffn_s_down, dtype=np.float32)
    scaler = np.asarray(L0_attn_softmax_scaler, dtype=np.float32).reshape(16)
    lam = np.concatenate([np.asarray(v, np.float32).reshape(64) for v in
                          (L0_attn_lambda_q1, L0_attn_lambda_k1,
                           L0_attn_lambda_q2, L0_attn_lambda_k2)]).reshape(1, 256)

    ctab, stab, logpt, maskd, identb, identf, onesf = _host_tables()
    cstv = np.broadcast_to(np.array([1.0 / D, -1.0, EPS_LN, 1.0 / HD, EPS_RMS,
                                     -LAMBDA_INIT, 1.0 / math.sqrt(HD), -0.5],
                                    np.float32), (128, 8)).copy()
    wg_r = np.ascontiguousarray(
        Wg.reshape(16, 128, 16, 512).transpose(2, 1, 0, 3).reshape(16, 128, 8192))
    wu_r = np.ascontiguousarray(
        Wu.reshape(16, 128, 16, 512).transpose(2, 1, 0, 3).reshape(16, 128, 8192))
    wd_r = np.ascontiguousarray(
        Wd.reshape(64, 128, 16, 128).transpose(2, 1, 0, 3).reshape(16, 128, 8192))
    sgt = np.ascontiguousarray(sg.reshape(64, 128).T)
    sut = np.ascontiguousarray(su.reshape(64, 128).T)
    sdt = np.ascontiguousarray(sd.reshape(16, 128).T)

    in_maps = []
    for c in range(8):
        b, j = c // 2, c % 2
        qheads = [4 * j + i for i in range(4)] + [8 + 4 * j + i for i in range(4)]
        qcols = np.concatenate([np.arange(h * 128, (h + 1) * 128) for h in qheads])
        vcols = np.arange(4 * j * 256, 4 * j * 256 + 1024)

        def tile_w(W, cols):
            Ws = np.ascontiguousarray(W[:, cols])
            return np.ascontiguousarray(
                Ws.reshape(16, 128, 2, 512).transpose(2, 1, 0, 3).reshape(2, 128, 8192))

        pp = np.arange(128, dtype=np.int64)
        idxg = np.zeros((128, 16), np.int32)
        for tp in range(8):
            idxg[:, tp] = j * 1024 + tp * 128 + pp
            idxg[:, 8 + tp] = 2048 + j * 1024 + tp * 128 + pp

        in_maps.append({
            "x": np.ascontiguousarray(x[b]),
            "xown": np.ascontiguousarray(x[b, j * 1024:(j + 1) * 1024]),
            "wq": tile_w(Wq, qcols),
            "wk": tile_w(Wk, qcols),
            "wv": tile_w(Wv, vcols),
            "wg": wg_r, "wu": wu_r, "wd": wd_r,
            "sgt": sgt, "sut": sut, "sdt": sdt,
            "lam": lam,
            "scl": np.ascontiguousarray(scaler[qheads]).reshape(1, 8),
            "ctab": ctab, "stab": stab, "logpt": logpt, "maskd": maskd,
            "identb": identb, "identf": identf, "onesf": onesf,
            "idxg": idxg, "cst": cstv,
        })

    return in_maps


def kernel(**inputs):
    in_maps = make_inmaps(**inputs)
    nc = _get_nc()
    res = run_bass_kernel_spmd(nc, in_maps, core_ids=list(range(8)))
    out = np.empty((4, T, D), np.float32)
    for c in range(8):
        b, j = c // 2, c % 2
        out[b, j * 1024:(j + 1) * 1024, :] = np.asarray(res.results[c]["out"])
    return out
